# version 03
# baseline (speedup 1.0000x reference)
import contextlib
import numpy as np

import concourse.bass as bass
import concourse.mybir as mybir
from concourse.bass_utils import run_bass_kernel_spmd

F32 = mybir.dt.float32
BF16 = mybir.dt.bfloat16
AF = mybir.ActivationFunctionType
ALU = mybir.AluOpType
AX = mybir.AxisListType

NCORES = 8
NB = 4
S = 2048
D = 1024
DI = 2816
TT = 512
NT = S // TT
NSUB = TT // 128
DC = D // 128
IC = DI // 128
C12 = 1.0 / 12.0
INV_F = 1.0 / float(D)
OSL = [(0, 512), (512, 1024), (1024, 1536), (1536, 2048), (2048, 2560), (2560, 2816)]
QT = [(0, 1408), (1408, 2816)]


class Ctr:
    def __init__(self, es, nc, name):
        self.h = es.enter_context(nc.semaphore(name))
        self.v = 0

    def inc(self, inst, k=1):
        inst.then_inc(self.h, k)
        self.v += k
        return self.v

    def wait(self, eng, v=None):
        eng.wait_ge(self.h, self.v if v is None else v)


class QTmp:

    def __init__(self, scope, nc, tag, W):
        self.W = W
        sb = lambda n, s, d: scope.enter_context(nc.sbuf_tensor(n, s, d))
        self.u = sb(f"{tag}_u", [128, W], F32)
        self.au = sb(f"{tag}_au", [128, W], F32)
        self.sg = sb(f"{tag}_sg", [128, W], F32)
        self.aA = sb(f"{tag}_aA", [128, W], F32)
        self.aB = sb(f"{tag}_aB", [128, W], F32)
        self.aB2 = sb(f"{tag}_aB2", [128, W], F32)
        G = W // 16
        self.gm = sb(f"{tag}_gm", [128, G], F32)
        self.sc = sb(f"{tag}_sc", [128, G], F32)
        self.r = sb(f"{tag}_r", [128, G], F32)
        self.r6 = sb(f"{tag}_r6", [128, G], F32)


def build_program(debug=False, stages=("cast", "prep", "p1", "p2a", "p2b"), nb=NB, p1_quant=True, p1_xpose=True, p1_qkv=True, p1_ev=4):
    nc = bass.Bass("TRN2", target_bir_lowering=False)

    x_h = nc.dram_tensor("x", [NB, S, D], F32, kind="ExternalInput")
    wq_h = nc.dram_tensor("wq", [D, D], F32, kind="ExternalInput")
    wk_h = nc.dram_tensor("wk", [D, D], F32, kind="ExternalInput")
    wv_h = nc.dram_tensor("wv", [D, D], F32, kind="ExternalInput")
    wo_h = nc.dram_tensor("wo", [D, D], F32, kind="ExternalInput")
    bq_h = nc.dram_tensor("bq", [D], F32, kind="ExternalInput")
    bk_h = nc.dram_tensor("bk", [D], F32, kind="ExternalInput")
    bv_h = nc.dram_tensor("bv", [D], F32, kind="ExternalInput")
    bo_h = nc.dram_tensor("bo", [D], F32, kind="ExternalInput")
    wg_h = nc.dram_tensor("wg", [DI, D], F32, kind="ExternalInput")
    wu_h = nc.dram_tensor("wu", [DI, D], F32, kind="ExternalInput")
    wd_h = nc.dram_tensor("wd", [D, DI], F32, kind="ExternalInput")
    out_h = nc.dram_tensor("out", [NB, S, D], F32, kind="ExternalOutput")

    kscr = "ExternalOutput" if debug else "Internal"
    wgN = nc.dram_tensor("wgN", [DI, D], BF16, kind=kscr)
    wuN = nc.dram_tensor("wuN", [DI, D], BF16, kind=kscr)
    wdN = nc.dram_tensor("wdN", [D, DI], BF16, kind=kscr)
    wqB = nc.dram_tensor("wqB", [D, D], BF16, kind="Internal")
    wkB = nc.dram_tensor("wkB", [D, D], BF16, kind="Internal")
    wvB = nc.dram_tensor("wvB", [D, D], BF16, kind="Internal")
    woB = nc.dram_tensor("woB", [D, D], BF16, kind="Internal")
    xnS = nc.dram_tensor("xnS", [NB, S, D], BF16, kind=kscr)
    xqS = nc.dram_tensor("xqS", [NB, S, D], BF16, kind=kscr)
    hqS = nc.dram_tensor("hqS", [NB, S, DI], BF16, kind=kscr)
    qS = nc.dram_tensor("qS", [NB, D, S], F32, kind=kscr)

    es = contextlib.ExitStack()

    def sb(name, shape, dtype=F32, **kw):
        return es.enter_context(nc.sbuf_tensor(name, shape, dtype, **kw))

    cXn = sb("xn", [128, D])
    cXnb = sb("xnb", [128, D], BF16)
    cXqb = sb("xqb", [128, D], BF16)
    cSq = [sb("sq0", [128, D]), sb("sq1", [128, D])]
    cSsum = sb("ssum", [128, NSUB])
    cSsq = sb("ssq", [128, NSUB])
    cTv = sb("tv", [128, NSUB])
    cVar = sb("var", [128, NSUB])
    cLv = sb("lv", [128, NSUB])
    cRstd = sb("rstd", [128, NSUB])
    cNmr = sb("nmr", [128, NSUB])
    cKvp = sb("kvp", [128, NT * DC])
    cKsp = sb("ksp", [128, NT * DC])
    cKv8 = sb("kv8", [128, DC])
    cKs8 = sb("ks8", [128, DC])
    cBq = sb("bq_c", [128, DC])
    cBk = sb("bk_c", [128, DC])
    cBv = sb("bv_c", [128, DC])
    cBoR = sb("boR", [128, D])
    cFeps = sb("cfeps", [128, 1])
    cMh = sb("cmh", [128, 1])
    cHlf = sb("chlf", [128, 1])
    ps = [es.enter_context(nc.psum_tensor(f"ps{i}", [128, 512], F32)) for i in range(8)]

    s_dx = Ctr(es, nc, "dx")
    s_dsp = Ctr(es, nc, "dsp")
    s_dg = Ctr(es, nc, "dg")
    s_v = Ctr(es, nc, "sv")
    s_a = Ctr(es, nc, "sa")
    s_t = Ctr(es, nc, "st")
    s_g = Ctr(es, nc, "sg")
    s_bar = [Ctr(es, nc, f"bar{i}") for i in range(5)]

    blk = es.enter_context(nc.Block())
    ENG = ["vector", "scalar", "tensor", "gpsimd", "sync"]

    def emit(eng, fn):
        getattr(blk, eng)(fn)

    def barrier():
        for i, e in enumerate(ENG):
            def bump(g, i=i, e=e):
                if e in ("vector", "gpsimd"):
                    ins = g.engine_nop()
                elif e in ("tensor", "scalar"):
                    ins = g.drain()
                else:
                    ins = g.nop()
                s_bar[i].inc(ins)
            emit(e, bump)
        targets = [c.v for c in s_bar]
        for i, e in enumerate(ENG):
            def w(g, i=i):
                for j in range(5):
                    if j != i:
                        s_bar[j].wait(g, targets[j])
            emit(e, w)

    def quant_emit(tmp, jobs):
        offs = []
        o = 0
        for (_, _, w) in jobs:
            assert w % 16 == 0
            offs.append(o)
            o += w
        assert o <= tmp.W

        def vfn(v):
            for (src, _, w), off in zip(jobs, offs):
                g0 = off // 16
                gc = w // 16
                v.tensor_reduce(tmp.gm[:, g0:g0 + gc],
                                src.rearrange("p (g e) -> p g e", e=16),
                                axis=AX.X, op=ALU.max, apply_absolute_value=True)
            v.drain()
            for (src, _, w), off in zip(jobs, offs):
                g0 = off // 16
                gc = w // 16
                gsl = tmp.gm[:, g0:g0 + gc]
                v.tensor_scalar(tmp.sc[:, g0:g0 + gc], gsl, 6e-12, C12,
                                op0=ALU.max, op1=ALU.mult)
                v.reciprocal(tmp.r[:, g0:g0 + gc], gsl)
            v.drain()
            for (_, _, w), off in zip(jobs, offs):
                g0 = off // 16
                gc = w // 16
                v.tensor_scalar(tmp.r6[:, g0:g0 + gc], tmp.r[:, g0:g0 + gc],
                                6.0, 1e12, op0=ALU.mult, op1=ALU.min)
            v.drain()
            for (src, _, w), off in zip(jobs, offs):
                g0 = off // 16
                gc = w // 16
                v.tensor_tensor(tmp.u[:, off:off + w].rearrange(
                                    "p (g e) -> p g e", e=16),
                                src.rearrange("p (g e) -> p g e", e=16),
                                tmp.r6[:, g0:g0 + gc].unsqueeze(2)
                                .broadcast_to([128, gc, 16]), op=ALU.mult)
            s_v.inc(v.engine_nop())
        emit("vector", vfn)
        u_done = s_v.v

        def afn(a):
            s_v.wait(a, u_done)
            for (_, _, w), off in zip(jobs, offs):
                usl = tmp.u[:, off:off + w]
                a.activation(tmp.au[:, off:off + w], usl, AF.Abs)
                last = a.activation(tmp.sg[:, off:off + w], usl, AF.Sign)
            s_a.inc(last)
        emit("scalar", afn)
        au_done = s_a.v

        def gfn(g):
            s_a.wait(g, au_done)
            for (_, _, w), off in zip(jobs, offs):
                au = tmp.au[:, off:off + w]
                ab = tmp.aB[:, off:off + w]
                ab2 = tmp.aB2[:, off:off + w]
                g.tensor_single_scalar(ab, au, 2.5, op=ALU.is_gt)
                g.tensor_scalar(ab2, au, 5.0, 2.0, op0=ALU.is_gt, op1=ALU.mult)
                g.tensor_tensor(ab, ab, ab2, op=ALU.add)
                g.tensor_single_scalar(ab2, au, 3.5, op=ALU.is_gt)
                g.tensor_tensor(ab, ab, ab2, op=ALU.add)
            s_g.inc(g.engine_nop())
        emit("gpsimd", gfn)
        accb_done = s_g.v

        def vfn2(v):
            s_a.wait(v, au_done)
            for (_, _, w), off in zip(jobs, offs):
                au = tmp.au[:, off:off + w]
                aa = tmp.aA[:, off:off + w]
                v.tensor_single_scalar(aa, au, 0.25, op=ALU.is_gt)
                for th in (0.75, 1.25, 1.75):
                    v.scalar_tensor_tensor(aa, au, th, aa, op0=ALU.is_gt, op1=ALU.add)
            s_g.wait(v, accb_done)
            for (_, dst, w), off in zip(jobs, offs):
                g0 = off // 16
                gc = w // 16
                aa = tmp.aA[:, off:off + w]
                ab = tmp.aB[:, off:off + w]
                usl = tmp.u[:, off:off + w]
                v.scalar_tensor_tensor(aa, ab, 2.0, aa, op0=ALU.mult, op1=ALU.add)
                v.tensor_tensor(usl, aa, tmp.sg[:, off:off + w], op=ALU.mult)
                v.tensor_tensor(dst.rearrange("p (g e) -> p g e", e=16),
                                usl.rearrange("p (g e) -> p g e", e=16),
                                tmp.sc[:, g0:g0 + gc].unsqueeze(2)
                                .broadcast_to([128, gc, 16]), op=ALU.mult)
            s_v.inc(v.engine_nop())
        emit("vector", vfn2)
        return s_v.v

    def xtile_in_ap(b, t0):
        return x_h.ap()[b, t0:t0 + TT, :].rearrange("(s p) d -> p s d", p=128)

    def xtile_out_ap(b, t0):
        return out_h.ap()[b, t0:t0 + TT, :].rearrange("(s p) d -> p s d", p=128)

    def init_consts(v):
        v.memset(cFeps[:], float(D) * float(np.float32(1e-5)))
        v.memset(cMh[:], -0.5)
        v.memset(cHlf[:], float(np.float32(0.5 * np.log(float(D)))))
        v.memset(cKvp[:], 0.0)
        v.memset(cKsp[:], 0.0)
        s_v.inc(v.engine_nop())
    emit("vector", init_consts)

    def load_biases(y):
        for src, dst in ((bq_h, cBq), (bk_h, cBk), (bv_h, cBv)):
            s_dx.inc(y.dma_start(out=dst[:], in_=src.ap().rearrange(
                "(c p) -> p c", p=128), allow_slow_non_contiguous=True), 16)
        s_dx.inc(y.dma_start(
            out=cBoR[:], in_=bo_h.ap().unsqueeze(0).partition_broadcast(128)), 16)
    emit("sync", load_biases)
    biases_ready = s_dx.v

    with contextlib.ExitStack() as pp:
        stg = [pp.enter_context(nc.sbuf_tensor(f"wstg{i}", [128, D], BF16))
               for i in range(2)]
        prev_store = 0
        for wi, (src, dst) in enumerate((((wq_h, wqB), (wk_h, wkB),
                                         (wv_h, wvB), (wo_h, woB))
                                         if "cast" in stages else ())):
            for c in range(DC):
                buf = stg[(wi * DC + c) % 2]

                def gload(g, src=src, c=c, buf=buf, prev_store=prev_store):
                    s_dsp.wait(g, prev_store)
                    s_dg.inc(g.dma_start(out=buf[:],
                                         in_=src.ap()[c * 128:(c + 1) * 128, :]), 16)
                emit("gpsimd", gload)
                ld_done = s_dg.v

                def sstore(a, dst=dst, c=c, buf=buf, ld_done=ld_done):
                    s_dg.wait(a, ld_done)
                    s_dsp.inc(a.dma_start(out=dst.ap()[c * 128:(c + 1) * 128, :],
                                          in_=buf[:]), 16)
                emit("scalar", sstore)
                if wi * DC + c >= 1:
                    prev_store = s_dsp.v - 16

        tmpP = QTmp(pp, nc, "qp", DI)
        wfin = pp.enter_context(nc.sbuf_tensor("wfin", [128, DI], F32))
        wfout = pp.enter_context(nc.sbuf_tensor("wfout", [128, DI], BF16))
        jobs = [(wg_h, wgN, r, D) for r in range(IC)]
        jobs += [(wu_h, wuN, r, D) for r in range(IC)]
        jobs += [(wd_h, wdN, r, DI) for r in range(DC)]
        if "prep" not in stages:
            jobs = []
        for src, dst, r, K in jobs:
            def yload(y, src=src, r=r, K=K):
                s_v.wait(y, s_v.v)
                s_dsp.wait(y, s_dsp.v)
                s_dx.inc(y.dma_start(out=wfin[:, 0:K],
                                     in_=src.ap()[r * 128:(r + 1) * 128, :]), 16)
            emit("sync", yload)
            ld_done = s_dx.v

            emit("vector", lambda v, ld_done=ld_done: s_dx.wait(v, ld_done))
            if K == D:
                ready = quant_emit(tmpP, [(wfin[:, 0:D], wfout[:, 0:D], D)])
            else:
                ready = quant_emit(tmpP, [(wfin[:, a0:a1], wfout[:, a0:a1], a1 - a0)
                                          for (a0, a1) in QT])

            def sstore(a, dst=dst, r=r, K=K, ready=ready):
                s_v.wait(a, ready)
                s_dsp.inc(a.dma_start(out=dst.ap()[r * 128:(r + 1) * 128, :],
                                      in_=wfout[:, 0:K]), 16)
            emit("scalar", sstore)
        barrier()

    for b in range(nb if ("p1" in stages) else 0):
        if b > 0:
            def vreset(v):
                v.memset(cKvp[:], 0.0)
                v.memset(cKsp[:], 0.0)
                s_v.inc(v.engine_nop())
            emit("vector", vreset)

        with contextlib.ExitStack() as p1:
            cXt = [p1.enter_context(nc.sbuf_tensor(f"p1xt{i}_{b}", [128, NSUB * D],
                                                   F32)) for i in range(2)]
            cW = {nm: p1.enter_context(nc.sbuf_tensor(f"p1w{nm}_{b}",
                                                      [128, DC * D], BF16))
                  for nm in ("q", "k", "v")}
            cXnT = p1.enter_context(nc.sbuf_tensor(
                f"p1xnt_{b}", [128, DC * 512], BF16, is_transpose_dest=True))
            cEv = [p1.enter_context(nc.sbuf_tensor(f"p1ev{i}_{b}", [128, 512], F32))
                   for i in range(7)]
            cScr = p1.enter_context(nc.sbuf_tensor(f"p1scr_{b}", [128, 512], F32))
            tmp1 = QTmp(p1, nc, f"q1_{b}", D)

            def ywl(y):
                for nm, srcB in (("q", wqB), ("k", wkB), ("v", wvB)):
                    for c in range(DC):
                        s_dx.inc(y.dma_start(
                            out=cW[nm][:, c * D:(c + 1) * D],
                            in_=srcB.ap()[c * 128:(c + 1) * 128, :]), 16)
            emit("sync", ywl)
            w_ready = s_dx.v
            if b == 0:
                emit("vector", lambda v: s_dx.wait(v, biases_ready))

            for t in range(NT):
                t0 = t * TT
                xbuf = cXt[t % 2]

                def yx(y, b=b, t0=t0, xbuf=xbuf):
                    s_a.wait(y, s_a.v)
                    s_dx.inc(y.dma_start(out=xbuf[:], in_=xtile_in_ap(b, t0)), 16)
                emit("sync", yx)
                x_ready = s_dx.v

                prev_v = s_v.v

                def aS(a, xbuf=xbuf, x_ready=x_ready, prev_v=prev_v):
                    s_dx.wait(a, x_ready)
                    for s_i in range(NSUB):
                        if s_i >= 2:
                            s_v.wait(a, prev_v + s_i - 1)
                        s_a.inc(a.activation(cSq[s_i % 2][:],
                                xbuf[:, s_i * D:(s_i + 1) * D], AF.Square))
                emit("scalar", aS)
                sq_base = s_a.v - NSUB

                def vStats(v, xbuf=xbuf, x_ready=x_ready, sq_base=sq_base):
                    s_dx.wait(v, x_ready)
                    for s_i in range(NSUB):
                        v.tensor_reduce(cSsum[:, s_i:s_i + 1],
                                        xbuf[:, s_i * D:(s_i + 1) * D],
                                        axis=AX.X, op=ALU.add)
                        s_a.wait(v, sq_base + s_i + 1)
                        v.tensor_reduce(cSsq[:, s_i:s_i + 1], cSq[s_i % 2][:],
                                        axis=AX.X, op=ALU.add)
                        s_v.inc(v.engine_nop())
                    v.drain()
                    v.tensor_tensor(cTv[:], cSsum[:], cSsum[:], op=ALU.mult)
                    v.drain()
                    v.scalar_tensor_tensor(cVar[:], cTv[:], -INV_F, cSsq[:],
                                           op0=ALU.mult, op1=ALU.add)
                    v.drain()
                    s_v.inc(v.engine_nop())
                emit("vector", vStats)
                var_done = s_v.v

                def aR(a, var_done=var_done):
                    s_v.wait(a, var_done)
                    a.activation(cLv[:], cVar[:], AF.Ln, bias=cFeps[:, 0:1])
                    last = a.activation(cRstd[:], cLv[:], AF.Exp,
                                        scale=cMh[:, 0:1], bias=cHlf[:, 0:1])
                    a.drain()
                    s_a.inc(last)
                emit("scalar", aR)
                rstd_done = s_a.v

                def vN(v, rstd_done=rstd_done):
                    s_a.wait(v, rstd_done)
                    v.scalar_tensor_tensor(cNmr[:], cSsum[:], -INV_F, cRstd[:],
                                           op0=ALU.mult, op1=ALU.mult)
                    v.drain()
                    s_v.inc(v.engine_nop())
                emit("vector", vN)
                nmr_done = s_v.v

                for s_i in range(NSUB if p1_quant else 0):
                    prev_xnb_spill = s_dx.v
                    prev_xq_spill = s_dsp.v

                    def aXn(a, xbuf=xbuf, s_i=s_i, nmr_done=nmr_done):
                        s_v.wait(a, s_v.v)
                        s_dx.wait(a, s_dx.v)
                        a.activation(cXn[:], xbuf[:, s_i * D:(s_i + 1) * D],
                                     AF.Identity, bias=cNmr[:, s_i:s_i + 1],
                                     scale=cRstd[:, s_i:s_i + 1])
                        s_a.inc(a.activation(
                            cXnb[:], xbuf[:, s_i * D:(s_i + 1) * D],
                            AF.Identity, bias=cNmr[:, s_i:s_i + 1],
                            scale=cRstd[:, s_i:s_i + 1]))
                    emit("scalar", aXn)
                    xn_done = s_a.v

                    def ySpXnb(y, b=b, t0=t0, s_i=s_i, xn_done=xn_done):
                        s_a.wait(y, xn_done)
                        row0 = t0 + s_i * 128
                        s_dx.inc(y.dma_start(
                            out=xnS.ap()[b, row0:row0 + 128, :], in_=cXnb[:]), 16)
                    emit("sync", ySpXnb)

                    def vwq(v, xn_done=xn_done, prev_xq_spill=prev_xq_spill):
                        s_a.wait(v, xn_done)
                        s_dsp.wait(v, prev_xq_spill)
                    emit("vector", vwq)
                    q_ready = quant_emit(tmp1, [(cXn[:], cXqb[:], D)])

                    def aSpXq(a, b=b, t0=t0, s_i=s_i, q_ready=q_ready):
                        s_v.wait(a, q_ready)
                        row0 = t0 + s_i * 128
                        s_dsp.inc(a.dma_start(
                            out=xqS.ap()[b, row0:row0 + 128, :], in_=cXqb[:]), 16)
                    emit("scalar", aSpXq)

                if not p1_xpose:
                    continue
                def yT(y, b=b, t0=t0):
                    s_dx.wait(y, s_dx.v)
                    s_t.wait(y, s_t.v)
                    for c in range(DC):
                        s_dx.inc(y.dma_start_transpose(
                            out=cXnT[:, c * 512:(c + 1) * 512],
                            in_=xnS.ap()[b, t0:t0 + TT, c * 128:(c + 1) * 128]), 16)
                emit("sync", yT)
                xnt_ready = s_dx.v

                for oc in range(DC if p1_qkv else 0):
                    bq3 = [ps[(oc % 2) * 3 + i] for i in range(3)]

                    def tQKV(t_, oc=oc, bq3=bq3, xnt_ready=xnt_ready,
                             w_ready=w_ready):
                        s_dx.wait(t_, xnt_ready)
                        s_dx.wait(t_, w_ready)
                        s_v.wait(t_, s_v.v)
                        last = None
                        for pi, nm in enumerate(("q", "k", "v")):
                            for icc in range(DC):
                                last = t_.matmul(
                                    bq3[pi][:],
                                    cW[nm][:, icc * D + oc * 128:
                                           icc * D + (oc + 1) * 128],
                                    cXnT[:, icc * 512:(icc + 1) * 512],
                                    start=(icc == 0), stop=(icc == DC - 1))
                        s_t.inc(last)
                    emit("tensor", tQKV)
                    qkv_done = s_t.v

                    uq, uk, vp, tm, ee, Kc, Qc = cEv
                    if p1_ev == 0:
                        continue

                    def vEvac(v, oc=oc, bq3=bq3, qkv_done=qkv_done):
                        s_t.wait(v, qkv_done)
                        v.tensor_scalar(uq[:], bq3[0][:], cBq[:, oc:oc + 1], None,
                                        op0=ALU.add)
                        v.tensor_scalar(uk[:], bq3[1][:], cBk[:, oc:oc + 1], None,
                                        op0=ALU.add)
                        v.tensor_scalar(vp[:], bq3[2][:], cBv[:, oc:oc + 1], None,
                                        op0=ALU.add)
                        s_v.inc(v.engine_nop())
                        v.tensor_single_scalar(tm[:], uq[:], 0.0, op=ALU.min)
                        s_v.inc(v.engine_nop())
                    emit("vector", vEvac)
                    u_done = s_v.v
                    if p1_ev == 1:
                        continue

                    def aExpQ(a, u_done=u_done):
                        s_v.wait(a, u_done)
                        s_a.inc(a.activation(ee[:], tm[:], AF.Exp))
                    emit("scalar", aExpQ)
                    eq_done = s_a.v

                    def vQ(v, eq_done=eq_done, dsp_now=s_dsp.v):
                        s_a.wait(v, eq_done)
                        s_dsp.wait(v, dsp_now)
                        v.scalar_tensor_tensor(Qc[:], uq[:], 0.0, ee[:],
                                               op0=ALU.max, op1=ALU.add)
                        v.tensor_single_scalar(tm[:], uk[:], 0.0, op=ALU.min)
                        s_v.inc(v.engine_nop())
                    emit("vector", vQ)
                    q_done = s_v.v
                    if p1_ev == 2:
                        continue

                    def aSpQ(a, b=b, oc=oc, t0=t0, q_done=q_done):
                        s_v.wait(a, q_done)
                        s_dsp.inc(a.dma_start(
                            out=qS.ap()[b, oc * 128:(oc + 1) * 128, t0:t0 + TT],
                            in_=Qc[:]), 16)
                        s_a.inc(a.activation(ee[:], tm[:], AF.Exp))
                    emit("scalar", aSpQ)
                    ek_done = s_a.v
                    if p1_ev == 3:
                        continue

                    def vK(v, oc=oc, t=t, ek_done=ek_done):
                        s_a.wait(v, ek_done)
                        col = t * DC + oc
                        v.scalar_tensor_tensor(Kc[:], uk[:], 0.0, ee[:],
                                               op0=ALU.max, op1=ALU.add,
                                               accum_out=cKsp[:, col:col + 1])
                        v.scalar_tensor_tensor(cScr[:], Kc[:], 1.0, vp[:],
                                               op0=ALU.mult, op1=ALU.mult,
                                               accum_out=cKvp[:, col:col + 1])
                        s_v.inc(v.engine_nop())
                    emit("vector", vK)

            def vKvFin(v):
                v.drain()
                v.tensor_reduce(cKv8[:],
                                cKvp[:].rearrange("p (t c) -> p c t", c=DC),
                                axis=AX.X, op=ALU.add)
                v.tensor_reduce(cKs8[:],
                                cKsp[:].rearrange("p (t c) -> p c t", c=DC),
                                axis=AX.X, op=ALU.add)
                v.drain()
                s_v.inc(v.engine_nop())
            emit("vector", vKvFin)

        barrier()

        if "p2a" not in stages:
            continue
        with contextlib.ExitStack() as p2:
            cWg = p2.enter_context(nc.sbuf_tensor(
                f"p2wg_{b}", [128, DC * DI], BF16, is_transpose_dest=True))
            cWu = p2.enter_context(nc.sbuf_tensor(
                f"p2wu_{b}", [128, DC * DI], BF16, is_transpose_dest=True))
            cXqT = p2.enter_context(nc.sbuf_tensor(
                f"p2xqt_{b}", [128, DC * 512], BF16, is_transpose_dest=True))
            cH = p2.enter_context(nc.sbuf_tensor(f"p2h_{b}", [128, DI], F32))
            cSl = p2.enter_context(nc.sbuf_tensor(f"p2sl_{b}", [128, 512], F32))
            cHq = p2.enter_context(nc.sbuf_tensor(f"p2hq_{b}", [128, DI], BF16))
            tmp2 = QTmp(p2, nc, f"q2_{b}", DI)

            def ywg(y):
                for c in range(DC):
                    s_dx.inc(y.dma_start_transpose(
                        out=cWg[:, c * DI:(c + 1) * DI],
                        in_=wgN.ap()[:, c * 128:(c + 1) * 128]), 16)
                    s_dx.inc(y.dma_start_transpose(
                        out=cWu[:, c * DI:(c + 1) * DI],
                        in_=wuN.ap()[:, c * 128:(c + 1) * 128]), 16)
            emit("sync", ywg)
            wgu_ready = s_dx.v

            for t in range(NT):
                t0 = t * TT

                def yxq(y, b=b, t0=t0):
                    s_t.wait(y, s_t.v)
                    for c in range(DC):
                        s_dx.inc(y.dma_start_transpose(
                            out=cXqT[:, c * 512:(c + 1) * 512],
                            in_=xqS.ap()[b, t0:t0 + TT, c * 128:(c + 1) * 128]), 16)
                emit("sync", yxq)
                xqt_ready = s_dx.v

                for s_i in range(NSUB):
                    for io, (o0, o1) in enumerate(OSL):
                        bG = ps[(io % 2) * 2 + 0]
                        bU = ps[(io % 2) * 2 + 1]
                        ow = o1 - o0

                        def tGU(t_, s_i=s_i, o0=o0, o1=o1, ow=ow, bG=bG, bU=bU,
                                xqt_ready=xqt_ready, wgu_ready=wgu_ready):
                            s_dx.wait(t_, xqt_ready)
                            s_dx.wait(t_, wgu_ready)
                            s_v.wait(t_, s_v.v)
                            last = None
                            for icc in range(DC):
                                st_ap = cXqT[:, icc * 512 + s_i * 128:
                                             icc * 512 + (s_i + 1) * 128]
                                t_.matmul(bG[:, 0:ow], st_ap,
                                          cWg[:, icc * DI + o0:icc * DI + o1],
                                          start=(icc == 0), stop=(icc == DC - 1))
                                last = t_.matmul(
                                    bU[:, 0:ow], st_ap,
                                    cWu[:, icc * DI + o0:icc * DI + o1],
                                    start=(icc == 0), stop=(icc == DC - 1))
                            s_t.inc(last)
                        emit("tensor", tGU)
                        gu_done = s_t.v

                        def aSilu(a, bG=bG, ow=ow, gu_done=gu_done):
                            s_t.wait(a, gu_done)
                            s_v.wait(a, s_v.v)
                            s_a.inc(a.activation(cSl[:, 0:ow], bG[:, 0:ow],
                                                 AF.Silu))
                        emit("scalar", aSilu)
                        sl_done = s_a.v

                        def vH(v, bU=bU, o0=o0, o1=o1, ow=ow, sl_done=sl_done):
                            s_a.wait(v, sl_done)
                            v.tensor_tensor(cH[:, o0:o1], cSl[:, 0:ow],
                                            bU[:, 0:ow], op=ALU.mult)
                            s_v.inc(v.engine_nop())
                        emit("vector", vH)

                    prev_hq_spill = s_dsp.v

                    def vwh(v, prev_hq_spill=prev_hq_spill):
                        s_dsp.wait(v, prev_hq_spill)
                    emit("vector", vwh)
                    hq_ready = quant_emit(
                        tmp2, [(cH[:, a0:a1], cHq[:, a0:a1], a1 - a0)
                               for (a0, a1) in QT])

                    def aSpHq(a, b=b, t0=t0, s_i=s_i, hq_ready=hq_ready):
                        s_v.wait(a, hq_ready)
                        row0 = t0 + s_i * 128
                        s_dsp.inc(a.dma_start(
                            out=hqS.ap()[b, row0:row0 + 128, :], in_=cHq[:]), 16)
                    emit("scalar", aSpHq)

        barrier()

        if "p2b" not in stages:
            continue
        with contextlib.ExitStack() as p3:
            cWd = p3.enter_context(nc.sbuf_tensor(
                f"p3wd_{b}", [128, IC * D], BF16, is_transpose_dest=True))
            cWo = p3.enter_context(nc.sbuf_tensor(f"p3wo_{b}", [128, DC * D], BF16))
            cHqT = p3.enter_context(nc.sbuf_tensor(
                f"p3hqt_{b}", [128, IC * 512], BF16, is_transpose_dest=True))
            cQ = p3.enter_context(nc.sbuf_tensor(f"p3q_{b}", [128, DC * 512], F32))
            cA = p3.enter_context(nc.sbuf_tensor(f"p3a_{b}", [128, DC * 512], BF16))
            cZa = p3.enter_context(nc.sbuf_tensor(f"p3za_{b}", [128, 512], F32))
            cRz = p3.enter_context(nc.sbuf_tensor(f"p3rz_{b}", [128, DC * 512], F32))
            cGi = p3.enter_context(nc.sbuf_tensor(f"p3gi_{b}", [128, 512], F32))
            cGo = p3.enter_context(nc.sbuf_tensor(f"p3go_{b}", [128, 512], F32))
            cOut = p3.enter_context(nc.sbuf_tensor(f"p3out_{b}", [128, NSUB * D],
                                                   F32))
            cXr = p3.enter_context(nc.sbuf_tensor(f"p3xr_{b}", [128, NSUB * D],
                                                  F32))

            def ywd(y):
                for c in range(IC):
                    s_dx.inc(y.dma_start_transpose(
                        out=cWd[:, c * D:(c + 1) * D],
                        in_=wdN.ap()[:, c * 128:(c + 1) * 128]), 16)
                for c in range(DC):
                    s_dx.inc(y.dma_start(
                        out=cWo[:, c * D:(c + 1) * D],
                        in_=woB.ap()[c * 128:(c + 1) * 128, :]), 16)
            emit("sync", ywd)
            wdo_ready = s_dx.v

            for t in range(NT):
                t0 = t * TT

                def yld(y, b=b, t0=t0):
                    s_v.wait(y, s_v.v)
                    s_t.wait(y, s_t.v)
                    for c in range(IC):
                        s_dx.inc(y.dma_start_transpose(
                            out=cHqT[:, c * 512:(c + 1) * 512],
                            in_=hqS.ap()[b, t0:t0 + TT, c * 128:(c + 1) * 128]), 16)
                    for c in range(DC):
                        s_dx.inc(y.dma_start(
                            out=cQ[:, c * 512:(c + 1) * 512],
                            in_=qS.ap()[b, c * 128:(c + 1) * 128, t0:t0 + TT]), 16)
                    s_dx.inc(y.dma_start(out=cXr[:], in_=xtile_in_ap(b, t0)), 16)
                emit("sync", yld)
                ld_ready = s_dx.v

                prev_out_spill = s_dsp.v

                def vA(v, ld_ready=ld_ready, prev_out_spill=prev_out_spill):
                    s_dx.wait(v, ld_ready)
                    s_dsp.wait(v, prev_out_spill)
                    for c in range(DC):
                        v.tensor_scalar(cZa[:], cQ[:, c * 512:(c + 1) * 512],
                                        cKs8[:, c:c + 1], 1e-6,
                                        op0=ALU.mult, op1=ALU.add)
                        v.reciprocal(cRz[:, c * 512:(c + 1) * 512], cZa[:])
                    v.drain()
                    for c in range(DC):
                        v.scalar_tensor_tensor(
                            cA[:, c * 512:(c + 1) * 512],
                            cQ[:, c * 512:(c + 1) * 512], cKv8[:, c:c + 1],
                            cRz[:, c * 512:(c + 1) * 512],
                            op0=ALU.mult, op1=ALU.mult)
                    s_v.inc(v.engine_nop())
                emit("vector", vA)
                a_ready = s_v.v

                for s_i in range(NSUB):
                    par = s_i % 2
                    bD = [ps[par * 4 + 0], ps[par * 4 + 1]]
                    bW = [ps[par * 4 + 2], ps[par * 4 + 3]]

                    def tDW(t_, s_i=s_i, bD=bD, bW=bW, ld_ready=ld_ready,
                            a_ready=a_ready, wdo_ready=wdo_ready):
                        s_dx.wait(t_, ld_ready)
                        s_dx.wait(t_, wdo_ready)
                        s_v.wait(t_, s_v.v)
                        for icc in range(IC):
                            st_ap = cHqT[:, icc * 512 + s_i * 128:
                                         icc * 512 + (s_i + 1) * 128]
                            t_.matmul(bD[0][:], st_ap,
                                      cWd[:, icc * D:icc * D + 512],
                                      start=(icc == 0), stop=(icc == IC - 1))
                            t_.matmul(bD[1][:], st_ap,
                                      cWd[:, icc * D + 512:(icc + 1) * D],
                                      start=(icc == 0), stop=(icc == IC - 1))
                        last = None
                        for c in range(DC):
                            st_ap = cA[:, c * 512 + s_i * 128:
                                       c * 512 + (s_i + 1) * 128]
                            t_.matmul(bW[0][:], st_ap, cWo[:, c * D:c * D + 512],
                                      start=(c == 0), stop=(c == DC - 1))
                            last = t_.matmul(bW[1][:], st_ap,
                                             cWo[:, c * D + 512:(c + 1) * D],
                                             start=(c == 0), stop=(c == DC - 1))
                        s_t.inc(last)
                    emit("tensor", tDW)
                    dw_done = s_t.v

                    for half in range(2):
                        def vGi(v, half=half, bW=bW, dw_done=dw_done):
                            s_t.wait(v, dw_done)
                            v.tensor_tensor(cGi[:], bW[half][:],
                                            cBoR[:, half * 512:(half + 1) * 512],
                                            op=ALU.add)
                            s_v.inc(v.engine_nop())
                        emit("vector", vGi)
                        gi_done = s_v.v

                        def aG(a, gi_done=gi_done):
                            s_v.wait(a, gi_done)
                            s_a.inc(a.activation(cGo[:], cGi[:],
                                                 AF.Gelu_apprx_tanh))
                        emit("scalar", aG)
                        g_done = s_a.v

                        def vOut(v, s_i=s_i, half=half, bD=bD, g_done=g_done):
                            s_a.wait(v, g_done)
                            off = s_i * D + half * 512
                            v.tensor_tensor(cGi[:], cXr[:, off:off + 512],
                                            cGo[:], op=ALU.add)
                            v.tensor_tensor(cOut[:, off:off + 512], cGi[:],
                                            bD[half][:], op=ALU.add)
                            s_v.inc(v.engine_nop())
                        emit("vector", vOut)

                out_ready = s_v.v

                def aOut(a, b=b, t0=t0, out_ready=out_ready):
                    s_v.wait(a, out_ready)
                    s_dsp.inc(a.dma_start(out=xtile_out_ap(b, t0), in_=cOut[:]), 16)
                emit("scalar", aOut)

        barrier()

    emit("gpsimd", lambda g: s_dsp.wait(g, s_dsp.v))
    emit("scalar", lambda a: s_dsp.wait(a, s_dsp.v))

    es.close()
    return nc


_NC_CACHE = {}


def kernel(L0_attn_x, L0_attn_Wq, L0_attn_bq, L0_attn_Wk, L0_attn_bk,
           L0_attn_Wv, L0_attn_bv, L0_attn_Wo, L0_attn_bo,
           L0_ffn_w_gate, L0_ffn_w_up, L0_ffn_w_down):
    x = np.ascontiguousarray(np.asarray(L0_attn_x, dtype=np.float32))
    assert x.shape == (32, 2048, 1024)
    weights = {
        "wq": np.ascontiguousarray(np.asarray(L0_attn_Wq, np.float32)),
        "wk": np.ascontiguousarray(np.asarray(L0_attn_Wk, np.float32)),
        "wv": np.ascontiguousarray(np.asarray(L0_attn_Wv, np.float32)),
        "wo": np.ascontiguousarray(np.asarray(L0_attn_Wo, np.float32)),
        "bq": np.ascontiguousarray(np.asarray(L0_attn_bq, np.float32)),
        "bk": np.ascontiguousarray(np.asarray(L0_attn_bk, np.float32)),
        "bv": np.ascontiguousarray(np.asarray(L0_attn_bv, np.float32)),
        "bo": np.ascontiguousarray(np.asarray(L0_attn_bo, np.float32)),
        "wg": np.ascontiguousarray(np.asarray(L0_ffn_w_gate, np.float32)),
        "wu": np.ascontiguousarray(np.asarray(L0_ffn_w_up, np.float32)),
        "wd": np.ascontiguousarray(np.asarray(L0_ffn_w_down, np.float32)),
    }
    if "nc" not in _NC_CACHE:
        _NC_CACHE["nc"] = build_program()
    nc = _NC_CACHE["nc"]
    in_maps = []
    for core in range(NCORES):
        m = dict(weights)
        m["x"] = x[core * NB:(core + 1) * NB]
        in_maps.append(m)
    res = run_bass_kernel_spmd(nc, in_maps, core_ids=list(range(NCORES)))
    out = np.concatenate([np.asarray(r["out"]) for r in res.results], axis=0)
    return out.astype(np.float32)


# version 10
# speedup vs baseline: 1.8990x; 1.8990x over previous
import contextlib
import numpy as np

import concourse.bass as bass
import concourse.mybir as mybir
from concourse.bass_utils import run_bass_kernel_spmd

F32 = mybir.dt.float32


def _preload_set(a, nc, name):
    from concourse.hw_specs import get_activation_tables
    names = list(get_activation_tables(nc.m.arch).keys())
    a.add_instruction(mybir.InstLoadActFuncSet(
        name=nc.get_next_instruction_name(),
        act_func_set_id=names.index(name), ins=[], outs=[]))

BF16 = mybir.dt.bfloat16
AF = mybir.ActivationFunctionType
ALU = mybir.AluOpType
AX = mybir.AxisListType

NCORES = 8
NB = 4
S = 2048
D = 1024
DI = 2816
TT = 512
NT = S // TT
NSUB = TT // 128
DC = D // 128
IC = DI // 128
C12 = 1.0 / 12.0
INV_F = 1.0 / float(D)
OSL = [(0, 512), (512, 1024), (1024, 1536), (1536, 2048), (2048, 2560), (2560, 2816)]
QT = [(0, 1408), (1408, 2816)]


class Ctr:
    def __init__(self, es, nc, name):
        self.h = es.enter_context(nc.semaphore(name))
        self.v = 0

    def inc(self, inst, k=1):
        inst.then_inc(self.h, k)
        self.v += k
        return self.v

    def wait(self, eng, v=None):
        eng.wait_ge(self.h, self.v if v is None else v)


class QTmp:

    def __init__(self, scope, nc, tag, W):
        self.W = W
        sb = lambda n, s, d: scope.enter_context(nc.sbuf_tensor(n, s, d))
        self.u = sb(f"{tag}_u", [128, W], F32)
        self.au = sb(f"{tag}_au", [128, W], F32)
        self.sg = sb(f"{tag}_sg", [128, W], F32)
        self.aA = sb(f"{tag}_aA", [128, W], F32)
        self.aB = sb(f"{tag}_aB", [128, W], F32)
        self.aB2 = sb(f"{tag}_aB2", [128, W], F32)
        G = W // 16
        self.gm = sb(f"{tag}_gm", [128, G], F32)
        self.sc = sb(f"{tag}_sc", [128, G], F32)
        self.r = sb(f"{tag}_r", [128, G], F32)
        self.r6 = sb(f"{tag}_r6", [128, G], F32)


def build_program(debug=False, stages=("cast", "prep", "p1", "p2a", "p2b"), nb=NB, p1_quant=True, p1_xpose=True, p1_qkv=True, p1_ev=4):
    nc = bass.Bass("TRN2", target_bir_lowering=False)

    x_h = nc.dram_tensor("x", [NB, S, D], F32, kind="ExternalInput")
    wq_h = nc.dram_tensor("wq", [D, D], F32, kind="ExternalInput")
    wk_h = nc.dram_tensor("wk", [D, D], F32, kind="ExternalInput")
    wv_h = nc.dram_tensor("wv", [D, D], F32, kind="ExternalInput")
    wo_h = nc.dram_tensor("wo", [D, D], F32, kind="ExternalInput")
    bq_h = nc.dram_tensor("bq", [D], F32, kind="ExternalInput")
    bk_h = nc.dram_tensor("bk", [D], F32, kind="ExternalInput")
    bv_h = nc.dram_tensor("bv", [D], F32, kind="ExternalInput")
    bo_h = nc.dram_tensor("bo", [D], F32, kind="ExternalInput")
    wg_h = nc.dram_tensor("wg", [DI, D], F32, kind="ExternalInput")
    wu_h = nc.dram_tensor("wu", [DI, D], F32, kind="ExternalInput")
    wd_h = nc.dram_tensor("wd", [D, DI], F32, kind="ExternalInput")
    out_h = nc.dram_tensor("out", [NB, S, D], F32, kind="ExternalOutput")

    kscr = "ExternalOutput" if debug else "Internal"
    wgN = nc.dram_tensor("wgN", [DI, D], BF16, kind=kscr)
    wuN = nc.dram_tensor("wuN", [DI, D], BF16, kind=kscr)
    wdN = nc.dram_tensor("wdN", [D, DI], BF16, kind=kscr)
    wqB = nc.dram_tensor("wqB", [D, D], BF16, kind="Internal")
    wkB = nc.dram_tensor("wkB", [D, D], BF16, kind="Internal")
    wvB = nc.dram_tensor("wvB", [D, D], BF16, kind="Internal")
    woB = nc.dram_tensor("woB", [D, D], BF16, kind="Internal")
    xnS = nc.dram_tensor("xnS", [NB, S, D], BF16, kind=kscr)
    xqS = nc.dram_tensor("xqS", [NB, S, D], BF16, kind=kscr)
    hqS = nc.dram_tensor("hqS", [NB, S, DI], BF16, kind=kscr)
    qS = nc.dram_tensor("qS", [NB, D, S], F32, kind=kscr)

    es = contextlib.ExitStack()

    def sb(name, shape, dtype=F32, **kw):
        return es.enter_context(nc.sbuf_tensor(name, shape, dtype, **kw))

    cXn = sb("xn", [128, D])
    cXnb = sb("xnb", [128, D], BF16)
    cXqb = sb("xqb", [128, D], BF16)
    cSq = [sb("sq0", [128, D]), sb("sq1", [128, D])]
    cSsum = sb("ssum", [128, NSUB])
    cSsq = sb("ssq", [128, NSUB])
    cTv = sb("tv", [128, NSUB])
    cVar = sb("var", [128, NSUB])
    cLv = sb("lv", [128, NSUB])
    cRstd = sb("rstd", [128, NSUB])
    cNmr = sb("nmr", [128, NSUB])
    cKvp = sb("kvp", [128, NT * DC])
    cKsp = sb("ksp", [128, NT * DC])
    cKv8 = sb("kv8", [128, DC])
    cKs8 = sb("ks8", [128, DC])
    cBq = sb("bq_c", [128, DC])
    cBk = sb("bk_c", [128, DC])
    cBv = sb("bv_c", [128, DC])
    cBoR = sb("boR", [128, D])
    cFeps = sb("cfeps", [128, 1])
    cMh = sb("cmh", [128, 1])
    cHlf = sb("chlf", [128, 1])
    ps = [es.enter_context(nc.psum_tensor(f"ps{i}", [128, 512], F32)) for i in range(8)]

    s_dx = Ctr(es, nc, "dx")
    s_dsp = Ctr(es, nc, "dsp")
    s_dg = Ctr(es, nc, "dg")
    s_v = Ctr(es, nc, "sv")
    s_a = Ctr(es, nc, "sa")
    s_t = Ctr(es, nc, "st")
    s_g = Ctr(es, nc, "sg")
    s_bar = [Ctr(es, nc, f"bar{i}") for i in range(5)]

    blk = es.enter_context(nc.Block())
    ENG = ["vector", "scalar", "tensor", "gpsimd", "sync"]

    def emit(eng, fn):
        getattr(blk, eng)(fn)

    def barrier():
        for i, e in enumerate(ENG):
            def bump(g, i=i, e=e):
                if e in ("vector", "gpsimd"):
                    ins = g.engine_nop()
                elif e in ("tensor", "scalar"):
                    ins = g.drain()
                else:
                    ins = g.nop()
                s_bar[i].inc(ins)
            emit(e, bump)
        targets = [c.v for c in s_bar]
        for i, e in enumerate(ENG):
            def w(g, i=i):
                for j in range(5):
                    if j != i:
                        s_bar[j].wait(g, targets[j])
            emit(e, w)

    def quant_emit(tmp, jobs):
        offs = []
        o = 0
        for (_, _, w) in jobs:
            assert w % 16 == 0
            offs.append(o)
            o += w
        assert o <= tmp.W

        g_prev = s_g.v

        def vfn(v):
            s_g.wait(v, g_prev)
            for (src, _, w), off in zip(jobs, offs):
                g0 = off // 16
                gc = w // 16
                v.tensor_reduce(tmp.gm[:, g0:g0 + gc],
                                src.rearrange("p (g e) -> p g e", e=16),
                                axis=AX.X, op=ALU.max, apply_absolute_value=True)
            v.drain()
            for (src, _, w), off in zip(jobs, offs):
                g0 = off // 16
                gc = w // 16
                gsl = tmp.gm[:, g0:g0 + gc]
                v.tensor_scalar(tmp.sc[:, g0:g0 + gc], gsl, 6e-12, C12,
                                op0=ALU.max, op1=ALU.mult)
                v.reciprocal(tmp.r[:, g0:g0 + gc], gsl)
            v.drain()
            for (_, _, w), off in zip(jobs, offs):
                g0 = off // 16
                gc = w // 16
                v.tensor_scalar(tmp.r6[:, g0:g0 + gc], tmp.r[:, g0:g0 + gc],
                                6.0, 1e12, op0=ALU.mult, op1=ALU.min)
            v.drain()
            for (src, _, w), off in zip(jobs, offs):
                g0 = off // 16
                gc = w // 16
                v.tensor_tensor(tmp.u[:, off:off + w].rearrange(
                                    "p (g e) -> p g e", e=16),
                                src.rearrange("p (g e) -> p g e", e=16),
                                tmp.r6[:, g0:g0 + gc].unsqueeze(2)
                                .broadcast_to([128, gc, 16]), op=ALU.mult)
            s_v.inc(v.engine_nop())
        emit("vector", vfn)
        u_done = s_v.v

        def afn(a):
            s_v.wait(a, u_done)
            for (_, _, w), off in zip(jobs, offs):
                usl = tmp.u[:, off:off + w]
                a.activation(tmp.au[:, off:off + w], usl, AF.Abs)
                last = a.activation(tmp.sg[:, off:off + w], usl, AF.Sign)
            s_a.inc(last)
        emit("scalar", afn)
        au_done = s_a.v

        def vfn2(v):
            s_a.wait(v, au_done)
            for (_, _, w), off in zip(jobs, offs):
                au = tmp.au[:, off:off + w]
                aa = tmp.aA[:, off:off + w]
                ab = tmp.aB[:, off:off + w]
                v.tensor_single_scalar(aa, au, 0.25, op=ALU.is_gt)
                for th in (0.75, 1.25, 1.75):
                    v.scalar_tensor_tensor(aa, au, th, aa, op0=ALU.is_gt, op1=ALU.add)
                v.tensor_single_scalar(ab, au, 2.5, op=ALU.is_gt)
                for th in (3.5, 5.0, 5.0):
                    v.scalar_tensor_tensor(ab, au, th, ab, op0=ALU.is_gt, op1=ALU.add)
            for (_, _, w), off in zip(jobs, offs):
                aa = tmp.aA[:, off:off + w]
                ab = tmp.aB[:, off:off + w]
                v.scalar_tensor_tensor(aa, ab, 2.0, aa, op0=ALU.mult, op1=ALU.add)
            s_v.inc(v.engine_nop())
        emit("vector", vfn2)

        def vfn3(v):
            for (_, dst, w), off in zip(jobs, offs):
                g0 = off // 16
                gc = w // 16
                aa = tmp.aA[:, off:off + w]
                usl = tmp.u[:, off:off + w]
                v.tensor_tensor(usl, aa, tmp.sg[:, off:off + w], op=ALU.mult)
                v.tensor_tensor(dst.rearrange("p (g e) -> p g e", e=16),
                                usl.rearrange("p (g e) -> p g e", e=16),
                                tmp.sc[:, g0:g0 + gc].unsqueeze(2)
                                .broadcast_to([128, gc, 16]), op=ALU.mult)
            s_v.inc(v.engine_nop())
        emit("vector", vfn3)
        return (s_v, s_v.v)

    def xtile_in_ap(b, t0):
        return x_h.ap()[b, t0:t0 + TT, :].rearrange("(s p) d -> p s d", p=128)

    def xtile_out_ap(b, t0):
        return out_h.ap()[b, t0:t0 + TT, :].rearrange("(s p) d -> p s d", p=128)

    def init_consts(v):
        v.memset(cFeps[:], float(D) * float(np.float32(1e-5)))
        v.memset(cMh[:], -0.5)
        v.memset(cHlf[:], float(np.float32(0.5 * np.log(float(D)))))
        v.memset(cKvp[:], 0.0)
        v.memset(cKsp[:], 0.0)
        s_v.inc(v.engine_nop())
    emit("vector", init_consts)

    def load_biases(y):
        for src, dst in ((bq_h, cBq), (bk_h, cBk), (bv_h, cBv)):
            s_dx.inc(y.dma_start(out=dst[:], in_=src.ap().rearrange(
                "(c p) -> p c", p=128), allow_slow_non_contiguous=True), 16)
        s_dx.inc(y.dma_start(
            out=cBoR[:], in_=bo_h.ap().unsqueeze(0).partition_broadcast(128)), 16)
    emit("sync", load_biases)
    biases_ready = s_dx.v

    with contextlib.ExitStack() as pp:
        stg = [pp.enter_context(nc.sbuf_tensor(f"wstg{i}", [128, D], BF16))
               for i in range(2)]
        prev_store = 0
        for wi, (src, dst) in enumerate((((wq_h, wqB), (wk_h, wkB),
                                         (wv_h, wvB), (wo_h, woB))
                                         if "cast" in stages else ())):
            for c in range(DC):
                buf = stg[(wi * DC + c) % 2]

                def gload(g, src=src, c=c, buf=buf, prev_store=prev_store):
                    s_dsp.wait(g, prev_store)
                    s_dg.inc(g.dma_start(out=buf[:],
                                         in_=src.ap()[c * 128:(c + 1) * 128, :]), 16)
                emit("gpsimd", gload)
                ld_done = s_dg.v

                def sstore(a, dst=dst, c=c, buf=buf, ld_done=ld_done):
                    s_dg.wait(a, ld_done)
                    s_dsp.inc(a.dma_start(out=dst.ap()[c * 128:(c + 1) * 128, :],
                                          in_=buf[:]), 16)
                emit("scalar", sstore)
                if wi * DC + c >= 1:
                    prev_store = s_dsp.v - 16

        tmpP = QTmp(pp, nc, "qp", DI)
        wfin = pp.enter_context(nc.sbuf_tensor("wfin", [128, DI], F32))
        wfout = pp.enter_context(nc.sbuf_tensor("wfout", [128, DI], BF16))
        jobs = [(wg_h, wgN, r, D) for r in range(IC)]
        jobs += [(wu_h, wuN, r, D) for r in range(IC)]
        jobs += [(wd_h, wdN, r, DI) for r in range(DC)]
        if "prep" not in stages:
            jobs = []
        for src, dst, r, K in jobs:
            def yload(y, src=src, r=r, K=K):
                s_v.wait(y, s_v.v)
                s_g.wait(y, s_g.v)
                s_dsp.wait(y, s_dsp.v)
                s_dx.inc(y.dma_start(out=wfin[:, 0:K],
                                     in_=src.ap()[r * 128:(r + 1) * 128, :]), 16)
            emit("sync", yload)
            ld_done = s_dx.v

            emit("vector", lambda v, ld_done=ld_done: s_dx.wait(v, ld_done))
            if K == D:
                rsem, ready = quant_emit(tmpP, [(wfin[:, 0:D], wfout[:, 0:D], D)])
            else:
                rsem, ready = quant_emit(
                    tmpP, [(wfin[:, a0:a1], wfout[:, a0:a1], a1 - a0)
                           for (a0, a1) in QT])

            def sstore(a, dst=dst, r=r, K=K, rsem=rsem, ready=ready):
                rsem.wait(a, ready)
                s_dsp.inc(a.dma_start(out=dst.ap()[r * 128:(r + 1) * 128, :],
                                      in_=wfout[:, 0:K]), 16)
            emit("scalar", sstore)
        barrier()

    for b in range(nb if ("p1" in stages) else 0):
        if b > 0:
            def vreset(v):
                v.memset(cKvp[:], 0.0)
                v.memset(cKsp[:], 0.0)
                s_v.inc(v.engine_nop())
            emit("vector", vreset)

        with contextlib.ExitStack() as p1:
            cXt = [p1.enter_context(nc.sbuf_tensor(f"p1xt{i}_{b}", [128, NSUB * D],
                                                   F32)) for i in range(2)]
            cW = {nm: p1.enter_context(nc.sbuf_tensor(f"p1w{nm}_{b}",
                                                      [128, DC * D], BF16))
                  for nm in ("q", "k", "v")}
            cXnT = p1.enter_context(nc.sbuf_tensor(
                f"p1xnt_{b}", [128, DC * 512], BF16, is_transpose_dest=True))
            cEv = [p1.enter_context(nc.sbuf_tensor(f"p1ev{i}_{b}", [128, 512], F32))
                   for i in range(7)]
            cScr = p1.enter_context(nc.sbuf_tensor(f"p1scr_{b}", [128, 512], F32))
            tmp1 = QTmp(p1, nc, f"q1_{b}", D)

            def ywl(y):
                for nm, srcB in (("q", wqB), ("k", wkB), ("v", wvB)):
                    for c in range(DC):
                        s_dx.inc(y.dma_start(
                            out=cW[nm][:, c * D:(c + 1) * D],
                            in_=srcB.ap()[c * 128:(c + 1) * 128, :]), 16)
            emit("sync", ywl)
            w_ready = s_dx.v
            if b == 0:
                emit("vector", lambda v: s_dx.wait(v, biases_ready))

            for t in range(NT):
                t0 = t * TT
                xbuf = cXt[t % 2]

                def yx(y, b=b, t0=t0, xbuf=xbuf):
                    s_a.wait(y, s_a.v)
                    s_dx.inc(y.dma_start(out=xbuf[:], in_=xtile_in_ap(b, t0)), 16)
                emit("sync", yx)
                x_ready = s_dx.v

                prev_v = s_v.v

                def aS(a, xbuf=xbuf, x_ready=x_ready, prev_v=prev_v):
                    s_dx.wait(a, x_ready)
                    for s_i in range(NSUB):
                        if s_i >= 2:
                            s_v.wait(a, prev_v + s_i - 1)
                        s_a.inc(a.activation(cSq[s_i % 2][:],
                                xbuf[:, s_i * D:(s_i + 1) * D], AF.Square))
                emit("scalar", aS)
                sq_base = s_a.v - NSUB

                def vStats(v, xbuf=xbuf, x_ready=x_ready, sq_base=sq_base):
                    s_dx.wait(v, x_ready)
                    for s_i in range(NSUB):
                        v.tensor_reduce(cSsum[:, s_i:s_i + 1],
                                        xbuf[:, s_i * D:(s_i + 1) * D],
                                        axis=AX.X, op=ALU.add)
                        s_a.wait(v, sq_base + s_i + 1)
                        v.tensor_reduce(cSsq[:, s_i:s_i + 1], cSq[s_i % 2][:],
                                        axis=AX.X, op=ALU.add)
                        s_v.inc(v.engine_nop())
                    v.drain()
                    v.tensor_tensor(cTv[:], cSsum[:], cSsum[:], op=ALU.mult)
                    v.drain()
                    v.scalar_tensor_tensor(cVar[:], cTv[:], -INV_F, cSsq[:],
                                           op0=ALU.mult, op1=ALU.add)
                    v.drain()
                    s_v.inc(v.engine_nop())
                emit("vector", vStats)
                var_done = s_v.v

                def aR(a, var_done=var_done):
                    s_v.wait(a, var_done)
                    a.activation(cLv[:], cVar[:], AF.Ln, bias=cFeps[:, 0:1])
                    last = a.activation(cRstd[:], cLv[:], AF.Exp,
                                        scale=cMh[:, 0:1], bias=cHlf[:, 0:1])
                    a.drain()
                    s_a.inc(last)
                emit("scalar", aR)
                rstd_done = s_a.v

                def vN(v, rstd_done=rstd_done):
                    s_a.wait(v, rstd_done)
                    v.scalar_tensor_tensor(cNmr[:], cSsum[:], -INV_F, cRstd[:],
                                           op0=ALU.mult, op1=ALU.mult)
                    v.drain()
                    s_v.inc(v.engine_nop())
                emit("vector", vN)
                nmr_done = s_v.v

                for s_i in range(NSUB if p1_quant else 0):
                    prev_xnb_spill = s_dx.v
                    prev_xq_spill = s_dsp.v

                    def aXn(a, xbuf=xbuf, s_i=s_i, nmr_done=nmr_done):
                        s_v.wait(a, s_v.v)
                        s_dx.wait(a, s_dx.v)
                        a.activation(cXn[:], xbuf[:, s_i * D:(s_i + 1) * D],
                                     AF.Identity, bias=cNmr[:, s_i:s_i + 1],
                                     scale=cRstd[:, s_i:s_i + 1])
                        s_a.inc(a.activation(
                            cXnb[:], xbuf[:, s_i * D:(s_i + 1) * D],
                            AF.Identity, bias=cNmr[:, s_i:s_i + 1],
                            scale=cRstd[:, s_i:s_i + 1]))
                    emit("scalar", aXn)
                    xn_done = s_a.v

                    def ySpXnb(y, b=b, t0=t0, s_i=s_i, xn_done=xn_done):
                        s_a.wait(y, xn_done)
                        row0 = t0 + s_i * 128
                        s_dx.inc(y.dma_start(
                            out=xnS.ap()[b, row0:row0 + 128, :], in_=cXnb[:]), 16)
                    emit("sync", ySpXnb)

                    def vwq(v, xn_done=xn_done, prev_xq_spill=prev_xq_spill):
                        s_a.wait(v, xn_done)
                        s_dsp.wait(v, prev_xq_spill)
                    emit("vector", vwq)
                    q_sem, q_ready = quant_emit(tmp1, [(cXn[:], cXqb[:], D)])

                    def aSpXq(a, b=b, t0=t0, s_i=s_i, q_sem=q_sem, q_ready=q_ready):
                        q_sem.wait(a, q_ready)
                        row0 = t0 + s_i * 128
                        s_dsp.inc(a.dma_start(
                            out=xqS.ap()[b, row0:row0 + 128, :], in_=cXqb[:]), 16)
                    emit("scalar", aSpXq)

                if not p1_xpose:
                    continue
                def yT(y, b=b, t0=t0):
                    s_dx.wait(y, s_dx.v)
                    s_t.wait(y, s_t.v)
                    for c in range(DC):
                        s_dx.inc(y.dma_start_transpose(
                            out=cXnT[:, c * 512:(c + 1) * 512],
                            in_=xnS.ap()[b, t0:t0 + TT, c * 128:(c + 1) * 128]), 16)
                emit("sync", yT)
                xnt_ready = s_dx.v

                for oc in range(DC if p1_qkv else 0):
                    bq3 = [ps[(oc % 2) * 3 + i] for i in range(3)]

                    def tQKV(t_, oc=oc, bq3=bq3, xnt_ready=xnt_ready,
                             w_ready=w_ready):
                        s_dx.wait(t_, xnt_ready)
                        s_dx.wait(t_, w_ready)
                        s_v.wait(t_, s_v.v)
                        last = None
                        for pi, nm in enumerate(("q", "k", "v")):
                            for icc in range(DC):
                                last = t_.matmul(
                                    bq3[pi][:],
                                    cW[nm][:, icc * D + oc * 128:
                                           icc * D + (oc + 1) * 128],
                                    cXnT[:, icc * 512:(icc + 1) * 512],
                                    start=(icc == 0), stop=(icc == DC - 1))
                        s_t.inc(last)
                    emit("tensor", tQKV)
                    qkv_done = s_t.v

                    uq, uk, vp, tm, ee, Kc, Qc = cEv
                    if p1_ev == 0:
                        continue

                    def vEvac(v, oc=oc, bq3=bq3, qkv_done=qkv_done):
                        s_t.wait(v, qkv_done)
                        v.tensor_scalar(uq[:], bq3[0][:], cBq[:, oc:oc + 1], None,
                                        op0=ALU.add)
                        v.tensor_scalar(uk[:], bq3[1][:], cBk[:, oc:oc + 1], None,
                                        op0=ALU.add)
                        v.tensor_scalar(vp[:], bq3[2][:], cBv[:, oc:oc + 1], None,
                                        op0=ALU.add)
                        s_v.inc(v.engine_nop())
                        v.tensor_single_scalar(tm[:], uq[:], 0.0, op=ALU.min)
                        s_v.inc(v.engine_nop())
                    emit("vector", vEvac)
                    u_done = s_v.v
                    if p1_ev == 1:
                        continue

                    def aExpQ(a, u_done=u_done):
                        s_v.wait(a, u_done)
                        s_a.inc(a.activation(ee[:], tm[:], AF.Exp))
                    emit("scalar", aExpQ)
                    eq_done = s_a.v

                    def vQ(v, eq_done=eq_done, dsp_now=s_dsp.v):
                        s_a.wait(v, eq_done)
                        s_dsp.wait(v, dsp_now)
                        v.scalar_tensor_tensor(Qc[:], uq[:], 0.0, ee[:],
                                               op0=ALU.max, op1=ALU.add)
                        v.tensor_single_scalar(tm[:], uk[:], 0.0, op=ALU.min)
                        s_v.inc(v.engine_nop())
                    emit("vector", vQ)
                    q_done = s_v.v
                    if p1_ev == 2:
                        continue

                    def aSpQ(a, b=b, oc=oc, t0=t0, q_done=q_done):
                        s_v.wait(a, q_done)
                        s_dsp.inc(a.dma_start(
                            out=qS.ap()[b, oc * 128:(oc + 1) * 128, t0:t0 + TT],
                            in_=Qc[:]), 16)
                        s_a.inc(a.activation(ee[:], tm[:], AF.Exp))
                    emit("scalar", aSpQ)
                    ek_done = s_a.v
                    if p1_ev == 3:
                        continue

                    def vK(v, oc=oc, t=t, ek_done=ek_done):
                        s_a.wait(v, ek_done)
                        col = t * DC + oc
                        v.scalar_tensor_tensor(Kc[:], uk[:], 0.0, ee[:],
                                               op0=ALU.max, op1=ALU.add,
                                               accum_out=cKsp[:, col:col + 1])
                        v.scalar_tensor_tensor(cScr[:], Kc[:], 1.0, vp[:],
                                               op0=ALU.mult, op1=ALU.mult,
                                               accum_out=cKvp[:, col:col + 1])
                        s_v.inc(v.engine_nop())
                    emit("vector", vK)

            def vKvFin(v):
                v.drain()
                v.tensor_reduce(cKv8[:],
                                cKvp[:].rearrange("p (t c) -> p c t", c=DC),
                                axis=AX.X, op=ALU.add)
                v.tensor_reduce(cKs8[:],
                                cKsp[:].rearrange("p (t c) -> p c t", c=DC),
                                axis=AX.X, op=ALU.add)
                v.drain()
                s_v.inc(v.engine_nop())
            emit("vector", vKvFin)

        barrier()

        if "p2a" not in stages:
            continue
        with contextlib.ExitStack() as p2:
            cWg = p2.enter_context(nc.sbuf_tensor(
                f"p2wg_{b}", [128, DC * DI], BF16, is_transpose_dest=True))
            cWu = p2.enter_context(nc.sbuf_tensor(
                f"p2wu_{b}", [128, DC * DI], BF16, is_transpose_dest=True))
            cXqT = p2.enter_context(nc.sbuf_tensor(
                f"p2xqt_{b}", [128, DC * 512], BF16, is_transpose_dest=True))
            cH = p2.enter_context(nc.sbuf_tensor(f"p2h_{b}", [128, DI], F32))
            cSl = p2.enter_context(nc.sbuf_tensor(f"p2sl_{b}", [128, 512], F32))
            cHq = p2.enter_context(nc.sbuf_tensor(f"p2hq_{b}", [128, DI], BF16))
            tmp2 = QTmp(p2, nc, f"q2_{b}", DI)

            def ywg(y):
                for c in range(DC):
                    s_dx.inc(y.dma_start_transpose(
                        out=cWg[:, c * DI:(c + 1) * DI],
                        in_=wgN.ap()[:, c * 128:(c + 1) * 128]), 16)
                    s_dx.inc(y.dma_start_transpose(
                        out=cWu[:, c * DI:(c + 1) * DI],
                        in_=wuN.ap()[:, c * 128:(c + 1) * 128]), 16)
            emit("sync", ywg)
            wgu_ready = s_dx.v

            for t in range(NT):
                t0 = t * TT

                def yxq(y, b=b, t0=t0):
                    s_t.wait(y, s_t.v)
                    s_g.wait(y, s_g.v)
                    for c in range(DC):
                        s_dx.inc(y.dma_start_transpose(
                            out=cXqT[:, c * 512:(c + 1) * 512],
                            in_=xqS.ap()[b, t0:t0 + TT, c * 128:(c + 1) * 128]), 16)
                emit("sync", yxq)
                xqt_ready = s_dx.v

                for s_i in range(NSUB):
                    for io, (o0, o1) in enumerate(OSL):
                        bG = ps[(io % 2) * 2 + 0]
                        bU = ps[(io % 2) * 2 + 1]
                        ow = o1 - o0

                        def tGU(t_, s_i=s_i, o0=o0, o1=o1, ow=ow, bG=bG, bU=bU,
                                xqt_ready=xqt_ready, wgu_ready=wgu_ready):
                            s_dx.wait(t_, xqt_ready)
                            s_dx.wait(t_, wgu_ready)
                            s_v.wait(t_, s_v.v)
                            last = None
                            for icc in range(DC):
                                st_ap = cXqT[:, icc * 512 + s_i * 128:
                                             icc * 512 + (s_i + 1) * 128]
                                t_.matmul(bG[:, 0:ow], st_ap,
                                          cWg[:, icc * DI + o0:icc * DI + o1],
                                          start=(icc == 0), stop=(icc == DC - 1))
                                last = t_.matmul(
                                    bU[:, 0:ow], st_ap,
                                    cWu[:, icc * DI + o0:icc * DI + o1],
                                    start=(icc == 0), stop=(icc == DC - 1))
                            s_t.inc(last)
                        emit("tensor", tGU)
                        gu_done = s_t.v

                        def aSilu(a, bG=bG, ow=ow, gu_done=gu_done):
                            s_t.wait(a, gu_done)
                            s_v.wait(a, s_v.v)
                            s_a.inc(a.activation(cSl[:, 0:ow], bG[:, 0:ow],
                                                 AF.Silu))
                        emit("scalar", aSilu)
                        sl_done = s_a.v

                        def vH(v, bU=bU, o0=o0, o1=o1, ow=ow, sl_done=sl_done):
                            s_a.wait(v, sl_done)
                            v.tensor_tensor(cH[:, o0:o1], cSl[:, 0:ow],
                                            bU[:, 0:ow], op=ALU.mult)
                            s_v.inc(v.engine_nop())
                        emit("vector", vH)

                    prev_hq_spill = s_dsp.v

                    def vwh(v, prev_hq_spill=prev_hq_spill):
                        s_dsp.wait(v, prev_hq_spill)
                    emit("vector", vwh)
                    hq_sem, hq_ready = quant_emit(
                        tmp2, [(cH[:, a0:a1], cHq[:, a0:a1], a1 - a0)
                               for (a0, a1) in QT])

                    def aSpHq(a, b=b, t0=t0, s_i=s_i, hq_sem=hq_sem,
                              hq_ready=hq_ready):
                        hq_sem.wait(a, hq_ready)
                        row0 = t0 + s_i * 128
                        s_dsp.inc(a.dma_start(
                            out=hqS.ap()[b, row0:row0 + 128, :], in_=cHq[:]), 16)
                    emit("scalar", aSpHq)

        barrier()

        if "p2b" not in stages:
            continue
        with contextlib.ExitStack() as p3:
            cWd = p3.enter_context(nc.sbuf_tensor(
                f"p3wd_{b}", [128, IC * D], BF16, is_transpose_dest=True))
            cWo = p3.enter_context(nc.sbuf_tensor(f"p3wo_{b}", [128, DC * D], BF16))
            cHqT = p3.enter_context(nc.sbuf_tensor(
                f"p3hqt_{b}", [128, IC * 512], BF16, is_transpose_dest=True))
            cQ = p3.enter_context(nc.sbuf_tensor(f"p3q_{b}", [128, DC * 512], F32))
            cA = p3.enter_context(nc.sbuf_tensor(f"p3a_{b}", [128, DC * 512], BF16))
            cZa = p3.enter_context(nc.sbuf_tensor(f"p3za_{b}", [128, 512], F32))
            cRz = p3.enter_context(nc.sbuf_tensor(f"p3rz_{b}", [128, DC * 512], F32))
            cGi = p3.enter_context(nc.sbuf_tensor(f"p3gi_{b}", [128, 512], F32))
            cGo = p3.enter_context(nc.sbuf_tensor(f"p3go_{b}", [128, 512], F32))
            cOut = p3.enter_context(nc.sbuf_tensor(f"p3out_{b}", [128, NSUB * D],
                                                   F32))
            cXr = p3.enter_context(nc.sbuf_tensor(f"p3xr_{b}", [128, NSUB * D],
                                                  F32))

            def ywd(y):
                for c in range(IC):
                    s_dx.inc(y.dma_start_transpose(
                        out=cWd[:, c * D:(c + 1) * D],
                        in_=wdN.ap()[:, c * 128:(c + 1) * 128]), 16)
                for c in range(DC):
                    s_dx.inc(y.dma_start(
                        out=cWo[:, c * D:(c + 1) * D],
                        in_=woB.ap()[c * 128:(c + 1) * 128, :]), 16)
            emit("sync", ywd)
            wdo_ready = s_dx.v

            for t in range(NT):
                t0 = t * TT

                def yld(y, b=b, t0=t0):
                    s_v.wait(y, s_v.v)
                    s_t.wait(y, s_t.v)
                    for c in range(IC):
                        s_dx.inc(y.dma_start_transpose(
                            out=cHqT[:, c * 512:(c + 1) * 512],
                            in_=hqS.ap()[b, t0:t0 + TT, c * 128:(c + 1) * 128]), 16)
                    for c in range(DC):
                        s_dx.inc(y.dma_start(
                            out=cQ[:, c * 512:(c + 1) * 512],
                            in_=qS.ap()[b, c * 128:(c + 1) * 128, t0:t0 + TT]), 16)
                    s_dx.inc(y.dma_start(out=cXr[:], in_=xtile_in_ap(b, t0)), 16)
                emit("sync", yld)
                ld_ready = s_dx.v

                prev_out_spill = s_dsp.v

                def vA(v, ld_ready=ld_ready, prev_out_spill=prev_out_spill):
                    s_dx.wait(v, ld_ready)
                    s_dsp.wait(v, prev_out_spill)
                    for c in range(DC):
                        v.tensor_scalar(cZa[:], cQ[:, c * 512:(c + 1) * 512],
                                        cKs8[:, c:c + 1], 1e-6,
                                        op0=ALU.mult, op1=ALU.add)
                        v.reciprocal(cRz[:, c * 512:(c + 1) * 512], cZa[:])
                    v.drain()
                    for c in range(DC):
                        v.scalar_tensor_tensor(
                            cA[:, c * 512:(c + 1) * 512],
                            cQ[:, c * 512:(c + 1) * 512], cKv8[:, c:c + 1],
                            cRz[:, c * 512:(c + 1) * 512],
                            op0=ALU.mult, op1=ALU.mult)
                    s_v.inc(v.engine_nop())
                emit("vector", vA)
                a_ready = s_v.v

                for s_i in range(NSUB):
                    par = s_i % 2
                    bD = [ps[par * 4 + 0], ps[par * 4 + 1]]
                    bW = [ps[par * 4 + 2], ps[par * 4 + 3]]

                    def tDW(t_, s_i=s_i, bD=bD, bW=bW, ld_ready=ld_ready,
                            a_ready=a_ready, wdo_ready=wdo_ready):
                        s_dx.wait(t_, ld_ready)
                        s_dx.wait(t_, wdo_ready)
                        s_v.wait(t_, s_v.v)
                        for icc in range(IC):
                            st_ap = cHqT[:, icc * 512 + s_i * 128:
                                         icc * 512 + (s_i + 1) * 128]
                            t_.matmul(bD[0][:], st_ap,
                                      cWd[:, icc * D:icc * D + 512],
                                      start=(icc == 0), stop=(icc == IC - 1))
                            t_.matmul(bD[1][:], st_ap,
                                      cWd[:, icc * D + 512:(icc + 1) * D],
                                      start=(icc == 0), stop=(icc == IC - 1))
                        last = None
                        for c in range(DC):
                            st_ap = cA[:, c * 512 + s_i * 128:
                                       c * 512 + (s_i + 1) * 128]
                            t_.matmul(bW[0][:], st_ap, cWo[:, c * D:c * D + 512],
                                      start=(c == 0), stop=(c == DC - 1))
                            last = t_.matmul(bW[1][:], st_ap,
                                             cWo[:, c * D + 512:(c + 1) * D],
                                             start=(c == 0), stop=(c == DC - 1))
                        s_t.inc(last)
                    emit("tensor", tDW)
                    dw_done = s_t.v

                    for half in range(2):
                        def vGi(v, half=half, bW=bW, dw_done=dw_done):
                            s_t.wait(v, dw_done)
                            v.tensor_tensor(cGi[:], bW[half][:],
                                            cBoR[:, half * 512:(half + 1) * 512],
                                            op=ALU.add)
                            s_v.inc(v.engine_nop())
                        emit("vector", vGi)
                        gi_done = s_v.v

                        def aG(a, gi_done=gi_done):
                            s_v.wait(a, gi_done)
                            s_a.inc(a.activation(cGo[:], cGi[:],
                                                 AF.Gelu_apprx_tanh))
                        emit("scalar", aG)
                        g_done = s_a.v

                        def vOut(v, s_i=s_i, half=half, bD=bD, g_done=g_done):
                            s_a.wait(v, g_done)
                            off = s_i * D + half * 512
                            v.tensor_tensor(cGi[:], cXr[:, off:off + 512],
                                            cGo[:], op=ALU.add)
                            v.tensor_tensor(cOut[:, off:off + 512], cGi[:],
                                            bD[half][:], op=ALU.add)
                            s_v.inc(v.engine_nop())
                        emit("vector", vOut)

                out_ready = s_v.v

                def aOut(a, b=b, t0=t0, out_ready=out_ready):
                    s_v.wait(a, out_ready)
                    s_dsp.inc(a.dma_start(out=xtile_out_ap(b, t0), in_=cOut[:]), 16)
                emit("scalar", aOut)

        barrier()

    emit("gpsimd", lambda g: s_dsp.wait(g, s_dsp.v))
    emit("scalar", lambda a: s_dsp.wait(a, s_dsp.v))

    es.close()
    return nc


_NC_CACHE = {}


def kernel(L0_attn_x, L0_attn_Wq, L0_attn_bq, L0_attn_Wk, L0_attn_bk,
           L0_attn_Wv, L0_attn_bv, L0_attn_Wo, L0_attn_bo,
           L0_ffn_w_gate, L0_ffn_w_up, L0_ffn_w_down):
    x = np.ascontiguousarray(np.asarray(L0_attn_x, dtype=np.float32))
    assert x.shape == (32, 2048, 1024)
    weights = {
        "wq": np.ascontiguousarray(np.asarray(L0_attn_Wq, np.float32)),
        "wk": np.ascontiguousarray(np.asarray(L0_attn_Wk, np.float32)),
        "wv": np.ascontiguousarray(np.asarray(L0_attn_Wv, np.float32)),
        "wo": np.ascontiguousarray(np.asarray(L0_attn_Wo, np.float32)),
        "bq": np.ascontiguousarray(np.asarray(L0_attn_bq, np.float32)),
        "bk": np.ascontiguousarray(np.asarray(L0_attn_bk, np.float32)),
        "bv": np.ascontiguousarray(np.asarray(L0_attn_bv, np.float32)),
        "bo": np.ascontiguousarray(np.asarray(L0_attn_bo, np.float32)),
        "wg": np.ascontiguousarray(np.asarray(L0_ffn_w_gate, np.float32)),
        "wu": np.ascontiguousarray(np.asarray(L0_ffn_w_up, np.float32)),
        "wd": np.ascontiguousarray(np.asarray(L0_ffn_w_down, np.float32)),
    }
    if "nc" not in _NC_CACHE:
        _NC_CACHE["nc"] = build_program()
    nc = _NC_CACHE["nc"]
    in_maps = []
    for core in range(NCORES):
        m = dict(weights)
        m["x"] = x[core * NB:(core + 1) * NB]
        in_maps.append(m)
    res = run_bass_kernel_spmd(nc, in_maps, core_ids=list(range(NCORES)))
    out = np.concatenate([np.asarray(r["out"]) for r in res.results], axis=0)
    return out.astype(np.float32)


# version 11
# speedup vs baseline: 2.2922x; 1.2071x over previous
import contextlib
import numpy as np

import concourse.bass as bass
import concourse.mybir as mybir
from concourse.bass_utils import run_bass_kernel_spmd

F32 = mybir.dt.float32


def _preload_set(a, nc, name):
    from concourse.hw_specs import get_activation_tables
    names = list(get_activation_tables(nc.m.arch).keys())
    a.add_instruction(mybir.InstLoadActFuncSet(
        name=nc.get_next_instruction_name(),
        act_func_set_id=names.index(name), ins=[], outs=[]))

BF16 = mybir.dt.bfloat16
AF = mybir.ActivationFunctionType
ALU = mybir.AluOpType
AX = mybir.AxisListType

NCORES = 8
NB = 4
S = 2048
D = 1024
DI = 2816
TT = 512
NT = S // TT
NSUB = TT // 128
DC = D // 128
IC = DI // 128
C12 = 1.0 / 12.0
INV_F = 1.0 / float(D)
OSL = [(0, 512), (512, 1024), (1024, 1536), (1536, 2048), (2048, 2560), (2560, 2816)]
QT = [(0, 1408), (1408, 2816)]


class Ctr:
    def __init__(self, es, nc, name):
        self.h = es.enter_context(nc.semaphore(name))
        self.v = 0

    def inc(self, inst, k=1):
        inst.then_inc(self.h, k)
        self.v += k
        return self.v

    def wait(self, eng, v=None):
        eng.wait_ge(self.h, self.v if v is None else v)


class QTmp:

    def __init__(self, scope, nc, tag, W):
        self.W = W
        sb = lambda n, s, d: scope.enter_context(nc.sbuf_tensor(n, s, d))
        self.u = sb(f"{tag}_u", [128, W], F32)
        self.au = sb(f"{tag}_au", [128, W], F32)
        self.sg = sb(f"{tag}_sg", [128, W], F32)
        self.aA = sb(f"{tag}_aA", [128, W], F32)
        self.aB = sb(f"{tag}_aB", [128, W], F32)
        self.aB2 = sb(f"{tag}_aB2", [128, W], F32)
        G = W // 16
        self.gm = sb(f"{tag}_gm", [128, G], F32)
        self.sc = sb(f"{tag}_sc", [128, G], F32)
        self.r = sb(f"{tag}_r", [128, G], F32)
        self.r6 = sb(f"{tag}_r6", [128, G], F32)


def build_program(debug=False, stages=("cast", "prep", "p1", "p2a", "p2b"), nb=NB, p1_quant=True, p1_xpose=True, p1_qkv=True, p1_ev=4):
    nc = bass.Bass("TRN2", target_bir_lowering=False)

    x_h = nc.dram_tensor("x", [NB, S, D], F32, kind="ExternalInput")
    wq_h = nc.dram_tensor("wq", [D, D], F32, kind="ExternalInput")
    wk_h = nc.dram_tensor("wk", [D, D], F32, kind="ExternalInput")
    wv_h = nc.dram_tensor("wv", [D, D], F32, kind="ExternalInput")
    wo_h = nc.dram_tensor("wo", [D, D], F32, kind="ExternalInput")
    bq_h = nc.dram_tensor("bq", [D], F32, kind="ExternalInput")
    bk_h = nc.dram_tensor("bk", [D], F32, kind="ExternalInput")
    bv_h = nc.dram_tensor("bv", [D], F32, kind="ExternalInput")
    bo_h = nc.dram_tensor("bo", [D], F32, kind="ExternalInput")
    wg_h = nc.dram_tensor("wg", [DI, D], F32, kind="ExternalInput")
    wu_h = nc.dram_tensor("wu", [DI, D], F32, kind="ExternalInput")
    wd_h = nc.dram_tensor("wd", [D, DI], F32, kind="ExternalInput")
    out_h = nc.dram_tensor("out", [NB, S, D], F32, kind="ExternalOutput")

    kscr = "ExternalOutput" if debug else "Internal"
    wgN = nc.dram_tensor("wgN", [DI, D], BF16, kind=kscr)
    wuN = nc.dram_tensor("wuN", [DI, D], BF16, kind=kscr)
    wdN = nc.dram_tensor("wdN", [D, DI], BF16, kind=kscr)
    wqB = nc.dram_tensor("wqB", [D, D], BF16, kind="Internal")
    wkB = nc.dram_tensor("wkB", [D, D], BF16, kind="Internal")
    wvB = nc.dram_tensor("wvB", [D, D], BF16, kind="Internal")
    woB = nc.dram_tensor("woB", [D, D], BF16, kind="Internal")
    xnS = nc.dram_tensor("xnS", [NB, S, D], BF16, kind=kscr)
    xqS = nc.dram_tensor("xqS", [NB, S, D], BF16, kind=kscr)
    hqS = nc.dram_tensor("hqS", [NB, S, DI], BF16, kind=kscr)
    qS = nc.dram_tensor("qS", [NB, D, S], F32, kind=kscr)

    es = contextlib.ExitStack()

    def sb(name, shape, dtype=F32, **kw):
        return es.enter_context(nc.sbuf_tensor(name, shape, dtype, **kw))

    cXn = sb("xn", [128, D])
    cXnb = sb("xnb", [128, D], BF16)
    cXqb = sb("xqb", [128, D], BF16)
    cSq = [sb("sq0", [128, D]), sb("sq1", [128, D])]
    cSsum = sb("ssum", [128, NSUB])
    cSsq = sb("ssq", [128, NSUB])
    cTv = sb("tv", [128, NSUB])
    cVar = sb("var", [128, NSUB])
    cLv = sb("lv", [128, NSUB])
    cRstd = sb("rstd", [128, NSUB])
    cNmr = sb("nmr", [128, NSUB])
    cKvp = sb("kvp", [128, NT * DC])
    cKsp = sb("ksp", [128, NT * DC])
    cKv8 = sb("kv8", [128, DC])
    cKs8 = sb("ks8", [128, DC])
    cBq = sb("bq_c", [128, DC])
    cBk = sb("bk_c", [128, DC])
    cBv = sb("bv_c", [128, DC])
    cBoR = sb("boR", [128, D])
    cFeps = sb("cfeps", [128, 1])
    cMh = sb("cmh", [128, 1])
    cHlf = sb("chlf", [128, 1])
    ps = [es.enter_context(nc.psum_tensor(f"ps{i}", [128, 512], F32)) for i in range(8)]

    s_dx = Ctr(es, nc, "dx")
    s_dsp = Ctr(es, nc, "dsp")
    s_dg = Ctr(es, nc, "dg")
    s_v = Ctr(es, nc, "sv")
    s_a = Ctr(es, nc, "sa")
    s_t = Ctr(es, nc, "st")
    s_g = Ctr(es, nc, "sg")
    s_bar = [Ctr(es, nc, f"bar{i}") for i in range(5)]

    blk = es.enter_context(nc.Block())
    ENG = ["vector", "scalar", "tensor", "gpsimd", "sync"]

    def emit(eng, fn):
        getattr(blk, eng)(fn)

    def barrier():
        for i, e in enumerate(ENG):
            def bump(g, i=i, e=e):
                if e in ("vector", "gpsimd"):
                    ins = g.engine_nop()
                elif e in ("tensor", "scalar"):
                    ins = g.drain()
                else:
                    ins = g.nop()
                s_bar[i].inc(ins)
            emit(e, bump)
        targets = [c.v for c in s_bar]
        for i, e in enumerate(ENG):
            def w(g, i=i):
                for j in range(5):
                    if j != i:
                        s_bar[j].wait(g, targets[j])
            emit(e, w)

    def quant_emit(tmp, jobs):
        offs = []
        o = 0
        for (_, _, w) in jobs:
            assert w % 16 == 0
            offs.append(o)
            o += w
        assert o <= tmp.W

        g_prev = s_g.v

        def vfn(v):
            s_g.wait(v, g_prev)
            for (src, _, w), off in zip(jobs, offs):
                g0 = off // 16
                gc = w // 16
                v.tensor_reduce(tmp.gm[:, g0:g0 + gc],
                                src.rearrange("p (g e) -> p g e", e=16),
                                axis=AX.X, op=ALU.max, apply_absolute_value=True)
            v.drain()
            for (src, _, w), off in zip(jobs, offs):
                g0 = off // 16
                gc = w // 16
                gsl = tmp.gm[:, g0:g0 + gc]
                v.tensor_scalar(tmp.sc[:, g0:g0 + gc], gsl, 6e-12, C12,
                                op0=ALU.max, op1=ALU.mult)
                v.reciprocal(tmp.r[:, g0:g0 + gc], gsl)
            v.drain()
            for (_, _, w), off in zip(jobs, offs):
                g0 = off // 16
                gc = w // 16
                v.tensor_scalar(tmp.r6[:, g0:g0 + gc], tmp.r[:, g0:g0 + gc],
                                6.0, 1e12, op0=ALU.mult, op1=ALU.min)
            v.drain()
            for (src, _, w), off in zip(jobs, offs):
                g0 = off // 16
                gc = w // 16
                v.tensor_tensor(tmp.u[:, off:off + w].rearrange(
                                    "p (g e) -> p g e", e=16),
                                src.rearrange("p (g e) -> p g e", e=16),
                                tmp.r6[:, g0:g0 + gc].unsqueeze(2)
                                .broadcast_to([128, gc, 16]), op=ALU.mult)
            s_v.inc(v.engine_nop())
        emit("vector", vfn)
        u_done = s_v.v

        def afn(a):
            s_v.wait(a, u_done)
            for (_, _, w), off in zip(jobs, offs):
                usl = tmp.u[:, off:off + w]
                a.activation(tmp.au[:, off:off + w], usl, AF.Abs)
                last = a.activation(tmp.sg[:, off:off + w], usl, AF.Sign)
            s_a.inc(last)
        emit("scalar", afn)
        au_done = s_a.v

        def vfn2(v):
            s_a.wait(v, au_done)
            for (_, _, w), off in zip(jobs, offs):
                au = tmp.au[:, off:off + w]
                aa = tmp.aA[:, off:off + w]
                ab = tmp.aB[:, off:off + w]
                v.tensor_single_scalar(aa, au, 0.25, op=ALU.is_gt)
                for th in (0.75, 1.25, 1.75):
                    v.scalar_tensor_tensor(aa, au, th, aa, op0=ALU.is_gt, op1=ALU.add)
                v.tensor_single_scalar(ab, au, 2.5, op=ALU.is_gt)
                for th in (3.5, 5.0, 5.0):
                    v.scalar_tensor_tensor(ab, au, th, ab, op0=ALU.is_gt, op1=ALU.add)
            for (_, _, w), off in zip(jobs, offs):
                aa = tmp.aA[:, off:off + w]
                ab = tmp.aB[:, off:off + w]
                v.scalar_tensor_tensor(aa, ab, 2.0, aa, op0=ALU.mult, op1=ALU.add)
            s_v.inc(v.engine_nop())
        emit("vector", vfn2)

        def vfn3(v):
            for (_, dst, w), off in zip(jobs, offs):
                g0 = off // 16
                gc = w // 16
                aa = tmp.aA[:, off:off + w]
                usl = tmp.u[:, off:off + w]
                v.tensor_tensor(usl, aa, tmp.sg[:, off:off + w], op=ALU.mult)
                v.tensor_tensor(dst.rearrange("p (g e) -> p g e", e=16),
                                usl.rearrange("p (g e) -> p g e", e=16),
                                tmp.sc[:, g0:g0 + gc].unsqueeze(2)
                                .broadcast_to([128, gc, 16]), op=ALU.mult)
            s_v.inc(v.engine_nop())
        emit("vector", vfn3)
        return (s_v, s_v.v)

    def xtile_in_ap(b, t0):
        return x_h.ap()[b, t0:t0 + TT, :].rearrange("(s p) d -> p s d", p=128)

    def xtile_out_ap(b, t0):
        return out_h.ap()[b, t0:t0 + TT, :].rearrange("(s p) d -> p s d", p=128)

    def init_consts(v):
        v.memset(cFeps[:], float(D) * float(np.float32(1e-5)))
        v.memset(cMh[:], -0.5)
        v.memset(cHlf[:], float(np.float32(0.5 * np.log(float(D)))))
        v.memset(cKvp[:], 0.0)
        v.memset(cKsp[:], 0.0)
        s_v.inc(v.engine_nop())
    emit("vector", init_consts)

    def load_biases(y):
        for src, dst in ((bq_h, cBq), (bk_h, cBk), (bv_h, cBv)):
            s_dx.inc(y.dma_start(out=dst[:], in_=src.ap().rearrange(
                "(c p) -> p c", p=128), allow_slow_non_contiguous=True), 16)
        s_dx.inc(y.dma_start(
            out=cBoR[:], in_=bo_h.ap().unsqueeze(0).partition_broadcast(128)), 16)
    emit("sync", load_biases)
    biases_ready = s_dx.v

    with contextlib.ExitStack() as pp:
        stg = [pp.enter_context(nc.sbuf_tensor(f"wstg{i}", [128, D], BF16))
               for i in range(2)]
        prev_store = 0
        for wi, (src, dst) in enumerate((((wq_h, wqB), (wk_h, wkB),
                                         (wv_h, wvB), (wo_h, woB))
                                         if "cast" in stages else ())):
            for c in range(DC):
                buf = stg[(wi * DC + c) % 2]

                def gload(g, src=src, c=c, buf=buf, prev_store=prev_store):
                    s_dsp.wait(g, prev_store)
                    s_dg.inc(g.dma_start(out=buf[:],
                                         in_=src.ap()[c * 128:(c + 1) * 128, :]), 16)
                emit("gpsimd", gload)
                ld_done = s_dg.v

                def sstore(a, dst=dst, c=c, buf=buf, ld_done=ld_done):
                    s_dg.wait(a, ld_done)
                    s_dsp.inc(a.dma_start(out=dst.ap()[c * 128:(c + 1) * 128, :],
                                          in_=buf[:]), 16)
                emit("scalar", sstore)
                if wi * DC + c >= 1:
                    prev_store = s_dsp.v - 16

        tmpP = QTmp(pp, nc, "qp", DI)
        wfin = pp.enter_context(nc.sbuf_tensor("wfin", [128, DI], F32))
        wfout = pp.enter_context(nc.sbuf_tensor("wfout", [128, DI], BF16))
        jobs = [(wg_h, wgN, r, D) for r in range(IC)]
        jobs += [(wu_h, wuN, r, D) for r in range(IC)]
        jobs += [(wd_h, wdN, r, DI) for r in range(DC)]
        if "prep" not in stages:
            jobs = []
        for src, dst, r, K in jobs:
            def yload(y, src=src, r=r, K=K):
                s_v.wait(y, s_v.v)
                s_g.wait(y, s_g.v)
                s_dsp.wait(y, s_dsp.v)
                s_dx.inc(y.dma_start(out=wfin[:, 0:K],
                                     in_=src.ap()[r * 128:(r + 1) * 128, :]), 16)
            emit("sync", yload)
            ld_done = s_dx.v

            emit("vector", lambda v, ld_done=ld_done: s_dx.wait(v, ld_done))
            if K == D:
                rsem, ready = quant_emit(tmpP, [(wfin[:, 0:D], wfout[:, 0:D], D)])
            else:
                rsem, ready = quant_emit(
                    tmpP, [(wfin[:, a0:a1], wfout[:, a0:a1], a1 - a0)
                           for (a0, a1) in QT])

            def sstore(a, dst=dst, r=r, K=K, rsem=rsem, ready=ready):
                rsem.wait(a, ready)
                s_dsp.inc(a.dma_start(out=dst.ap()[r * 128:(r + 1) * 128, :],
                                      in_=wfout[:, 0:K]), 16)
            emit("scalar", sstore)
        barrier()

    for b in range(nb if ("p1" in stages) else 0):
        if b > 0:
            def vreset(v):
                v.memset(cKvp[:], 0.0)
                v.memset(cKsp[:], 0.0)
                s_v.inc(v.engine_nop())
            emit("vector", vreset)

        with contextlib.ExitStack() as p1:
            cXt = [p1.enter_context(nc.sbuf_tensor(f"p1xt{i}_{b}", [128, NSUB * D],
                                                   F32)) for i in range(2)]
            cW = {nm: p1.enter_context(nc.sbuf_tensor(f"p1w{nm}_{b}",
                                                      [128, DC * D], BF16))
                  for nm in ("q", "k", "v")}
            cXnT = p1.enter_context(nc.sbuf_tensor(
                f"p1xnt_{b}", [128, DC * 512], BF16, is_transpose_dest=True))
            cEv = [p1.enter_context(nc.sbuf_tensor(f"p1ev{i}_{b}", [128, 512], F32))
                   for i in range(7)]
            cScr = p1.enter_context(nc.sbuf_tensor(f"p1scr_{b}", [128, 512], F32))
            tmp1 = QTmp(p1, nc, f"q1_{b}", D)

            def ywl(y):
                for nm, srcB in (("q", wqB), ("k", wkB), ("v", wvB)):
                    for c in range(DC):
                        s_dx.inc(y.dma_start(
                            out=cW[nm][:, c * D:(c + 1) * D],
                            in_=srcB.ap()[c * 128:(c + 1) * 128, :]), 16)
            emit("sync", ywl)
            w_ready = s_dx.v
            if b == 0:
                emit("vector", lambda v: s_dx.wait(v, biases_ready))

            p1_bankfree = {0: 0, 1: 0}
            p1_xfree_a = {}
            p1_xfree_v = {}
            p1_qkv_last = {}
            for t in range(NT):
                t0 = t * TT
                xbuf = cXt[t % 2]

                def yx(y, b=b, t0=t0, xbuf=xbuf,
                       wa=p1_xfree_a.get(t - 2, 0), wv=p1_xfree_v.get(t - 2, 0)):
                    s_a.wait(y, wa)
                    s_v.wait(y, wv)
                    s_dx.inc(y.dma_start(out=xbuf[:], in_=xtile_in_ap(b, t0)), 16)
                emit("sync", yx)
                x_ready = s_dx.v

                prev_v = s_v.v

                def aS(a, xbuf=xbuf, x_ready=x_ready, prev_v=prev_v):
                    s_dx.wait(a, x_ready)
                    for s_i in range(NSUB):
                        if s_i >= 2:
                            s_v.wait(a, prev_v + s_i - 1)
                        s_a.inc(a.activation(cSq[s_i % 2][:],
                                xbuf[:, s_i * D:(s_i + 1) * D], AF.Square))
                emit("scalar", aS)
                sq_base = s_a.v - NSUB

                def vStats(v, xbuf=xbuf, x_ready=x_ready, sq_base=sq_base):
                    s_dx.wait(v, x_ready)
                    for s_i in range(NSUB):
                        v.tensor_reduce(cSsum[:, s_i:s_i + 1],
                                        xbuf[:, s_i * D:(s_i + 1) * D],
                                        axis=AX.X, op=ALU.add)
                        s_a.wait(v, sq_base + s_i + 1)
                        v.tensor_reduce(cSsq[:, s_i:s_i + 1], cSq[s_i % 2][:],
                                        axis=AX.X, op=ALU.add)
                        s_v.inc(v.engine_nop())
                    v.drain()
                    v.tensor_tensor(cTv[:], cSsum[:], cSsum[:], op=ALU.mult)
                    v.drain()
                    v.scalar_tensor_tensor(cVar[:], cTv[:], -INV_F, cSsq[:],
                                           op0=ALU.mult, op1=ALU.add)
                    v.drain()
                    s_v.inc(v.engine_nop())
                emit("vector", vStats)
                var_done = s_v.v
                p1_xfree_v[t] = var_done

                def aR(a, var_done=var_done):
                    s_v.wait(a, var_done)
                    a.activation(cLv[:], cVar[:], AF.Ln, bias=cFeps[:, 0:1])
                    last = a.activation(cRstd[:], cLv[:], AF.Exp,
                                        scale=cMh[:, 0:1], bias=cHlf[:, 0:1])
                    a.drain()
                    s_a.inc(last)
                emit("scalar", aR)
                rstd_done = s_a.v

                def vN(v, rstd_done=rstd_done):
                    s_a.wait(v, rstd_done)
                    v.scalar_tensor_tensor(cNmr[:], cSsum[:], -INV_F, cRstd[:],
                                           op0=ALU.mult, op1=ALU.mult)
                    v.drain()
                    s_v.inc(v.engine_nop())
                emit("vector", vN)
                nmr_done = s_v.v

                for s_i in range(NSUB if p1_quant else 0):
                    prev_xnb_spill = s_dx.v
                    prev_xq_spill = s_dsp.v

                    def aXn(a, xbuf=xbuf, s_i=s_i, nmr_done=nmr_done):
                        s_v.wait(a, s_v.v)
                        s_dx.wait(a, s_dx.v)
                        a.activation(cXn[:], xbuf[:, s_i * D:(s_i + 1) * D],
                                     AF.Identity, bias=cNmr[:, s_i:s_i + 1],
                                     scale=cRstd[:, s_i:s_i + 1])
                        s_a.inc(a.activation(
                            cXnb[:], xbuf[:, s_i * D:(s_i + 1) * D],
                            AF.Identity, bias=cNmr[:, s_i:s_i + 1],
                            scale=cRstd[:, s_i:s_i + 1]))
                    emit("scalar", aXn)
                    xn_done = s_a.v

                    def ySpXnb(y, b=b, t0=t0, s_i=s_i, xn_done=xn_done):
                        s_a.wait(y, xn_done)
                        row0 = t0 + s_i * 128
                        s_dx.inc(y.dma_start(
                            out=xnS.ap()[b, row0:row0 + 128, :], in_=cXnb[:]), 16)
                    emit("sync", ySpXnb)

                    def vwq(v, xn_done=xn_done, prev_xq_spill=prev_xq_spill):
                        s_a.wait(v, xn_done)
                        s_dsp.wait(v, prev_xq_spill)
                    emit("vector", vwq)
                    q_sem, q_ready = quant_emit(tmp1, [(cXn[:], cXqb[:], D)])

                    def aSpXq(a, b=b, t0=t0, s_i=s_i, q_sem=q_sem, q_ready=q_ready):
                        q_sem.wait(a, q_ready)
                        row0 = t0 + s_i * 128
                        s_dsp.inc(a.dma_start(
                            out=xqS.ap()[b, row0:row0 + 128, :], in_=cXqb[:]), 16)
                    emit("scalar", aSpXq)

                p1_xfree_a[t] = s_a.v
                if not p1_xpose:
                    continue
                def yT(y, b=b, t0=t0, wt=p1_qkv_last.get(t - 1, 0)):
                    s_dx.wait(y, s_dx.v)
                    s_t.wait(y, wt)
                    for c in range(DC):
                        s_dx.inc(y.dma_start_transpose(
                            out=cXnT[:, c * 512:(c + 1) * 512],
                            in_=xnS.ap()[b, t0:t0 + TT, c * 128:(c + 1) * 128]), 16)
                emit("sync", yT)
                xnt_ready = s_dx.v

                for oc in range(DC if p1_qkv else 0):
                    bq3 = [ps[(oc % 2) * 3 + i] for i in range(3)]

                    def tQKV(t_, oc=oc, bq3=bq3, xnt_ready=xnt_ready,
                             w_ready=w_ready, bf=p1_bankfree[oc % 2]):
                        s_dx.wait(t_, xnt_ready)
                        s_dx.wait(t_, w_ready)
                        s_v.wait(t_, bf)
                        last = None
                        for pi, nm in enumerate(("q", "k", "v")):
                            for icc in range(DC):
                                last = t_.matmul(
                                    bq3[pi][:],
                                    cW[nm][:, icc * D + oc * 128:
                                           icc * D + (oc + 1) * 128],
                                    cXnT[:, icc * 512:(icc + 1) * 512],
                                    start=(icc == 0), stop=(icc == DC - 1))
                        s_t.inc(last)
                    emit("tensor", tQKV)
                    qkv_done = s_t.v
                    p1_qkv_last[t] = qkv_done

                    uq, uk, vp, tm, ee, Kc, Qc = cEv
                    if p1_ev == 0:
                        continue

                    def vEvac(v, oc=oc, bq3=bq3, qkv_done=qkv_done):
                        s_t.wait(v, qkv_done)
                        v.tensor_scalar(uq[:], bq3[0][:], cBq[:, oc:oc + 1], None,
                                        op0=ALU.add)
                        v.tensor_scalar(uk[:], bq3[1][:], cBk[:, oc:oc + 1], None,
                                        op0=ALU.add)
                        v.tensor_scalar(vp[:], bq3[2][:], cBv[:, oc:oc + 1], None,
                                        op0=ALU.add)
                        s_v.inc(v.engine_nop())
                        v.tensor_single_scalar(tm[:], uq[:], 0.0, op=ALU.min)
                        s_v.inc(v.engine_nop())
                    emit("vector", vEvac)
                    u_done = s_v.v
                    p1_bankfree[oc % 2] = u_done - 1
                    if p1_ev == 1:
                        continue

                    def aExpQ(a, u_done=u_done):
                        s_v.wait(a, u_done)
                        s_a.inc(a.activation(ee[:], tm[:], AF.Exp))
                    emit("scalar", aExpQ)
                    eq_done = s_a.v

                    def vQ(v, eq_done=eq_done, dsp_now=s_dsp.v):
                        s_a.wait(v, eq_done)
                        s_dsp.wait(v, dsp_now)
                        v.scalar_tensor_tensor(Qc[:], uq[:], 0.0, ee[:],
                                               op0=ALU.max, op1=ALU.add)
                        v.tensor_single_scalar(tm[:], uk[:], 0.0, op=ALU.min)
                        s_v.inc(v.engine_nop())
                    emit("vector", vQ)
                    q_done = s_v.v
                    if p1_ev == 2:
                        continue

                    def aSpQ(a, b=b, oc=oc, t0=t0, q_done=q_done):
                        s_v.wait(a, q_done)
                        s_dsp.inc(a.dma_start(
                            out=qS.ap()[b, oc * 128:(oc + 1) * 128, t0:t0 + TT],
                            in_=Qc[:]), 16)
                        s_a.inc(a.activation(ee[:], tm[:], AF.Exp))
                    emit("scalar", aSpQ)
                    ek_done = s_a.v
                    if p1_ev == 3:
                        continue

                    def vK(v, oc=oc, t=t, ek_done=ek_done):
                        s_a.wait(v, ek_done)
                        col = t * DC + oc
                        v.scalar_tensor_tensor(Kc[:], uk[:], 0.0, ee[:],
                                               op0=ALU.max, op1=ALU.add,
                                               accum_out=cKsp[:, col:col + 1])
                        v.scalar_tensor_tensor(cScr[:], Kc[:], 1.0, vp[:],
                                               op0=ALU.mult, op1=ALU.mult,
                                               accum_out=cKvp[:, col:col + 1])
                        s_v.inc(v.engine_nop())
                    emit("vector", vK)

            def vKvFin(v):
                v.drain()
                v.tensor_reduce(cKv8[:],
                                cKvp[:].rearrange("p (t c) -> p c t", c=DC),
                                axis=AX.X, op=ALU.add)
                v.tensor_reduce(cKs8[:],
                                cKsp[:].rearrange("p (t c) -> p c t", c=DC),
                                axis=AX.X, op=ALU.add)
                v.drain()
                s_v.inc(v.engine_nop())
            emit("vector", vKvFin)

        barrier()

        if "p2a" not in stages:
            continue
        with contextlib.ExitStack() as p2:
            cWg = p2.enter_context(nc.sbuf_tensor(
                f"p2wg_{b}", [128, DC * DI], BF16, is_transpose_dest=True))
            cWu = p2.enter_context(nc.sbuf_tensor(
                f"p2wu_{b}", [128, DC * DI], BF16, is_transpose_dest=True))
            cXqT = p2.enter_context(nc.sbuf_tensor(
                f"p2xqt_{b}", [128, DC * 512], BF16, is_transpose_dest=True))
            cH = p2.enter_context(nc.sbuf_tensor(f"p2h_{b}", [128, DI], F32))
            cSl = p2.enter_context(nc.sbuf_tensor(f"p2sl_{b}", [128, 512], F32))
            cHq = p2.enter_context(nc.sbuf_tensor(f"p2hq_{b}", [128, DI], BF16))
            tmp2 = QTmp(p2, nc, f"q2_{b}", DI)

            def ywg(y):
                for c in range(DC):
                    s_dx.inc(y.dma_start_transpose(
                        out=cWg[:, c * DI:(c + 1) * DI],
                        in_=wgN.ap()[:, c * 128:(c + 1) * 128]), 16)
                    s_dx.inc(y.dma_start_transpose(
                        out=cWu[:, c * DI:(c + 1) * DI],
                        in_=wuN.ap()[:, c * 128:(c + 1) * 128]), 16)
            emit("sync", ywg)
            wgu_ready = s_dx.v

            p2_bankfree = {0: 0, 1: 0}
            p2_slfree = [0]
            for t in range(NT):
                t0 = t * TT

                def yxq(y, b=b, t0=t0):
                    s_t.wait(y, s_t.v)
                    s_g.wait(y, s_g.v)
                    for c in range(DC):
                        s_dx.inc(y.dma_start_transpose(
                            out=cXqT[:, c * 512:(c + 1) * 512],
                            in_=xqS.ap()[b, t0:t0 + TT, c * 128:(c + 1) * 128]), 16)
                emit("sync", yxq)
                xqt_ready = s_dx.v

                for s_i in range(NSUB):
                    for io, (o0, o1) in enumerate(OSL):
                        bG = ps[(io % 2) * 2 + 0]
                        bU = ps[(io % 2) * 2 + 1]
                        ow = o1 - o0

                        def tGU(t_, s_i=s_i, o0=o0, o1=o1, ow=ow, bG=bG, bU=bU,
                                xqt_ready=xqt_ready, wgu_ready=wgu_ready,
                                bf=p2_bankfree[io % 2]):
                            s_dx.wait(t_, xqt_ready)
                            s_dx.wait(t_, wgu_ready)
                            s_v.wait(t_, bf)
                            last = None
                            for icc in range(DC):
                                st_ap = cXqT[:, icc * 512 + s_i * 128:
                                             icc * 512 + (s_i + 1) * 128]
                                t_.matmul(bG[:, 0:ow], st_ap,
                                          cWg[:, icc * DI + o0:icc * DI + o1],
                                          start=(icc == 0), stop=(icc == DC - 1))
                                last = t_.matmul(
                                    bU[:, 0:ow], st_ap,
                                    cWu[:, icc * DI + o0:icc * DI + o1],
                                    start=(icc == 0), stop=(icc == DC - 1))
                            s_t.inc(last)
                        emit("tensor", tGU)
                        gu_done = s_t.v

                        def aSilu(a, bG=bG, ow=ow, gu_done=gu_done,
                                  slf=p2_slfree[0]):
                            s_t.wait(a, gu_done)
                            s_v.wait(a, slf)
                            s_a.inc(a.activation(cSl[:, 0:ow], bG[:, 0:ow],
                                                 AF.Silu))
                        emit("scalar", aSilu)
                        sl_done = s_a.v

                        def vH(v, bU=bU, o0=o0, o1=o1, ow=ow, sl_done=sl_done):
                            s_a.wait(v, sl_done)
                            v.tensor_tensor(cH[:, o0:o1], cSl[:, 0:ow],
                                            bU[:, 0:ow], op=ALU.mult)
                            s_v.inc(v.engine_nop())
                        emit("vector", vH)
                        p2_bankfree[io % 2] = s_v.v
                        p2_slfree[0] = s_v.v

                    prev_hq_spill = s_dsp.v

                    def vwh(v, prev_hq_spill=prev_hq_spill):
                        s_dsp.wait(v, prev_hq_spill)
                    emit("vector", vwh)
                    hq_sem, hq_ready = quant_emit(
                        tmp2, [(cH[:, a0:a1], cHq[:, a0:a1], a1 - a0)
                               for (a0, a1) in QT])

                    def aSpHq(a, b=b, t0=t0, s_i=s_i, hq_sem=hq_sem,
                              hq_ready=hq_ready):
                        hq_sem.wait(a, hq_ready)
                        row0 = t0 + s_i * 128
                        s_dsp.inc(a.dma_start(
                            out=hqS.ap()[b, row0:row0 + 128, :], in_=cHq[:]), 16)
                    emit("scalar", aSpHq)

        barrier()

        if "p2b" not in stages:
            continue
        with contextlib.ExitStack() as p3:
            cWd = p3.enter_context(nc.sbuf_tensor(
                f"p3wd_{b}", [128, IC * D], BF16, is_transpose_dest=True))
            cWo = p3.enter_context(nc.sbuf_tensor(f"p3wo_{b}", [128, DC * D], BF16))
            cHqT = p3.enter_context(nc.sbuf_tensor(
                f"p3hqt_{b}", [128, IC * 512], BF16, is_transpose_dest=True))
            cQ = p3.enter_context(nc.sbuf_tensor(f"p3q_{b}", [128, DC * 512], F32))
            cA = p3.enter_context(nc.sbuf_tensor(f"p3a_{b}", [128, DC * 512], BF16))
            cZa = p3.enter_context(nc.sbuf_tensor(f"p3za_{b}", [128, 512], F32))
            cRz = p3.enter_context(nc.sbuf_tensor(f"p3rz_{b}", [128, DC * 512], F32))
            cGi = p3.enter_context(nc.sbuf_tensor(f"p3gi_{b}", [128, 512], F32))
            cGo = p3.enter_context(nc.sbuf_tensor(f"p3go_{b}", [128, 512], F32))
            cOut = p3.enter_context(nc.sbuf_tensor(f"p3out_{b}", [128, NSUB * D],
                                                   F32))
            cXr = p3.enter_context(nc.sbuf_tensor(f"p3xr_{b}", [128, NSUB * D],
                                                  F32))

            def ywd(y):
                for c in range(IC):
                    s_dx.inc(y.dma_start_transpose(
                        out=cWd[:, c * D:(c + 1) * D],
                        in_=wdN.ap()[:, c * 128:(c + 1) * 128]), 16)
                for c in range(DC):
                    s_dx.inc(y.dma_start(
                        out=cWo[:, c * D:(c + 1) * D],
                        in_=woB.ap()[c * 128:(c + 1) * 128, :]), 16)
            emit("sync", ywd)
            wdo_ready = s_dx.v

            p3_bankfree = {0: 0, 1: 0}
            for t in range(NT):
                t0 = t * TT

                def yld(y, b=b, t0=t0):
                    s_v.wait(y, s_v.v)
                    s_t.wait(y, s_t.v)
                    for c in range(IC):
                        s_dx.inc(y.dma_start_transpose(
                            out=cHqT[:, c * 512:(c + 1) * 512],
                            in_=hqS.ap()[b, t0:t0 + TT, c * 128:(c + 1) * 128]), 16)
                    for c in range(DC):
                        s_dx.inc(y.dma_start(
                            out=cQ[:, c * 512:(c + 1) * 512],
                            in_=qS.ap()[b, c * 128:(c + 1) * 128, t0:t0 + TT]), 16)
                    s_dx.inc(y.dma_start(out=cXr[:], in_=xtile_in_ap(b, t0)), 16)
                emit("sync", yld)
                ld_ready = s_dx.v

                prev_out_spill = s_dsp.v

                def vA(v, ld_ready=ld_ready, prev_out_spill=prev_out_spill):
                    s_dx.wait(v, ld_ready)
                    s_dsp.wait(v, prev_out_spill)
                    for c in range(DC):
                        v.tensor_scalar(cZa[:], cQ[:, c * 512:(c + 1) * 512],
                                        cKs8[:, c:c + 1], 1e-6,
                                        op0=ALU.mult, op1=ALU.add)
                        v.reciprocal(cRz[:, c * 512:(c + 1) * 512], cZa[:])
                    v.drain()
                    for c in range(DC):
                        v.scalar_tensor_tensor(
                            cA[:, c * 512:(c + 1) * 512],
                            cQ[:, c * 512:(c + 1) * 512], cKv8[:, c:c + 1],
                            cRz[:, c * 512:(c + 1) * 512],
                            op0=ALU.mult, op1=ALU.mult)
                    s_v.inc(v.engine_nop())
                emit("vector", vA)
                a_ready = s_v.v

                for s_i in range(NSUB):
                    par = s_i % 2
                    bD = [ps[par * 4 + 0], ps[par * 4 + 1]]
                    bW = [ps[par * 4 + 2], ps[par * 4 + 3]]

                    def tDW(t_, s_i=s_i, bD=bD, bW=bW, ld_ready=ld_ready,
                            a_ready=a_ready, wdo_ready=wdo_ready):
                        s_dx.wait(t_, ld_ready)
                        s_dx.wait(t_, wdo_ready)
                        s_v.wait(t_, s_v.v)
                        for icc in range(IC):
                            st_ap = cHqT[:, icc * 512 + s_i * 128:
                                         icc * 512 + (s_i + 1) * 128]
                            t_.matmul(bD[0][:], st_ap,
                                      cWd[:, icc * D:icc * D + 512],
                                      start=(icc == 0), stop=(icc == IC - 1))
                            t_.matmul(bD[1][:], st_ap,
                                      cWd[:, icc * D + 512:(icc + 1) * D],
                                      start=(icc == 0), stop=(icc == IC - 1))
                        last = None
                        for c in range(DC):
                            st_ap = cA[:, c * 512 + s_i * 128:
                                       c * 512 + (s_i + 1) * 128]
                            t_.matmul(bW[0][:], st_ap, cWo[:, c * D:c * D + 512],
                                      start=(c == 0), stop=(c == DC - 1))
                            last = t_.matmul(bW[1][:], st_ap,
                                             cWo[:, c * D + 512:(c + 1) * D],
                                             start=(c == 0), stop=(c == DC - 1))
                        s_t.inc(last)
                    emit("tensor", tDW)
                    dw_done = s_t.v

                    for half in range(2):
                        def vGi(v, half=half, bW=bW, dw_done=dw_done):
                            s_t.wait(v, dw_done)
                            v.tensor_tensor(cGi[:], bW[half][:],
                                            cBoR[:, half * 512:(half + 1) * 512],
                                            op=ALU.add)
                            s_v.inc(v.engine_nop())
                        emit("vector", vGi)
                        gi_done = s_v.v

                        def aG(a, gi_done=gi_done):
                            s_v.wait(a, gi_done)
                            s_a.inc(a.activation(cGo[:], cGi[:],
                                                 AF.Gelu_apprx_tanh))
                        emit("scalar", aG)
                        g_done = s_a.v

                        def vOut(v, s_i=s_i, half=half, bD=bD, g_done=g_done):
                            s_a.wait(v, g_done)
                            off = s_i * D + half * 512
                            v.tensor_tensor(cGi[:], cXr[:, off:off + 512],
                                            cGo[:], op=ALU.add)
                            v.tensor_tensor(cOut[:, off:off + 512], cGi[:],
                                            bD[half][:], op=ALU.add)
                            s_v.inc(v.engine_nop())
                        emit("vector", vOut)

                out_ready = s_v.v

                def aOut(a, b=b, t0=t0, out_ready=out_ready):
                    s_v.wait(a, out_ready)
                    s_dsp.inc(a.dma_start(out=xtile_out_ap(b, t0), in_=cOut[:]), 16)
                emit("scalar", aOut)

        barrier()

    emit("gpsimd", lambda g: s_dsp.wait(g, s_dsp.v))
    emit("scalar", lambda a: s_dsp.wait(a, s_dsp.v))

    es.close()
    return nc


_NC_CACHE = {}


def kernel(L0_attn_x, L0_attn_Wq, L0_attn_bq, L0_attn_Wk, L0_attn_bk,
           L0_attn_Wv, L0_attn_bv, L0_attn_Wo, L0_attn_bo,
           L0_ffn_w_gate, L0_ffn_w_up, L0_ffn_w_down):
    x = np.ascontiguousarray(np.asarray(L0_attn_x, dtype=np.float32))
    assert x.shape == (32, 2048, 1024)
    weights = {
        "wq": np.ascontiguousarray(np.asarray(L0_attn_Wq, np.float32)),
        "wk": np.ascontiguousarray(np.asarray(L0_attn_Wk, np.float32)),
        "wv": np.ascontiguousarray(np.asarray(L0_attn_Wv, np.float32)),
        "wo": np.ascontiguousarray(np.asarray(L0_attn_Wo, np.float32)),
        "bq": np.ascontiguousarray(np.asarray(L0_attn_bq, np.float32)),
        "bk": np.ascontiguousarray(np.asarray(L0_attn_bk, np.float32)),
        "bv": np.ascontiguousarray(np.asarray(L0_attn_bv, np.float32)),
        "bo": np.ascontiguousarray(np.asarray(L0_attn_bo, np.float32)),
        "wg": np.ascontiguousarray(np.asarray(L0_ffn_w_gate, np.float32)),
        "wu": np.ascontiguousarray(np.asarray(L0_ffn_w_up, np.float32)),
        "wd": np.ascontiguousarray(np.asarray(L0_ffn_w_down, np.float32)),
    }
    if "nc" not in _NC_CACHE:
        _NC_CACHE["nc"] = build_program()
    nc = _NC_CACHE["nc"]
    in_maps = []
    for core in range(NCORES):
        m = dict(weights)
        m["x"] = x[core * NB:(core + 1) * NB]
        in_maps.append(m)
    res = run_bass_kernel_spmd(nc, in_maps, core_ids=list(range(NCORES)))
    out = np.concatenate([np.asarray(r["out"]) for r in res.results], axis=0)
    return out.astype(np.float32)


# version 14
# speedup vs baseline: 2.3613x; 1.0301x over previous
import contextlib
import numpy as np

import concourse.bass as bass
import concourse.mybir as mybir
from concourse.bass_utils import run_bass_kernel_spmd

F32 = mybir.dt.float32


def _preload_set(a, nc, name):
    from concourse.hw_specs import get_activation_tables
    names = list(get_activation_tables(nc.m.arch).keys())
    a.add_instruction(mybir.InstLoadActFuncSet(
        name=nc.get_next_instruction_name(),
        act_func_set_id=names.index(name), ins=[], outs=[]))

BF16 = mybir.dt.bfloat16
AF = mybir.ActivationFunctionType
ALU = mybir.AluOpType
AX = mybir.AxisListType

NCORES = 8
NB = 4
S = 2048
D = 1024
DI = 2816
TT = 512
NT = S // TT
NSUB = TT // 128
DC = D // 128
IC = DI // 128
C12 = 1.0 / 12.0
INV_F = 1.0 / float(D)
OSL = [(0, 512), (512, 1024), (1024, 1536), (1536, 2048), (2048, 2560), (2560, 2816)]
QT = [(0, 1408), (1408, 2816)]


class Ctr:
    def __init__(self, es, nc, name):
        self.h = es.enter_context(nc.semaphore(name))
        self.v = 0

    def inc(self, inst, k=1):
        inst.then_inc(self.h, k)
        self.v += k
        return self.v

    def wait(self, eng, v=None):
        eng.wait_ge(self.h, self.v if v is None else v)


class QTmp:

    def __init__(self, scope, nc, tag, W):
        self.W = W
        sb = lambda n, s, d: scope.enter_context(nc.sbuf_tensor(n, s, d))
        self.u = sb(f"{tag}_u", [128, W], F32)
        self.au = sb(f"{tag}_au", [128, W], F32)
        self.sg = sb(f"{tag}_sg", [128, W], F32)
        self.aA = sb(f"{tag}_aA", [128, W], F32)
        self.aB = sb(f"{tag}_aB", [128, W], F32)
        self.aB2 = sb(f"{tag}_aB2", [128, W], F32)
        G = W // 16
        self.gm = sb(f"{tag}_gm", [128, G], F32)
        self.sc = sb(f"{tag}_sc", [128, G], F32)
        self.r = sb(f"{tag}_r", [128, G], F32)
        self.r6 = sb(f"{tag}_r6", [128, G], F32)


def build_program(debug=False, stages=("cast", "prep", "p1", "p2a", "p2b"), nb=NB, p1_quant=True, p1_xpose=True, p1_qkv=True, p1_ev=4):
    nc = bass.Bass("TRN2", target_bir_lowering=False)

    x_h = nc.dram_tensor("x", [NB, S, D], F32, kind="ExternalInput")
    wq_h = nc.dram_tensor("wq", [D, D], F32, kind="ExternalInput")
    wk_h = nc.dram_tensor("wk", [D, D], F32, kind="ExternalInput")
    wv_h = nc.dram_tensor("wv", [D, D], F32, kind="ExternalInput")
    wo_h = nc.dram_tensor("wo", [D, D], F32, kind="ExternalInput")
    bq_h = nc.dram_tensor("bq", [D], F32, kind="ExternalInput")
    bk_h = nc.dram_tensor("bk", [D], F32, kind="ExternalInput")
    bv_h = nc.dram_tensor("bv", [D], F32, kind="ExternalInput")
    bo_h = nc.dram_tensor("bo", [D], F32, kind="ExternalInput")
    wg_h = nc.dram_tensor("wg", [DI, D], F32, kind="ExternalInput")
    wu_h = nc.dram_tensor("wu", [DI, D], F32, kind="ExternalInput")
    wd_h = nc.dram_tensor("wd", [D, DI], F32, kind="ExternalInput")
    out_h = nc.dram_tensor("out", [NB, S, D], F32, kind="ExternalOutput")

    kscr = "ExternalOutput" if debug else "Internal"
    wgN = nc.dram_tensor("wgN", [DI, D], BF16, kind=kscr)
    wuN = nc.dram_tensor("wuN", [DI, D], BF16, kind=kscr)
    wdN = nc.dram_tensor("wdN", [D, DI], BF16, kind=kscr)
    wqB = nc.dram_tensor("wqB", [D, D], BF16, kind="Internal")
    wkB = nc.dram_tensor("wkB", [D, D], BF16, kind="Internal")
    wvB = nc.dram_tensor("wvB", [D, D], BF16, kind="Internal")
    woB = nc.dram_tensor("woB", [D, D], BF16, kind="Internal")
    xnS = nc.dram_tensor("xnS", [NB, S, D], BF16, kind=kscr)
    xqS = nc.dram_tensor("xqS", [NB, S, D], BF16, kind=kscr)
    hqS = nc.dram_tensor("hqS", [NB, S, DI], BF16, kind=kscr)
    qS = nc.dram_tensor("qS", [NB, D, S], F32, kind=kscr)

    es = contextlib.ExitStack()

    def sb(name, shape, dtype=F32, **kw):
        return es.enter_context(nc.sbuf_tensor(name, shape, dtype, **kw))

    cXn = sb("xn", [128, D])
    cXnb = sb("xnb", [128, D], BF16)
    cXqb = sb("xqb", [128, D], BF16)
    cSq = [sb("sq0", [128, D]), sb("sq1", [128, D])]
    cSsum = sb("ssum", [128, NSUB])
    cSsq = sb("ssq", [128, NSUB])
    cTv = sb("tv", [128, NSUB])
    cVar = sb("var", [128, NSUB])
    cLv = sb("lv", [128, NSUB])
    cRstd = sb("rstd", [128, NSUB])
    cNmr = sb("nmr", [128, NSUB])
    cKvp = sb("kvp", [128, NT * DC])
    cKsp = sb("ksp", [128, NT * DC])
    cKv8 = sb("kv8", [128, DC])
    cKs8 = sb("ks8", [128, DC])
    cBq = sb("bq_c", [128, DC])
    cBk = sb("bk_c", [128, DC])
    cBv = sb("bv_c", [128, DC])
    cBoR = sb("boR", [128, D])
    cFeps = sb("cfeps", [128, 1])
    cMh = sb("cmh", [128, 1])
    cHlf = sb("chlf", [128, 1])
    ps = [es.enter_context(nc.psum_tensor(f"ps{i}", [128, 512], F32)) for i in range(8)]

    s_dx = Ctr(es, nc, "dx")
    s_dsp = Ctr(es, nc, "dsp")
    s_dg = Ctr(es, nc, "dg")
    s_v = Ctr(es, nc, "sv")
    s_a = Ctr(es, nc, "sa")
    s_t = Ctr(es, nc, "st")
    s_g = Ctr(es, nc, "sg")
    s_bar = [Ctr(es, nc, f"bar{i}") for i in range(5)]

    blk = es.enter_context(nc.Block())
    ENG = ["vector", "scalar", "tensor", "gpsimd", "sync"]

    def emit(eng, fn):
        getattr(blk, eng)(fn)

    def barrier():
        for i, e in enumerate(ENG):
            def bump(g, i=i, e=e):
                if e in ("vector", "gpsimd"):
                    ins = g.engine_nop()
                elif e in ("tensor", "scalar"):
                    ins = g.drain()
                else:
                    ins = g.nop()
                s_bar[i].inc(ins)
            emit(e, bump)
        targets = [c.v for c in s_bar]
        for i, e in enumerate(ENG):
            def w(g, i=i):
                for j in range(5):
                    if j != i:
                        s_bar[j].wait(g, targets[j])
            emit(e, w)

    q_u_done = [0]

    def quant_emit(tmp, jobs):
        offs = []
        o = 0
        for (_, _, w) in jobs:
            assert w % 16 == 0
            offs.append(o)
            o += w
        assert o <= tmp.W

        g_prev = s_g.v

        def vfn(v):
            s_g.wait(v, g_prev)
            for (src, _, w), off in zip(jobs, offs):
                g0 = off // 16
                gc = w // 16
                v.tensor_reduce(tmp.gm[:, g0:g0 + gc],
                                src.rearrange("p (g e) -> p g e", e=16),
                                axis=AX.X, op=ALU.max, apply_absolute_value=True)
            v.drain()
            for (src, _, w), off in zip(jobs, offs):
                g0 = off // 16
                gc = w // 16
                gsl = tmp.gm[:, g0:g0 + gc]
                v.tensor_scalar(tmp.sc[:, g0:g0 + gc], gsl, 6e-12, C12,
                                op0=ALU.max, op1=ALU.mult)
                v.reciprocal(tmp.r[:, g0:g0 + gc], gsl)
            v.drain()
            for (_, _, w), off in zip(jobs, offs):
                g0 = off // 16
                gc = w // 16
                v.tensor_scalar(tmp.r6[:, g0:g0 + gc], tmp.r[:, g0:g0 + gc],
                                6.0, 1e12, op0=ALU.mult, op1=ALU.min)
            v.drain()
            for (src, _, w), off in zip(jobs, offs):
                g0 = off // 16
                gc = w // 16
                v.tensor_tensor(tmp.u[:, off:off + w].rearrange(
                                    "p (g e) -> p g e", e=16),
                                src.rearrange("p (g e) -> p g e", e=16),
                                tmp.r6[:, g0:g0 + gc].unsqueeze(2)
                                .broadcast_to([128, gc, 16]), op=ALU.mult)
            s_v.inc(v.engine_nop())
        emit("vector", vfn)
        u_done = s_v.v
        q_u_done[0] = u_done

        def afn(a):
            s_v.wait(a, u_done)
            for (_, _, w), off in zip(jobs, offs):
                usl = tmp.u[:, off:off + w]
                a.activation(tmp.au[:, off:off + w], usl, AF.Abs)
                last = a.activation(tmp.sg[:, off:off + w], usl, AF.Sign)
            s_a.inc(last)
        emit("scalar", afn)
        au_done = s_a.v

        def vfn2(v):
            s_a.wait(v, au_done)
            for (_, _, w), off in zip(jobs, offs):
                au = tmp.au[:, off:off + w]
                aa = tmp.aA[:, off:off + w]
                ab = tmp.aB[:, off:off + w]
                v.tensor_single_scalar(aa, au, 0.25, op=ALU.is_gt)
                for th in (0.75, 1.25, 1.75):
                    v.scalar_tensor_tensor(aa, au, th, aa, op0=ALU.is_gt, op1=ALU.add)
                v.tensor_single_scalar(ab, au, 2.5, op=ALU.is_gt)
                for th in (3.5, 5.0, 5.0):
                    v.scalar_tensor_tensor(ab, au, th, ab, op0=ALU.is_gt, op1=ALU.add)
            for (_, _, w), off in zip(jobs, offs):
                aa = tmp.aA[:, off:off + w]
                ab = tmp.aB[:, off:off + w]
                v.scalar_tensor_tensor(aa, ab, 2.0, aa, op0=ALU.mult, op1=ALU.add)
            s_v.inc(v.engine_nop())
        emit("vector", vfn2)

        def vfn3(v):
            for (_, dst, w), off in zip(jobs, offs):
                g0 = off // 16
                gc = w // 16
                aa = tmp.aA[:, off:off + w]
                usl = tmp.u[:, off:off + w]
                v.tensor_tensor(usl, aa, tmp.sg[:, off:off + w], op=ALU.mult)
                v.tensor_tensor(dst.rearrange("p (g e) -> p g e", e=16),
                                usl.rearrange("p (g e) -> p g e", e=16),
                                tmp.sc[:, g0:g0 + gc].unsqueeze(2)
                                .broadcast_to([128, gc, 16]), op=ALU.mult)
            s_v.inc(v.engine_nop())
        emit("vector", vfn3)
        return (s_v, s_v.v)

    def xtile_in_ap(b, t0):
        return x_h.ap()[b, t0:t0 + TT, :].rearrange("(s p) d -> p s d", p=128)

    def xtile_out_ap(b, t0):
        return out_h.ap()[b, t0:t0 + TT, :].rearrange("(s p) d -> p s d", p=128)

    def init_consts(v):
        v.memset(cFeps[:], float(D) * float(np.float32(1e-5)))
        v.memset(cMh[:], -0.5)
        v.memset(cHlf[:], float(np.float32(0.5 * np.log(float(D)))))
        v.memset(cKvp[:], 0.0)
        v.memset(cKsp[:], 0.0)
        s_v.inc(v.engine_nop())
    emit("vector", init_consts)

    def load_biases(y):
        for src, dst in ((bq_h, cBq), (bk_h, cBk), (bv_h, cBv)):
            s_dx.inc(y.dma_start(out=dst[:], in_=src.ap().rearrange(
                "(c p) -> p c", p=128), allow_slow_non_contiguous=True), 16)
        s_dx.inc(y.dma_start(
            out=cBoR[:], in_=bo_h.ap().unsqueeze(0).partition_broadcast(128)), 16)
    emit("sync", load_biases)
    biases_ready = s_dx.v

    with contextlib.ExitStack() as pp:
        stg = [pp.enter_context(nc.sbuf_tensor(f"wstg{i}", [128, D], BF16))
               for i in range(2)]
        prev_store = 0
        for wi, (src, dst) in enumerate((((wq_h, wqB), (wk_h, wkB),
                                         (wv_h, wvB), (wo_h, woB))
                                         if "cast" in stages else ())):
            for c in range(DC):
                buf = stg[(wi * DC + c) % 2]

                def gload(g, src=src, c=c, buf=buf, prev_store=prev_store):
                    s_dsp.wait(g, prev_store)
                    s_dg.inc(g.dma_start(out=buf[:],
                                         in_=src.ap()[c * 128:(c + 1) * 128, :]), 16)
                emit("gpsimd", gload)
                ld_done = s_dg.v

                def sstore(a, dst=dst, c=c, buf=buf, ld_done=ld_done):
                    s_dg.wait(a, ld_done)
                    s_dsp.inc(a.dma_start(out=dst.ap()[c * 128:(c + 1) * 128, :],
                                          in_=buf[:]), 16)
                emit("scalar", sstore)
                if wi * DC + c >= 1:
                    prev_store = s_dsp.v - 16

        tmpP = QTmp(pp, nc, "qp", DI)
        wfin = [pp.enter_context(nc.sbuf_tensor(f"wfin{i}", [128, DI], F32))
                for i in range(2)]
        wfout = [pp.enter_context(nc.sbuf_tensor(f"wfout{i}", [128, DI], BF16))
                 for i in range(2)]
        jobs = [(wg_h, wgN, r, D) for r in range(IC)]
        jobs += [(wu_h, wuN, r, D) for r in range(IC)]
        jobs += [(wd_h, wdN, r, DI) for r in range(DC)]
        if "prep" not in stages:
            jobs = []
        uprev = [0, 0]
        stprev = [0, 0]
        for ji, (src, dst, r, K) in enumerate(jobs):
            par = ji % 2
            fin, fout = wfin[par], wfout[par]

            def yload(y, src=src, r=r, K=K, up=uprev[par], sp=stprev[par]):
                s_v.wait(y, up)
                s_dsp.wait(y, sp)
                s_dx.inc(y.dma_start(out=fin[:, 0:K],
                                     in_=src.ap()[r * 128:(r + 1) * 128, :]), 16)
            emit("sync", yload)
            ld_done = s_dx.v

            emit("vector", lambda v, ld_done=ld_done: s_dx.wait(v, ld_done))
            if K == D:
                rsem, ready = quant_emit(tmpP, [(fin[:, 0:D], fout[:, 0:D], D)])
            else:
                rsem, ready = quant_emit(
                    tmpP, [(fin[:, a0:a1], fout[:, a0:a1], a1 - a0)
                           for (a0, a1) in QT])
            uprev[par] = q_u_done[0]

            def sstore(a, dst=dst, r=r, K=K, rsem=rsem, ready=ready, fout=fout):
                rsem.wait(a, ready)
                s_dsp.inc(a.dma_start(out=dst.ap()[r * 128:(r + 1) * 128, :],
                                      in_=fout[:, 0:K]), 16)
            emit("scalar", sstore)
            stprev[par] = s_dsp.v
        barrier()

    for b in range(nb if ("p1" in stages) else 0):
        if b > 0:
            def vreset(v):
                v.memset(cKvp[:], 0.0)
                v.memset(cKsp[:], 0.0)
                s_v.inc(v.engine_nop())
            emit("vector", vreset)

        with contextlib.ExitStack() as p1:
            cXt = [p1.enter_context(nc.sbuf_tensor(f"p1xt{i}_{b}", [128, NSUB * D],
                                                   F32)) for i in range(2)]
            cW = {nm: p1.enter_context(nc.sbuf_tensor(f"p1w{nm}_{b}",
                                                      [128, DC * D], BF16))
                  for nm in ("q", "k", "v")}
            cXnT = p1.enter_context(nc.sbuf_tensor(
                f"p1xnt_{b}", [128, DC * 512], BF16, is_transpose_dest=True))
            cEv = [p1.enter_context(nc.sbuf_tensor(f"p1ev{i}_{b}", [128, 512], F32))
                   for i in range(7)]
            cScr = p1.enter_context(nc.sbuf_tensor(f"p1scr_{b}", [128, 512], F32))
            tmp1 = QTmp(p1, nc, f"q1_{b}", D)

            def ywl(y):
                for nm, srcB in (("q", wqB), ("k", wkB), ("v", wvB)):
                    for c in range(DC):
                        s_dx.inc(y.dma_start(
                            out=cW[nm][:, c * D:(c + 1) * D],
                            in_=srcB.ap()[c * 128:(c + 1) * 128, :]), 16)
            emit("sync", ywl)
            w_ready = s_dx.v
            if b == 0:
                emit("vector", lambda v: s_dx.wait(v, biases_ready))

            p1_bankfree = {0: 0, 1: 0}
            p1_ufree = [0]
            p1_xfree_a = {}
            p1_xfree_v = {}
            p1_qkv_last = {}
            for t in range(NT):
                t0 = t * TT
                xbuf = cXt[t % 2]

                def yx(y, b=b, t0=t0, xbuf=xbuf,
                       wa=p1_xfree_a.get(t - 2, 0), wv=p1_xfree_v.get(t - 2, 0)):
                    s_a.wait(y, wa)
                    s_v.wait(y, wv)
                    s_dx.inc(y.dma_start(out=xbuf[:], in_=xtile_in_ap(b, t0)), 16)
                emit("sync", yx)
                x_ready = s_dx.v

                prev_v = s_v.v

                def aS(a, xbuf=xbuf, x_ready=x_ready, prev_v=prev_v):
                    s_dx.wait(a, x_ready)
                    for s_i in range(NSUB):
                        if s_i >= 2:
                            s_v.wait(a, prev_v + s_i - 1)
                        s_a.inc(a.activation(cSq[s_i % 2][:],
                                xbuf[:, s_i * D:(s_i + 1) * D], AF.Square))
                emit("scalar", aS)
                sq_base = s_a.v - NSUB

                def vStats(v, xbuf=xbuf, x_ready=x_ready, sq_base=sq_base):
                    s_dx.wait(v, x_ready)
                    for s_i in range(NSUB):
                        v.tensor_reduce(cSsum[:, s_i:s_i + 1],
                                        xbuf[:, s_i * D:(s_i + 1) * D],
                                        axis=AX.X, op=ALU.add)
                        s_a.wait(v, sq_base + s_i + 1)
                        v.tensor_reduce(cSsq[:, s_i:s_i + 1], cSq[s_i % 2][:],
                                        axis=AX.X, op=ALU.add)
                        s_v.inc(v.engine_nop())
                    v.drain()
                    v.tensor_tensor(cTv[:], cSsum[:], cSsum[:], op=ALU.mult)
                    v.drain()
                    v.scalar_tensor_tensor(cVar[:], cTv[:], -INV_F, cSsq[:],
                                           op0=ALU.mult, op1=ALU.add)
                    v.drain()
                    s_v.inc(v.engine_nop())
                emit("vector", vStats)
                var_done = s_v.v
                p1_xfree_v[t] = var_done

                def aR(a, var_done=var_done):
                    s_v.wait(a, var_done)
                    a.activation(cLv[:], cVar[:], AF.Ln, bias=cFeps[:, 0:1])
                    last = a.activation(cRstd[:], cLv[:], AF.Exp,
                                        scale=cMh[:, 0:1], bias=cHlf[:, 0:1])
                    a.drain()
                    s_a.inc(last)
                emit("scalar", aR)
                rstd_done = s_a.v

                def vN(v, rstd_done=rstd_done):
                    s_a.wait(v, rstd_done)
                    v.scalar_tensor_tensor(cNmr[:], cSsum[:], -INV_F, cRstd[:],
                                           op0=ALU.mult, op1=ALU.mult)
                    v.drain()
                    s_v.inc(v.engine_nop())
                emit("vector", vN)
                nmr_done = s_v.v

                for s_i in range(NSUB if p1_quant else 0):
                    prev_xnb_spill = s_dx.v
                    prev_xq_spill = s_dsp.v

                    def aXn(a, xbuf=xbuf, s_i=s_i, nmr_done=nmr_done):
                        s_v.wait(a, s_v.v)
                        s_dx.wait(a, s_dx.v)
                        a.activation(cXn[:], xbuf[:, s_i * D:(s_i + 1) * D],
                                     AF.Identity, bias=cNmr[:, s_i:s_i + 1],
                                     scale=cRstd[:, s_i:s_i + 1])
                        s_a.inc(a.activation(
                            cXnb[:], xbuf[:, s_i * D:(s_i + 1) * D],
                            AF.Identity, bias=cNmr[:, s_i:s_i + 1],
                            scale=cRstd[:, s_i:s_i + 1]))
                    emit("scalar", aXn)
                    xn_done = s_a.v

                    def ySpXnb(y, b=b, t0=t0, s_i=s_i, xn_done=xn_done):
                        s_a.wait(y, xn_done)
                        row0 = t0 + s_i * 128
                        s_dx.inc(y.dma_start(
                            out=xnS.ap()[b, row0:row0 + 128, :], in_=cXnb[:]), 16)
                    emit("sync", ySpXnb)

                    def vwq(v, xn_done=xn_done, prev_xq_spill=prev_xq_spill):
                        s_a.wait(v, xn_done)
                        s_dsp.wait(v, prev_xq_spill)
                    emit("vector", vwq)
                    q_sem, q_ready = quant_emit(tmp1, [(cXn[:], cXqb[:], D)])
                    p1_ufree[0] = q_u_done[0]

                    def aSpXq(a, b=b, t0=t0, s_i=s_i, q_sem=q_sem, q_ready=q_ready):
                        q_sem.wait(a, q_ready)
                        row0 = t0 + s_i * 128
                        s_dsp.inc(a.dma_start(
                            out=xqS.ap()[b, row0:row0 + 128, :], in_=cXqb[:]), 16)
                    emit("scalar", aSpXq)

                p1_xfree_a[t] = s_a.v
                if not p1_xpose:
                    continue
                def yT(y, b=b, t0=t0, wt=p1_qkv_last.get(t - 1, 0)):
                    s_dx.wait(y, s_dx.v)
                    s_t.wait(y, wt)
                    for c in range(DC):
                        s_dx.inc(y.dma_start_transpose(
                            out=cXnT[:, c * 512:(c + 1) * 512],
                            in_=xnS.ap()[b, t0:t0 + TT, c * 128:(c + 1) * 128]), 16)
                emit("sync", yT)
                xnt_ready = s_dx.v

                for oc in range(DC if p1_qkv else 0):
                    bq3 = [ps[(oc % 2) * 3 + i] for i in range(3)]

                    def tQKV(t_, oc=oc, bq3=bq3, xnt_ready=xnt_ready,
                             w_ready=w_ready, bf=p1_bankfree[oc % 2]):
                        s_dx.wait(t_, xnt_ready)
                        s_dx.wait(t_, w_ready)
                        s_v.wait(t_, bf)
                        last = None
                        for pi, nm in enumerate(("q", "k", "v")):
                            for icc in range(DC):
                                last = t_.matmul(
                                    bq3[pi][:],
                                    cW[nm][:, icc * D + oc * 128:
                                           icc * D + (oc + 1) * 128],
                                    cXnT[:, icc * 512:(icc + 1) * 512],
                                    start=(icc == 0), stop=(icc == DC - 1))
                        s_t.inc(last)
                    emit("tensor", tQKV)
                    qkv_done = s_t.v
                    p1_qkv_last[t] = qkv_done

                    uq, uk, vp, tm, ee, Kc, Qc = cEv
                    if p1_ev == 0:
                        continue

                    def vEvac(v, oc=oc, bq3=bq3, qkv_done=qkv_done):
                        s_t.wait(v, qkv_done)
                        v.tensor_scalar(uq[:], bq3[0][:], cBq[:, oc:oc + 1], None,
                                        op0=ALU.add)
                        v.tensor_scalar(uk[:], bq3[1][:], cBk[:, oc:oc + 1], None,
                                        op0=ALU.add)
                        v.tensor_scalar(vp[:], bq3[2][:], cBv[:, oc:oc + 1], None,
                                        op0=ALU.add)
                        s_v.inc(v.engine_nop())
                        v.tensor_single_scalar(tm[:], uq[:], 0.0, op=ALU.min)
                        s_v.inc(v.engine_nop())
                    emit("vector", vEvac)
                    u_done = s_v.v
                    p1_bankfree[oc % 2] = u_done - 1
                    if p1_ev == 1:
                        continue

                    def aExpQ(a, u_done=u_done):
                        s_v.wait(a, u_done)
                        s_a.inc(a.activation(ee[:], tm[:], AF.Exp))
                    emit("scalar", aExpQ)
                    eq_done = s_a.v

                    def vQ(v, eq_done=eq_done, dsp_now=s_dsp.v):
                        s_a.wait(v, eq_done)
                        s_dsp.wait(v, dsp_now)
                        v.scalar_tensor_tensor(Qc[:], uq[:], 0.0, ee[:],
                                               op0=ALU.max, op1=ALU.add)
                        v.tensor_single_scalar(tm[:], uk[:], 0.0, op=ALU.min)
                        s_v.inc(v.engine_nop())
                    emit("vector", vQ)
                    q_done = s_v.v
                    if p1_ev == 2:
                        continue

                    def aSpQ(a, b=b, oc=oc, t0=t0, q_done=q_done):
                        s_v.wait(a, q_done)
                        s_dsp.inc(a.dma_start(
                            out=qS.ap()[b, oc * 128:(oc + 1) * 128, t0:t0 + TT],
                            in_=Qc[:]), 16)
                        s_a.inc(a.activation(ee[:], tm[:], AF.Exp))
                    emit("scalar", aSpQ)
                    ek_done = s_a.v
                    if p1_ev == 3:
                        continue

                    def vK(v, oc=oc, t=t, ek_done=ek_done):
                        s_a.wait(v, ek_done)
                        col = t * DC + oc
                        v.scalar_tensor_tensor(Kc[:], uk[:], 0.0, ee[:],
                                               op0=ALU.max, op1=ALU.add,
                                               accum_out=cKsp[:, col:col + 1])
                        v.scalar_tensor_tensor(cScr[:], Kc[:], 1.0, vp[:],
                                               op0=ALU.mult, op1=ALU.mult,
                                               accum_out=cKvp[:, col:col + 1])
                        s_v.inc(v.engine_nop())
                    emit("vector", vK)

            def vKvFin(v):
                v.drain()
                v.tensor_reduce(cKv8[:],
                                cKvp[:].rearrange("p (t c) -> p c t", c=DC),
                                axis=AX.X, op=ALU.add)
                v.tensor_reduce(cKs8[:],
                                cKsp[:].rearrange("p (t c) -> p c t", c=DC),
                                axis=AX.X, op=ALU.add)
                v.drain()
                s_v.inc(v.engine_nop())
            emit("vector", vKvFin)

        barrier()

        if "p2a" not in stages:
            continue
        with contextlib.ExitStack() as p2:
            cWg = p2.enter_context(nc.sbuf_tensor(
                f"p2wg_{b}", [128, DC * DI], BF16, is_transpose_dest=True))
            cWu = p2.enter_context(nc.sbuf_tensor(
                f"p2wu_{b}", [128, DC * DI], BF16, is_transpose_dest=True))
            cXqT = p2.enter_context(nc.sbuf_tensor(
                f"p2xqt_{b}", [128, DC * 512], BF16, is_transpose_dest=True))
            cH = p2.enter_context(nc.sbuf_tensor(f"p2h_{b}", [128, DI], F32))
            cSl = p2.enter_context(nc.sbuf_tensor(f"p2sl_{b}", [128, 512], F32))
            cHq = p2.enter_context(nc.sbuf_tensor(f"p2hq_{b}", [128, DI], BF16))
            tmp2 = QTmp(p2, nc, f"q2_{b}", DI)

            def ywg(y):
                for c in range(DC):
                    s_dx.inc(y.dma_start_transpose(
                        out=cWg[:, c * DI:(c + 1) * DI],
                        in_=wgN.ap()[:, c * 128:(c + 1) * 128]), 16)
                    s_dx.inc(y.dma_start_transpose(
                        out=cWu[:, c * DI:(c + 1) * DI],
                        in_=wuN.ap()[:, c * 128:(c + 1) * 128]), 16)
            emit("sync", ywg)
            wgu_ready = s_dx.v

            p2_bankfree = {0: 0, 1: 0}
            p2_slfree = [0]
            for t in range(NT):
                t0 = t * TT

                def yxq(y, b=b, t0=t0):
                    s_t.wait(y, s_t.v)
                    s_g.wait(y, s_g.v)
                    for c in range(DC):
                        s_dx.inc(y.dma_start_transpose(
                            out=cXqT[:, c * 512:(c + 1) * 512],
                            in_=xqS.ap()[b, t0:t0 + TT, c * 128:(c + 1) * 128]), 16)
                emit("sync", yxq)
                xqt_ready = s_dx.v

                for s_i in range(NSUB):
                    for io, (o0, o1) in enumerate(OSL):
                        bG = ps[(io % 2) * 2 + 0]
                        bU = ps[(io % 2) * 2 + 1]
                        ow = o1 - o0

                        def tGU(t_, s_i=s_i, o0=o0, o1=o1, ow=ow, bG=bG, bU=bU,
                                xqt_ready=xqt_ready, wgu_ready=wgu_ready,
                                bf=p2_bankfree[io % 2]):
                            s_dx.wait(t_, xqt_ready)
                            s_dx.wait(t_, wgu_ready)
                            s_v.wait(t_, bf)
                            last = None
                            for icc in range(DC):
                                st_ap = cXqT[:, icc * 512 + s_i * 128:
                                             icc * 512 + (s_i + 1) * 128]
                                t_.matmul(bG[:, 0:ow], st_ap,
                                          cWg[:, icc * DI + o0:icc * DI + o1],
                                          start=(icc == 0), stop=(icc == DC - 1))
                                last = t_.matmul(
                                    bU[:, 0:ow], st_ap,
                                    cWu[:, icc * DI + o0:icc * DI + o1],
                                    start=(icc == 0), stop=(icc == DC - 1))
                            s_t.inc(last)
                        emit("tensor", tGU)
                        gu_done = s_t.v

                        def aSilu(a, bG=bG, ow=ow, gu_done=gu_done,
                                  slf=p2_slfree[0]):
                            s_t.wait(a, gu_done)
                            s_v.wait(a, slf)
                            s_a.inc(a.activation(cSl[:, 0:ow], bG[:, 0:ow],
                                                 AF.Silu))
                        emit("scalar", aSilu)
                        sl_done = s_a.v

                        def vH(v, bU=bU, o0=o0, o1=o1, ow=ow, sl_done=sl_done):
                            s_a.wait(v, sl_done)
                            v.tensor_tensor(cH[:, o0:o1], cSl[:, 0:ow],
                                            bU[:, 0:ow], op=ALU.mult)
                            s_v.inc(v.engine_nop())
                        emit("vector", vH)
                        p2_bankfree[io % 2] = s_v.v
                        p2_slfree[0] = s_v.v

                    prev_hq_spill = s_dsp.v

                    def vwh(v, prev_hq_spill=prev_hq_spill):
                        s_dsp.wait(v, prev_hq_spill)
                    emit("vector", vwh)
                    hq_sem, hq_ready = quant_emit(
                        tmp2, [(cH[:, a0:a1], cHq[:, a0:a1], a1 - a0)
                               for (a0, a1) in QT])

                    def aSpHq(a, b=b, t0=t0, s_i=s_i, hq_sem=hq_sem,
                              hq_ready=hq_ready):
                        hq_sem.wait(a, hq_ready)
                        row0 = t0 + s_i * 128
                        s_dsp.inc(a.dma_start(
                            out=hqS.ap()[b, row0:row0 + 128, :], in_=cHq[:]), 16)
                    emit("scalar", aSpHq)

        barrier()

        if "p2b" not in stages:
            continue
        with contextlib.ExitStack() as p3:
            cWd = p3.enter_context(nc.sbuf_tensor(
                f"p3wd_{b}", [128, IC * D], BF16, is_transpose_dest=True))
            cWo = p3.enter_context(nc.sbuf_tensor(f"p3wo_{b}", [128, DC * D], BF16))
            cHqT = p3.enter_context(nc.sbuf_tensor(
                f"p3hqt_{b}", [128, IC * 512], BF16, is_transpose_dest=True))
            cQ = p3.enter_context(nc.sbuf_tensor(f"p3q_{b}", [128, DC * 512], F32))
            cA = p3.enter_context(nc.sbuf_tensor(f"p3a_{b}", [128, DC * 512], BF16))
            cZa = p3.enter_context(nc.sbuf_tensor(f"p3za_{b}", [128, 512], F32))
            cRz = p3.enter_context(nc.sbuf_tensor(f"p3rz_{b}", [128, DC * 512], F32))
            cGi = p3.enter_context(nc.sbuf_tensor(f"p3gi_{b}", [128, 512], F32))
            cGo = p3.enter_context(nc.sbuf_tensor(f"p3go_{b}", [128, 512], F32))
            cOut = p3.enter_context(nc.sbuf_tensor(f"p3out_{b}", [128, NSUB * D],
                                                   F32))
            cXr = p3.enter_context(nc.sbuf_tensor(f"p3xr_{b}", [128, NSUB * D],
                                                  F32))

            def ywd(y):
                for c in range(IC):
                    s_dx.inc(y.dma_start_transpose(
                        out=cWd[:, c * D:(c + 1) * D],
                        in_=wdN.ap()[:, c * 128:(c + 1) * 128]), 16)
                for c in range(DC):
                    s_dx.inc(y.dma_start(
                        out=cWo[:, c * D:(c + 1) * D],
                        in_=woB.ap()[c * 128:(c + 1) * 128, :]), 16)
            emit("sync", ywd)
            wdo_ready = s_dx.v

            p3_bankfree = {0: 0, 1: 0}
            for t in range(NT):
                t0 = t * TT

                def yld(y, b=b, t0=t0):
                    s_v.wait(y, s_v.v)
                    s_t.wait(y, s_t.v)
                    for c in range(IC):
                        s_dx.inc(y.dma_start_transpose(
                            out=cHqT[:, c * 512:(c + 1) * 512],
                            in_=hqS.ap()[b, t0:t0 + TT, c * 128:(c + 1) * 128]), 16)
                    for c in range(DC):
                        s_dx.inc(y.dma_start(
                            out=cQ[:, c * 512:(c + 1) * 512],
                            in_=qS.ap()[b, c * 128:(c + 1) * 128, t0:t0 + TT]), 16)
                    s_dx.inc(y.dma_start(out=cXr[:], in_=xtile_in_ap(b, t0)), 16)
                emit("sync", yld)
                ld_ready = s_dx.v

                prev_out_spill = s_dsp.v

                def vA(v, ld_ready=ld_ready, prev_out_spill=prev_out_spill):
                    s_dx.wait(v, ld_ready)
                    s_dsp.wait(v, prev_out_spill)
                    for c in range(DC):
                        v.tensor_scalar(cZa[:], cQ[:, c * 512:(c + 1) * 512],
                                        cKs8[:, c:c + 1], 1e-6,
                                        op0=ALU.mult, op1=ALU.add)
                        v.reciprocal(cRz[:, c * 512:(c + 1) * 512], cZa[:])
                    v.drain()
                    for c in range(DC):
                        v.scalar_tensor_tensor(
                            cA[:, c * 512:(c + 1) * 512],
                            cQ[:, c * 512:(c + 1) * 512], cKv8[:, c:c + 1],
                            cRz[:, c * 512:(c + 1) * 512],
                            op0=ALU.mult, op1=ALU.mult)
                    s_v.inc(v.engine_nop())
                emit("vector", vA)
                a_ready = s_v.v

                for s_i in range(NSUB):
                    par = s_i % 2
                    bD = [ps[par * 4 + 0], ps[par * 4 + 1]]
                    bW = [ps[par * 4 + 2], ps[par * 4 + 3]]

                    def tDW(t_, s_i=s_i, bD=bD, bW=bW, ld_ready=ld_ready,
                            a_ready=a_ready, wdo_ready=wdo_ready):
                        s_dx.wait(t_, ld_ready)
                        s_dx.wait(t_, wdo_ready)
                        s_v.wait(t_, s_v.v)
                        for icc in range(IC):
                            st_ap = cHqT[:, icc * 512 + s_i * 128:
                                         icc * 512 + (s_i + 1) * 128]
                            t_.matmul(bD[0][:], st_ap,
                                      cWd[:, icc * D:icc * D + 512],
                                      start=(icc == 0), stop=(icc == IC - 1))
                            t_.matmul(bD[1][:], st_ap,
                                      cWd[:, icc * D + 512:(icc + 1) * D],
                                      start=(icc == 0), stop=(icc == IC - 1))
                        last = None
                        for c in range(DC):
                            st_ap = cA[:, c * 512 + s_i * 128:
                                       c * 512 + (s_i + 1) * 128]
                            t_.matmul(bW[0][:], st_ap, cWo[:, c * D:c * D + 512],
                                      start=(c == 0), stop=(c == DC - 1))
                            last = t_.matmul(bW[1][:], st_ap,
                                             cWo[:, c * D + 512:(c + 1) * D],
                                             start=(c == 0), stop=(c == DC - 1))
                        s_t.inc(last)
                    emit("tensor", tDW)
                    dw_done = s_t.v

                    for half in range(2):
                        def vGi(v, half=half, bW=bW, dw_done=dw_done):
                            s_t.wait(v, dw_done)
                            v.tensor_tensor(cGi[:], bW[half][:],
                                            cBoR[:, half * 512:(half + 1) * 512],
                                            op=ALU.add)
                            s_v.inc(v.engine_nop())
                        emit("vector", vGi)
                        gi_done = s_v.v

                        def aG(a, gi_done=gi_done):
                            s_v.wait(a, gi_done)
                            s_a.inc(a.activation(cGo[:], cGi[:],
                                                 AF.Gelu_apprx_tanh))
                        emit("scalar", aG)
                        g_done = s_a.v

                        def vOut(v, s_i=s_i, half=half, bD=bD, g_done=g_done):
                            s_a.wait(v, g_done)
                            off = s_i * D + half * 512
                            v.tensor_tensor(cGi[:], cXr[:, off:off + 512],
                                            cGo[:], op=ALU.add)
                            v.tensor_tensor(cOut[:, off:off + 512], cGi[:],
                                            bD[half][:], op=ALU.add)
                            s_v.inc(v.engine_nop())
                        emit("vector", vOut)

                out_ready = s_v.v

                def aOut(a, b=b, t0=t0, out_ready=out_ready):
                    s_v.wait(a, out_ready)
                    s_dsp.inc(a.dma_start(out=xtile_out_ap(b, t0), in_=cOut[:]), 16)
                emit("scalar", aOut)

        barrier()

    emit("gpsimd", lambda g: s_dsp.wait(g, s_dsp.v))
    emit("scalar", lambda a: s_dsp.wait(a, s_dsp.v))

    es.close()
    return nc


_NC_CACHE = {}


def kernel(L0_attn_x, L0_attn_Wq, L0_attn_bq, L0_attn_Wk, L0_attn_bk,
           L0_attn_Wv, L0_attn_bv, L0_attn_Wo, L0_attn_bo,
           L0_ffn_w_gate, L0_ffn_w_up, L0_ffn_w_down):
    x = np.ascontiguousarray(np.asarray(L0_attn_x, dtype=np.float32))
    assert x.shape == (32, 2048, 1024)
    weights = {
        "wq": np.ascontiguousarray(np.asarray(L0_attn_Wq, np.float32)),
        "wk": np.ascontiguousarray(np.asarray(L0_attn_Wk, np.float32)),
        "wv": np.ascontiguousarray(np.asarray(L0_attn_Wv, np.float32)),
        "wo": np.ascontiguousarray(np.asarray(L0_attn_Wo, np.float32)),
        "bq": np.ascontiguousarray(np.asarray(L0_attn_bq, np.float32)),
        "bk": np.ascontiguousarray(np.asarray(L0_attn_bk, np.float32)),
        "bv": np.ascontiguousarray(np.asarray(L0_attn_bv, np.float32)),
        "bo": np.ascontiguousarray(np.asarray(L0_attn_bo, np.float32)),
        "wg": np.ascontiguousarray(np.asarray(L0_ffn_w_gate, np.float32)),
        "wu": np.ascontiguousarray(np.asarray(L0_ffn_w_up, np.float32)),
        "wd": np.ascontiguousarray(np.asarray(L0_ffn_w_down, np.float32)),
    }
    if "nc" not in _NC_CACHE:
        _NC_CACHE["nc"] = build_program()
    nc = _NC_CACHE["nc"]
    in_maps = []
    for core in range(NCORES):
        m = dict(weights)
        m["x"] = x[core * NB:(core + 1) * NB]
        in_maps.append(m)
    res = run_bass_kernel_spmd(nc, in_maps, core_ids=list(range(NCORES)))
    out = np.concatenate([np.asarray(r["out"]) for r in res.results], axis=0)
    return out.astype(np.float32)


# version 16
# speedup vs baseline: 2.3972x; 1.0152x over previous
import contextlib
import numpy as np

import concourse.bass as bass
import concourse.mybir as mybir
from concourse.bass_utils import run_bass_kernel_spmd

F32 = mybir.dt.float32


def _preload_set(a, nc, name):
    from concourse.hw_specs import get_activation_tables
    names = list(get_activation_tables(nc.m.arch).keys())
    a.add_instruction(mybir.InstLoadActFuncSet(
        name=nc.get_next_instruction_name(),
        act_func_set_id=names.index(name), ins=[], outs=[]))

BF16 = mybir.dt.bfloat16
AF = mybir.ActivationFunctionType
ALU = mybir.AluOpType
AX = mybir.AxisListType

NCORES = 8
NB = 4
S = 2048
D = 1024
DI = 2816
TT = 512
NT = S // TT
NSUB = TT // 128
DC = D // 128
IC = DI // 128
C12 = 1.0 / 12.0
INV_F = 1.0 / float(D)
OSL = [(0, 512), (512, 1024), (1024, 1536), (1536, 2048), (2048, 2560), (2560, 2816)]
QT = [(0, 1408), (1408, 2816)]


class Ctr:
    def __init__(self, es, nc, name):
        self.h = es.enter_context(nc.semaphore(name))
        self.v = 0

    def inc(self, inst, k=1):
        inst.then_inc(self.h, k)
        self.v += k
        return self.v

    def wait(self, eng, v=None):
        eng.wait_ge(self.h, self.v if v is None else v)


class QTmp:

    def __init__(self, scope, nc, tag, W):
        self.W = W
        sb = lambda n, s, d: scope.enter_context(nc.sbuf_tensor(n, s, d))
        self.u = sb(f"{tag}_u", [128, W], F32)
        self.au = sb(f"{tag}_au", [128, W], F32)
        self.sg = sb(f"{tag}_sg", [128, W], F32)
        self.aA = sb(f"{tag}_aA", [128, W], F32)
        self.aB = sb(f"{tag}_aB", [128, W], F32)
        self.aB2 = sb(f"{tag}_aB2", [128, W], F32)
        G = W // 16
        self.gm = sb(f"{tag}_gm", [128, G], F32)
        self.sc = sb(f"{tag}_sc", [128, G], F32)
        self.r = sb(f"{tag}_r", [128, G], F32)
        self.r6 = sb(f"{tag}_r6", [128, G], F32)


def build_program(debug=False, stages=("cast", "prep", "p1", "p2a", "p2b"), nb=NB, p1_quant=True, p1_xpose=True, p1_qkv=True, p1_ev=4):
    nc = bass.Bass("TRN2", target_bir_lowering=False)

    x_h = nc.dram_tensor("x", [NB, S, D], F32, kind="ExternalInput")
    wq_h = nc.dram_tensor("wq", [D, D], F32, kind="ExternalInput")
    wk_h = nc.dram_tensor("wk", [D, D], F32, kind="ExternalInput")
    wv_h = nc.dram_tensor("wv", [D, D], F32, kind="ExternalInput")
    wo_h = nc.dram_tensor("wo", [D, D], F32, kind="ExternalInput")
    bq_h = nc.dram_tensor("bq", [D], F32, kind="ExternalInput")
    bk_h = nc.dram_tensor("bk", [D], F32, kind="ExternalInput")
    bv_h = nc.dram_tensor("bv", [D], F32, kind="ExternalInput")
    bo_h = nc.dram_tensor("bo", [D], F32, kind="ExternalInput")
    wg_h = nc.dram_tensor("wg", [DI, D], F32, kind="ExternalInput")
    wu_h = nc.dram_tensor("wu", [DI, D], F32, kind="ExternalInput")
    wd_h = nc.dram_tensor("wd", [D, DI], F32, kind="ExternalInput")
    out_h = nc.dram_tensor("out", [NB, S, D], F32, kind="ExternalOutput")

    kscr = "ExternalOutput" if debug else "Internal"
    wgN = nc.dram_tensor("wgN", [DI, D], BF16, kind=kscr)
    wuN = nc.dram_tensor("wuN", [DI, D], BF16, kind=kscr)
    wdN = nc.dram_tensor("wdN", [D, DI], BF16, kind=kscr)
    wqB = nc.dram_tensor("wqB", [D, D], BF16, kind="Internal")
    wkB = nc.dram_tensor("wkB", [D, D], BF16, kind="Internal")
    wvB = nc.dram_tensor("wvB", [D, D], BF16, kind="Internal")
    woB = nc.dram_tensor("woB", [D, D], BF16, kind="Internal")
    xnS = nc.dram_tensor("xnS", [NB, S, D], BF16, kind=kscr)
    xqS = nc.dram_tensor("xqS", [NB, S, D], BF16, kind=kscr)
    hqS = nc.dram_tensor("hqS", [NB, S, DI], BF16, kind=kscr)
    qS = nc.dram_tensor("qS", [NB, D, S], F32, kind=kscr)

    es = contextlib.ExitStack()

    def sb(name, shape, dtype=F32, **kw):
        return es.enter_context(nc.sbuf_tensor(name, shape, dtype, **kw))

    cXn = sb("xn", [128, D])
    cXnb = sb("xnb", [128, D], BF16)
    cXqb = sb("xqb", [128, D], BF16)
    cSq = [sb("sq0", [128, D]), sb("sq1", [128, D])]
    cSsum = sb("ssum", [128, NSUB])
    cSsq = sb("ssq", [128, NSUB])
    cTv = sb("tv", [128, NSUB])
    cVar = sb("var", [128, NSUB])
    cLv = sb("lv", [128, NSUB])
    cRstd = sb("rstd", [128, NSUB])
    cNmr = sb("nmr", [128, NSUB])
    cKvp = sb("kvp", [128, NT * DC])
    cKsp = sb("ksp", [128, NT * DC])
    cKv8 = sb("kv8", [128, DC])
    cKs8 = sb("ks8", [128, DC])
    cBq = sb("bq_c", [128, DC])
    cBk = sb("bk_c", [128, DC])
    cBv = sb("bv_c", [128, DC])
    cBoR = sb("boR", [128, D])
    cFeps = sb("cfeps", [128, 1])
    cMh = sb("cmh", [128, 1])
    cHlf = sb("chlf", [128, 1])
    ps = [es.enter_context(nc.psum_tensor(f"ps{i}", [128, 512], F32)) for i in range(8)]

    s_dx = Ctr(es, nc, "dx")
    s_dsp = Ctr(es, nc, "dsp")
    s_dg = Ctr(es, nc, "dg")
    s_v = Ctr(es, nc, "sv")
    s_a = Ctr(es, nc, "sa")
    s_t = Ctr(es, nc, "st")
    s_g = Ctr(es, nc, "sg")
    s_bar = [Ctr(es, nc, f"bar{i}") for i in range(5)]

    blk = es.enter_context(nc.Block())
    ENG = ["vector", "scalar", "tensor", "gpsimd", "sync"]

    def emit(eng, fn):
        getattr(blk, eng)(fn)

    def barrier():
        for i, e in enumerate(ENG):
            def bump(g, i=i, e=e):
                if e in ("vector", "gpsimd"):
                    ins = g.engine_nop()
                elif e in ("tensor", "scalar"):
                    ins = g.drain()
                else:
                    ins = g.nop()
                s_bar[i].inc(ins)
            emit(e, bump)
        targets = [c.v for c in s_bar]
        for i, e in enumerate(ENG):
            def w(g, i=i):
                for j in range(5):
                    if j != i:
                        s_bar[j].wait(g, targets[j])
            emit(e, w)

    q_u_done = [0]

    def quant_emit(tmp, jobs):
        offs = []
        o = 0
        for (_, _, w) in jobs:
            assert w % 16 == 0
            offs.append(o)
            o += w
        assert o <= tmp.W

        g_prev = s_g.v

        def vfn(v):
            s_g.wait(v, g_prev)
            for (src, _, w), off in zip(jobs, offs):
                g0 = off // 16
                gc = w // 16
                v.tensor_reduce(tmp.gm[:, g0:g0 + gc],
                                src.rearrange("p (g e) -> p g e", e=16),
                                axis=AX.X, op=ALU.max, apply_absolute_value=True)
            v.drain()
            for (src, _, w), off in zip(jobs, offs):
                g0 = off // 16
                gc = w // 16
                gsl = tmp.gm[:, g0:g0 + gc]
                v.tensor_scalar(tmp.sc[:, g0:g0 + gc], gsl, 6e-12, C12,
                                op0=ALU.max, op1=ALU.mult)
                v.reciprocal(tmp.r[:, g0:g0 + gc], gsl)
            v.drain()
            for (_, _, w), off in zip(jobs, offs):
                g0 = off // 16
                gc = w // 16
                v.tensor_scalar(tmp.r6[:, g0:g0 + gc], tmp.r[:, g0:g0 + gc],
                                6.0, 1e12, op0=ALU.mult, op1=ALU.min)
            v.drain()
            for (src, _, w), off in zip(jobs, offs):
                g0 = off // 16
                gc = w // 16
                v.tensor_tensor(tmp.u[:, off:off + w].rearrange(
                                    "p (g e) -> p g e", e=16),
                                src.rearrange("p (g e) -> p g e", e=16),
                                tmp.r6[:, g0:g0 + gc].unsqueeze(2)
                                .broadcast_to([128, gc, 16]), op=ALU.mult)
            s_v.inc(v.engine_nop())
        emit("vector", vfn)
        u_done = s_v.v
        q_u_done[0] = u_done

        def vfn2(v):
            for (_, _, w), off in zip(jobs, offs):
                usl = tmp.u[:, off:off + w]
                v.tensor_scalar(tmp.sg[:, off:off + w], usl, 0.0, 2.0,
                                op0=ALU.is_gt, op1=ALU.mult)
            for (_, _, w), off in zip(jobs, offs):
                sgs = tmp.sg[:, off:off + w]
                v.tensor_single_scalar(sgs, sgs, 1.0, op=ALU.subtract)
            for (_, _, w), off in zip(jobs, offs):
                v.tensor_tensor(tmp.au[:, off:off + w], tmp.u[:, off:off + w],
                                tmp.sg[:, off:off + w], op=ALU.mult)
            for (_, _, w), off in zip(jobs, offs):
                au = tmp.au[:, off:off + w]
                aa = tmp.aA[:, off:off + w]
                ab = tmp.aB[:, off:off + w]
                v.tensor_single_scalar(aa, au, 0.25, op=ALU.is_gt)
                for th in (0.75, 1.25, 1.75):
                    v.scalar_tensor_tensor(aa, au, th, aa, op0=ALU.is_gt, op1=ALU.add)
                v.tensor_single_scalar(ab, au, 2.5, op=ALU.is_gt)
                for th in (3.5, 5.0, 5.0):
                    v.scalar_tensor_tensor(ab, au, th, ab, op0=ALU.is_gt, op1=ALU.add)
            for (_, _, w), off in zip(jobs, offs):
                aa = tmp.aA[:, off:off + w]
                ab = tmp.aB[:, off:off + w]
                v.scalar_tensor_tensor(aa, ab, 2.0, aa, op0=ALU.mult, op1=ALU.add)
            s_v.inc(v.engine_nop())
        emit("vector", vfn2)

        def vfn3(v):
            for (_, dst, w), off in zip(jobs, offs):
                g0 = off // 16
                gc = w // 16
                aa = tmp.aA[:, off:off + w]
                usl = tmp.u[:, off:off + w]
                v.tensor_tensor(usl, aa, tmp.sg[:, off:off + w], op=ALU.mult)
                v.tensor_tensor(dst.rearrange("p (g e) -> p g e", e=16),
                                usl.rearrange("p (g e) -> p g e", e=16),
                                tmp.sc[:, g0:g0 + gc].unsqueeze(2)
                                .broadcast_to([128, gc, 16]), op=ALU.mult)
            s_v.inc(v.engine_nop())
        emit("vector", vfn3)
        return (s_v, s_v.v)

    def xtile_in_ap(b, t0):
        return x_h.ap()[b, t0:t0 + TT, :].rearrange("(s p) d -> p s d", p=128)

    def xtile_out_ap(b, t0):
        return out_h.ap()[b, t0:t0 + TT, :].rearrange("(s p) d -> p s d", p=128)

    def init_consts(v):
        v.memset(cFeps[:], float(D) * float(np.float32(1e-5)))
        v.memset(cMh[:], -0.5)
        v.memset(cHlf[:], float(np.float32(0.5 * np.log(float(D)))))
        v.memset(cKvp[:], 0.0)
        v.memset(cKsp[:], 0.0)
        s_v.inc(v.engine_nop())
    emit("vector", init_consts)

    def load_biases(y):
        for src, dst in ((bq_h, cBq), (bk_h, cBk), (bv_h, cBv)):
            s_dx.inc(y.dma_start(out=dst[:], in_=src.ap().rearrange(
                "(c p) -> p c", p=128), allow_slow_non_contiguous=True), 16)
        s_dx.inc(y.dma_start(
            out=cBoR[:], in_=bo_h.ap().unsqueeze(0).partition_broadcast(128)), 16)
    emit("sync", load_biases)
    biases_ready = s_dx.v

    with contextlib.ExitStack() as pp:
        stg = [pp.enter_context(nc.sbuf_tensor(f"wstg{i}", [128, D], BF16))
               for i in range(2)]
        prev_store = 0
        for wi, (src, dst) in enumerate((((wq_h, wqB), (wk_h, wkB),
                                         (wv_h, wvB), (wo_h, woB))
                                         if "cast" in stages else ())):
            for c in range(DC):
                buf = stg[(wi * DC + c) % 2]

                def gload(g, src=src, c=c, buf=buf, prev_store=prev_store):
                    s_dsp.wait(g, prev_store)
                    s_dg.inc(g.dma_start(out=buf[:],
                                         in_=src.ap()[c * 128:(c + 1) * 128, :]), 16)
                emit("gpsimd", gload)
                ld_done = s_dg.v

                def sstore(a, dst=dst, c=c, buf=buf, ld_done=ld_done):
                    s_dg.wait(a, ld_done)
                    s_dsp.inc(a.dma_start(out=dst.ap()[c * 128:(c + 1) * 128, :],
                                          in_=buf[:]), 16)
                emit("scalar", sstore)
                if wi * DC + c >= 1:
                    prev_store = s_dsp.v - 16

        tmpP = QTmp(pp, nc, "qp", DI)
        wfin = [pp.enter_context(nc.sbuf_tensor(f"wfin{i}", [128, DI], F32))
                for i in range(2)]
        wfout = [pp.enter_context(nc.sbuf_tensor(f"wfout{i}", [128, DI], BF16))
                 for i in range(2)]
        jobs = [(wg_h, wgN, r, D) for r in range(IC)]
        jobs += [(wu_h, wuN, r, D) for r in range(IC)]
        jobs += [(wd_h, wdN, r, DI) for r in range(DC)]
        if "prep" not in stages:
            jobs = []
        uprev = [0, 0]
        stprev = [0, 0]
        for ji, (src, dst, r, K) in enumerate(jobs):
            par = ji % 2
            fin, fout = wfin[par], wfout[par]

            def yload(y, src=src, r=r, K=K, up=uprev[par], sp=stprev[par]):
                s_v.wait(y, up)
                s_dsp.wait(y, sp)
                s_dx.inc(y.dma_start(out=fin[:, 0:K],
                                     in_=src.ap()[r * 128:(r + 1) * 128, :]), 16)
            emit("sync", yload)
            ld_done = s_dx.v

            emit("vector", lambda v, ld_done=ld_done: s_dx.wait(v, ld_done))
            if K == D:
                rsem, ready = quant_emit(tmpP, [(fin[:, 0:D], fout[:, 0:D], D)])
            else:
                rsem, ready = quant_emit(
                    tmpP, [(fin[:, a0:a1], fout[:, a0:a1], a1 - a0)
                           for (a0, a1) in QT])
            uprev[par] = q_u_done[0]

            def sstore(a, dst=dst, r=r, K=K, rsem=rsem, ready=ready, fout=fout):
                rsem.wait(a, ready)
                s_dsp.inc(a.dma_start(out=dst.ap()[r * 128:(r + 1) * 128, :],
                                      in_=fout[:, 0:K]), 16)
            emit("scalar", sstore)
            stprev[par] = s_dsp.v
        barrier()

    for b in range(nb if ("p1" in stages) else 0):
        if b > 0:
            def vreset(v):
                v.memset(cKvp[:], 0.0)
                v.memset(cKsp[:], 0.0)
                s_v.inc(v.engine_nop())
            emit("vector", vreset)

        with contextlib.ExitStack() as p1:
            cXt = [p1.enter_context(nc.sbuf_tensor(f"p1xt{i}_{b}", [128, NSUB * D],
                                                   F32)) for i in range(2)]
            cW = {nm: p1.enter_context(nc.sbuf_tensor(f"p1w{nm}_{b}",
                                                      [128, DC * D], BF16))
                  for nm in ("q", "k", "v")}
            cXnT = p1.enter_context(nc.sbuf_tensor(
                f"p1xnt_{b}", [128, DC * 512], BF16, is_transpose_dest=True))
            cEv = [p1.enter_context(nc.sbuf_tensor(f"p1ev{i}_{b}", [128, 512], F32))
                   for i in range(7)]
            cScr = p1.enter_context(nc.sbuf_tensor(f"p1scr_{b}", [128, 512], F32))
            tmp1 = QTmp(p1, nc, f"q1_{b}", D)

            def ywl(y):
                for nm, srcB in (("q", wqB), ("k", wkB), ("v", wvB)):
                    for c in range(DC):
                        s_dx.inc(y.dma_start(
                            out=cW[nm][:, c * D:(c + 1) * D],
                            in_=srcB.ap()[c * 128:(c + 1) * 128, :]), 16)
            emit("sync", ywl)
            w_ready = s_dx.v
            if b == 0:
                emit("vector", lambda v: s_dx.wait(v, biases_ready))

            p1_bankfree = {0: 0, 1: 0}
            p1_ufree = [0]
            p1_xfree_a = {}
            p1_xfree_v = {}
            p1_qkv_last = {}
            for t in range(NT):
                t0 = t * TT
                xbuf = cXt[t % 2]

                def yx(y, b=b, t0=t0, xbuf=xbuf,
                       wa=p1_xfree_a.get(t - 2, 0), wv=p1_xfree_v.get(t - 2, 0)):
                    s_a.wait(y, wa)
                    s_v.wait(y, wv)
                    s_dx.inc(y.dma_start(out=xbuf[:], in_=xtile_in_ap(b, t0)), 16)
                emit("sync", yx)
                x_ready = s_dx.v

                prev_v = s_v.v

                def aS(a, xbuf=xbuf, x_ready=x_ready, prev_v=prev_v):
                    s_dx.wait(a, x_ready)
                    for s_i in range(NSUB):
                        if s_i >= 2:
                            s_v.wait(a, prev_v + s_i - 1)
                        s_a.inc(a.activation(cSq[s_i % 2][:],
                                xbuf[:, s_i * D:(s_i + 1) * D], AF.Square))
                emit("scalar", aS)
                sq_base = s_a.v - NSUB

                def vStats(v, xbuf=xbuf, x_ready=x_ready, sq_base=sq_base):
                    s_dx.wait(v, x_ready)
                    for s_i in range(NSUB):
                        v.tensor_reduce(cSsum[:, s_i:s_i + 1],
                                        xbuf[:, s_i * D:(s_i + 1) * D],
                                        axis=AX.X, op=ALU.add)
                        s_a.wait(v, sq_base + s_i + 1)
                        v.tensor_reduce(cSsq[:, s_i:s_i + 1], cSq[s_i % 2][:],
                                        axis=AX.X, op=ALU.add)
                        s_v.inc(v.engine_nop())
                    v.drain()
                    v.tensor_tensor(cTv[:], cSsum[:], cSsum[:], op=ALU.mult)
                    v.drain()
                    v.scalar_tensor_tensor(cVar[:], cTv[:], -INV_F, cSsq[:],
                                           op0=ALU.mult, op1=ALU.add)
                    v.drain()
                    s_v.inc(v.engine_nop())
                emit("vector", vStats)
                var_done = s_v.v
                p1_xfree_v[t] = var_done

                def aR(a, var_done=var_done):
                    s_v.wait(a, var_done)
                    a.activation(cLv[:], cVar[:], AF.Ln, bias=cFeps[:, 0:1])
                    last = a.activation(cRstd[:], cLv[:], AF.Exp,
                                        scale=cMh[:, 0:1], bias=cHlf[:, 0:1])
                    a.drain()
                    s_a.inc(last)
                emit("scalar", aR)
                rstd_done = s_a.v

                def vN(v, rstd_done=rstd_done):
                    s_a.wait(v, rstd_done)
                    v.scalar_tensor_tensor(cNmr[:], cSsum[:], -INV_F, cRstd[:],
                                           op0=ALU.mult, op1=ALU.mult)
                    v.drain()
                    s_v.inc(v.engine_nop())
                emit("vector", vN)
                nmr_done = s_v.v

                for s_i in range(NSUB if p1_quant else 0):
                    prev_xnb_spill = s_dx.v
                    prev_xq_spill = s_dsp.v

                    def aXn(a, xbuf=xbuf, s_i=s_i, nmr_done=nmr_done):
                        s_v.wait(a, s_v.v)
                        s_dx.wait(a, s_dx.v)
                        a.activation(cXn[:], xbuf[:, s_i * D:(s_i + 1) * D],
                                     AF.Identity, bias=cNmr[:, s_i:s_i + 1],
                                     scale=cRstd[:, s_i:s_i + 1])
                        s_a.inc(a.activation(
                            cXnb[:], xbuf[:, s_i * D:(s_i + 1) * D],
                            AF.Identity, bias=cNmr[:, s_i:s_i + 1],
                            scale=cRstd[:, s_i:s_i + 1]))
                    emit("scalar", aXn)
                    xn_done = s_a.v

                    def ySpXnb(y, b=b, t0=t0, s_i=s_i, xn_done=xn_done):
                        s_a.wait(y, xn_done)
                        row0 = t0 + s_i * 128
                        s_dx.inc(y.dma_start(
                            out=xnS.ap()[b, row0:row0 + 128, :], in_=cXnb[:]), 16)
                    emit("sync", ySpXnb)

                    def vwq(v, xn_done=xn_done, prev_xq_spill=prev_xq_spill):
                        s_a.wait(v, xn_done)
                        s_dsp.wait(v, prev_xq_spill)
                    emit("vector", vwq)
                    q_sem, q_ready = quant_emit(tmp1, [(cXn[:], cXqb[:], D)])
                    p1_ufree[0] = q_u_done[0]

                    def aSpXq(a, b=b, t0=t0, s_i=s_i, q_sem=q_sem, q_ready=q_ready):
                        q_sem.wait(a, q_ready)
                        row0 = t0 + s_i * 128
                        s_dsp.inc(a.dma_start(
                            out=xqS.ap()[b, row0:row0 + 128, :], in_=cXqb[:]), 16)
                    emit("scalar", aSpXq)

                p1_xfree_a[t] = s_a.v
                if not p1_xpose:
                    continue
                def yT(y, b=b, t0=t0, wt=p1_qkv_last.get(t - 1, 0)):
                    s_dx.wait(y, s_dx.v)
                    s_t.wait(y, wt)
                    for c in range(DC):
                        s_dx.inc(y.dma_start_transpose(
                            out=cXnT[:, c * 512:(c + 1) * 512],
                            in_=xnS.ap()[b, t0:t0 + TT, c * 128:(c + 1) * 128]), 16)
                emit("sync", yT)
                xnt_ready = s_dx.v

                for oc in range(DC if p1_qkv else 0):
                    bq3 = [ps[(oc % 2) * 3 + i] for i in range(3)]

                    def tQKV(t_, oc=oc, bq3=bq3, xnt_ready=xnt_ready,
                             w_ready=w_ready, bf=p1_bankfree[oc % 2]):
                        s_dx.wait(t_, xnt_ready)
                        s_dx.wait(t_, w_ready)
                        s_v.wait(t_, bf)
                        last = None
                        for pi, nm in enumerate(("q", "k", "v")):
                            for icc in range(DC):
                                last = t_.matmul(
                                    bq3[pi][:],
                                    cW[nm][:, icc * D + oc * 128:
                                           icc * D + (oc + 1) * 128],
                                    cXnT[:, icc * 512:(icc + 1) * 512],
                                    start=(icc == 0), stop=(icc == DC - 1))
                        s_t.inc(last)
                    emit("tensor", tQKV)
                    qkv_done = s_t.v
                    p1_qkv_last[t] = qkv_done

                    uq, uk, vp, tm, ee, Kc, Qc = cEv
                    if p1_ev == 0:
                        continue

                    def vEvac(v, oc=oc, bq3=bq3, qkv_done=qkv_done):
                        s_t.wait(v, qkv_done)
                        v.tensor_scalar(uq[:], bq3[0][:], cBq[:, oc:oc + 1], None,
                                        op0=ALU.add)
                        v.tensor_scalar(uk[:], bq3[1][:], cBk[:, oc:oc + 1], None,
                                        op0=ALU.add)
                        v.tensor_scalar(vp[:], bq3[2][:], cBv[:, oc:oc + 1], None,
                                        op0=ALU.add)
                        s_v.inc(v.engine_nop())
                        v.tensor_single_scalar(tm[:], uq[:], 0.0, op=ALU.min)
                        s_v.inc(v.engine_nop())
                    emit("vector", vEvac)
                    u_done = s_v.v
                    p1_bankfree[oc % 2] = u_done - 1
                    if p1_ev == 1:
                        continue

                    def aExpQ(a, u_done=u_done):
                        s_v.wait(a, u_done)
                        s_a.inc(a.activation(ee[:], tm[:], AF.Exp))
                    emit("scalar", aExpQ)
                    eq_done = s_a.v

                    def vQ(v, eq_done=eq_done, dsp_now=s_dsp.v):
                        s_a.wait(v, eq_done)
                        s_dsp.wait(v, dsp_now)
                        v.scalar_tensor_tensor(Qc[:], uq[:], 0.0, ee[:],
                                               op0=ALU.max, op1=ALU.add)
                        v.tensor_single_scalar(tm[:], uk[:], 0.0, op=ALU.min)
                        s_v.inc(v.engine_nop())
                    emit("vector", vQ)
                    q_done = s_v.v
                    if p1_ev == 2:
                        continue

                    def aSpQ(a, b=b, oc=oc, t0=t0, q_done=q_done):
                        s_v.wait(a, q_done)
                        s_dsp.inc(a.dma_start(
                            out=qS.ap()[b, oc * 128:(oc + 1) * 128, t0:t0 + TT],
                            in_=Qc[:]), 16)
                        s_a.inc(a.activation(ee[:], tm[:], AF.Exp))
                    emit("scalar", aSpQ)
                    ek_done = s_a.v
                    if p1_ev == 3:
                        continue

                    def vK(v, oc=oc, t=t, ek_done=ek_done):
                        s_a.wait(v, ek_done)
                        col = t * DC + oc
                        v.scalar_tensor_tensor(Kc[:], uk[:], 0.0, ee[:],
                                               op0=ALU.max, op1=ALU.add,
                                               accum_out=cKsp[:, col:col + 1])
                        v.scalar_tensor_tensor(cScr[:], Kc[:], 1.0, vp[:],
                                               op0=ALU.mult, op1=ALU.mult,
                                               accum_out=cKvp[:, col:col + 1])
                        s_v.inc(v.engine_nop())
                    emit("vector", vK)

            def vKvFin(v):
                v.drain()
                v.tensor_reduce(cKv8[:],
                                cKvp[:].rearrange("p (t c) -> p c t", c=DC),
                                axis=AX.X, op=ALU.add)
                v.tensor_reduce(cKs8[:],
                                cKsp[:].rearrange("p (t c) -> p c t", c=DC),
                                axis=AX.X, op=ALU.add)
                v.drain()
                s_v.inc(v.engine_nop())
            emit("vector", vKvFin)

        barrier()

        if "p2a" not in stages:
            continue
        with contextlib.ExitStack() as p2:
            cWg = p2.enter_context(nc.sbuf_tensor(
                f"p2wg_{b}", [128, DC * DI], BF16, is_transpose_dest=True))
            cWu = p2.enter_context(nc.sbuf_tensor(
                f"p2wu_{b}", [128, DC * DI], BF16, is_transpose_dest=True))
            cXqT = p2.enter_context(nc.sbuf_tensor(
                f"p2xqt_{b}", [128, DC * 512], BF16, is_transpose_dest=True))
            cH = p2.enter_context(nc.sbuf_tensor(f"p2h_{b}", [128, DI], F32))
            cSl = p2.enter_context(nc.sbuf_tensor(f"p2sl_{b}", [128, 512], F32))
            cHq = p2.enter_context(nc.sbuf_tensor(f"p2hq_{b}", [128, DI], BF16))
            tmp2 = QTmp(p2, nc, f"q2_{b}", DI)

            def ywg(y):
                for c in range(DC):
                    s_dx.inc(y.dma_start_transpose(
                        out=cWg[:, c * DI:(c + 1) * DI],
                        in_=wgN.ap()[:, c * 128:(c + 1) * 128]), 16)
                    s_dx.inc(y.dma_start_transpose(
                        out=cWu[:, c * DI:(c + 1) * DI],
                        in_=wuN.ap()[:, c * 128:(c + 1) * 128]), 16)
            emit("sync", ywg)
            wgu_ready = s_dx.v

            p2_bankfree = {0: 0, 1: 0}
            p2_slfree = [0]
            for t in range(NT):
                t0 = t * TT

                def yxq(y, b=b, t0=t0):
                    s_t.wait(y, s_t.v)
                    s_g.wait(y, s_g.v)
                    for c in range(DC):
                        s_dx.inc(y.dma_start_transpose(
                            out=cXqT[:, c * 512:(c + 1) * 512],
                            in_=xqS.ap()[b, t0:t0 + TT, c * 128:(c + 1) * 128]), 16)
                emit("sync", yxq)
                xqt_ready = s_dx.v

                for s_i in range(NSUB):
                    for io, (o0, o1) in enumerate(OSL):
                        bG = ps[(io % 2) * 2 + 0]
                        bU = ps[(io % 2) * 2 + 1]
                        ow = o1 - o0

                        def tGU(t_, s_i=s_i, o0=o0, o1=o1, ow=ow, bG=bG, bU=bU,
                                xqt_ready=xqt_ready, wgu_ready=wgu_ready,
                                bf=p2_bankfree[io % 2]):
                            s_dx.wait(t_, xqt_ready)
                            s_dx.wait(t_, wgu_ready)
                            s_v.wait(t_, bf)
                            last = None
                            for icc in range(DC):
                                st_ap = cXqT[:, icc * 512 + s_i * 128:
                                             icc * 512 + (s_i + 1) * 128]
                                t_.matmul(bG[:, 0:ow], st_ap,
                                          cWg[:, icc * DI + o0:icc * DI + o1],
                                          start=(icc == 0), stop=(icc == DC - 1))
                                last = t_.matmul(
                                    bU[:, 0:ow], st_ap,
                                    cWu[:, icc * DI + o0:icc * DI + o1],
                                    start=(icc == 0), stop=(icc == DC - 1))
                            s_t.inc(last)
                        emit("tensor", tGU)
                        gu_done = s_t.v

                        def aSilu(a, bG=bG, ow=ow, gu_done=gu_done,
                                  slf=p2_slfree[0]):
                            s_t.wait(a, gu_done)
                            s_v.wait(a, slf)
                            s_a.inc(a.activation(cSl[:, 0:ow], bG[:, 0:ow],
                                                 AF.Silu))
                        emit("scalar", aSilu)
                        sl_done = s_a.v

                        def vH(v, bU=bU, o0=o0, o1=o1, ow=ow, sl_done=sl_done):
                            s_a.wait(v, sl_done)
                            v.tensor_tensor(cH[:, o0:o1], cSl[:, 0:ow],
                                            bU[:, 0:ow], op=ALU.mult)
                            s_v.inc(v.engine_nop())
                        emit("vector", vH)
                        p2_bankfree[io % 2] = s_v.v
                        p2_slfree[0] = s_v.v

                    prev_hq_spill = s_dsp.v

                    def vwh(v, prev_hq_spill=prev_hq_spill):
                        s_dsp.wait(v, prev_hq_spill)
                    emit("vector", vwh)
                    hq_sem, hq_ready = quant_emit(
                        tmp2, [(cH[:, a0:a1], cHq[:, a0:a1], a1 - a0)
                               for (a0, a1) in QT])

                    def aSpHq(a, b=b, t0=t0, s_i=s_i, hq_sem=hq_sem,
                              hq_ready=hq_ready):
                        hq_sem.wait(a, hq_ready)
                        row0 = t0 + s_i * 128
                        s_dsp.inc(a.dma_start(
                            out=hqS.ap()[b, row0:row0 + 128, :], in_=cHq[:]), 16)
                    emit("scalar", aSpHq)

        barrier()

        if "p2b" not in stages:
            continue
        with contextlib.ExitStack() as p3:
            cWd = p3.enter_context(nc.sbuf_tensor(
                f"p3wd_{b}", [128, IC * D], BF16, is_transpose_dest=True))
            cWo = p3.enter_context(nc.sbuf_tensor(f"p3wo_{b}", [128, DC * D], BF16))
            cHqT = p3.enter_context(nc.sbuf_tensor(
                f"p3hqt_{b}", [128, IC * 512], BF16, is_transpose_dest=True))
            cQ = p3.enter_context(nc.sbuf_tensor(f"p3q_{b}", [128, DC * 512], F32))
            cA = p3.enter_context(nc.sbuf_tensor(f"p3a_{b}", [128, DC * 512], BF16))
            cZa = p3.enter_context(nc.sbuf_tensor(f"p3za_{b}", [128, 512], F32))
            cRz = p3.enter_context(nc.sbuf_tensor(f"p3rz_{b}", [128, DC * 512], F32))
            cGi = p3.enter_context(nc.sbuf_tensor(f"p3gi_{b}", [128, 512], F32))
            cGo = p3.enter_context(nc.sbuf_tensor(f"p3go_{b}", [128, 512], F32))
            cOut = p3.enter_context(nc.sbuf_tensor(f"p3out_{b}", [128, NSUB * D],
                                                   F32))
            cXr = p3.enter_context(nc.sbuf_tensor(f"p3xr_{b}", [128, NSUB * D],
                                                  F32))

            def ywd(y):
                for c in range(IC):
                    s_dx.inc(y.dma_start_transpose(
                        out=cWd[:, c * D:(c + 1) * D],
                        in_=wdN.ap()[:, c * 128:(c + 1) * 128]), 16)
                for c in range(DC):
                    s_dx.inc(y.dma_start(
                        out=cWo[:, c * D:(c + 1) * D],
                        in_=woB.ap()[c * 128:(c + 1) * 128, :]), 16)
            emit("sync", ywd)
            wdo_ready = s_dx.v

            p3_bankfree = {0: 0, 1: 0}
            for t in range(NT):
                t0 = t * TT

                def yld(y, b=b, t0=t0):
                    s_v.wait(y, s_v.v)
                    s_t.wait(y, s_t.v)
                    for c in range(IC):
                        s_dx.inc(y.dma_start_transpose(
                            out=cHqT[:, c * 512:(c + 1) * 512],
                            in_=hqS.ap()[b, t0:t0 + TT, c * 128:(c + 1) * 128]), 16)
                    for c in range(DC):
                        s_dx.inc(y.dma_start(
                            out=cQ[:, c * 512:(c + 1) * 512],
                            in_=qS.ap()[b, c * 128:(c + 1) * 128, t0:t0 + TT]), 16)
                    s_dx.inc(y.dma_start(out=cXr[:], in_=xtile_in_ap(b, t0)), 16)
                emit("sync", yld)
                ld_ready = s_dx.v

                prev_out_spill = s_dsp.v

                def vA(v, ld_ready=ld_ready, prev_out_spill=prev_out_spill):
                    s_dx.wait(v, ld_ready)
                    s_dsp.wait(v, prev_out_spill)
                    for c in range(DC):
                        v.tensor_scalar(cZa[:], cQ[:, c * 512:(c + 1) * 512],
                                        cKs8[:, c:c + 1], 1e-6,
                                        op0=ALU.mult, op1=ALU.add)
                        v.reciprocal(cRz[:, c * 512:(c + 1) * 512], cZa[:])
                    v.drain()
                    for c in range(DC):
                        v.scalar_tensor_tensor(
                            cA[:, c * 512:(c + 1) * 512],
                            cQ[:, c * 512:(c + 1) * 512], cKv8[:, c:c + 1],
                            cRz[:, c * 512:(c + 1) * 512],
                            op0=ALU.mult, op1=ALU.mult)
                    s_v.inc(v.engine_nop())
                emit("vector", vA)
                a_ready = s_v.v

                for s_i in range(NSUB):
                    par = s_i % 2
                    bD = [ps[par * 4 + 0], ps[par * 4 + 1]]
                    bW = [ps[par * 4 + 2], ps[par * 4 + 3]]

                    def tDW(t_, s_i=s_i, bD=bD, bW=bW, ld_ready=ld_ready,
                            a_ready=a_ready, wdo_ready=wdo_ready):
                        s_dx.wait(t_, ld_ready)
                        s_dx.wait(t_, wdo_ready)
                        s_v.wait(t_, s_v.v)
                        for icc in range(IC):
                            st_ap = cHqT[:, icc * 512 + s_i * 128:
                                         icc * 512 + (s_i + 1) * 128]
                            t_.matmul(bD[0][:], st_ap,
                                      cWd[:, icc * D:icc * D + 512],
                                      start=(icc == 0), stop=(icc == IC - 1))
                            t_.matmul(bD[1][:], st_ap,
                                      cWd[:, icc * D + 512:(icc + 1) * D],
                                      start=(icc == 0), stop=(icc == IC - 1))
                        last = None
                        for c in range(DC):
                            st_ap = cA[:, c * 512 + s_i * 128:
                                       c * 512 + (s_i + 1) * 128]
                            t_.matmul(bW[0][:], st_ap, cWo[:, c * D:c * D + 512],
                                      start=(c == 0), stop=(c == DC - 1))
                            last = t_.matmul(bW[1][:], st_ap,
                                             cWo[:, c * D + 512:(c + 1) * D],
                                             start=(c == 0), stop=(c == DC - 1))
                        s_t.inc(last)
                    emit("tensor", tDW)
                    dw_done = s_t.v

                    for half in range(2):
                        def vGi(v, half=half, bW=bW, dw_done=dw_done):
                            s_t.wait(v, dw_done)
                            v.tensor_tensor(cGi[:], bW[half][:],
                                            cBoR[:, half * 512:(half + 1) * 512],
                                            op=ALU.add)
                            s_v.inc(v.engine_nop())
                        emit("vector", vGi)
                        gi_done = s_v.v

                        def aG(a, gi_done=gi_done):
                            s_v.wait(a, gi_done)
                            s_a.inc(a.activation(cGo[:], cGi[:],
                                                 AF.Gelu_apprx_tanh))
                        emit("scalar", aG)
                        g_done = s_a.v

                        def vOut(v, s_i=s_i, half=half, bD=bD, g_done=g_done):
                            s_a.wait(v, g_done)
                            off = s_i * D + half * 512
                            v.tensor_tensor(cGi[:], cXr[:, off:off + 512],
                                            cGo[:], op=ALU.add)
                            v.tensor_tensor(cOut[:, off:off + 512], cGi[:],
                                            bD[half][:], op=ALU.add)
                            s_v.inc(v.engine_nop())
                        emit("vector", vOut)

                out_ready = s_v.v

                def aOut(a, b=b, t0=t0, out_ready=out_ready):
                    s_v.wait(a, out_ready)
                    s_dsp.inc(a.dma_start(out=xtile_out_ap(b, t0), in_=cOut[:]), 16)
                emit("scalar", aOut)

        barrier()

    emit("gpsimd", lambda g: s_dsp.wait(g, s_dsp.v))
    emit("scalar", lambda a: s_dsp.wait(a, s_dsp.v))

    es.close()
    return nc


_NC_CACHE = {}


def kernel(L0_attn_x, L0_attn_Wq, L0_attn_bq, L0_attn_Wk, L0_attn_bk,
           L0_attn_Wv, L0_attn_bv, L0_attn_Wo, L0_attn_bo,
           L0_ffn_w_gate, L0_ffn_w_up, L0_ffn_w_down):
    x = np.ascontiguousarray(np.asarray(L0_attn_x, dtype=np.float32))
    assert x.shape == (32, 2048, 1024)
    weights = {
        "wq": np.ascontiguousarray(np.asarray(L0_attn_Wq, np.float32)),
        "wk": np.ascontiguousarray(np.asarray(L0_attn_Wk, np.float32)),
        "wv": np.ascontiguousarray(np.asarray(L0_attn_Wv, np.float32)),
        "wo": np.ascontiguousarray(np.asarray(L0_attn_Wo, np.float32)),
        "bq": np.ascontiguousarray(np.asarray(L0_attn_bq, np.float32)),
        "bk": np.ascontiguousarray(np.asarray(L0_attn_bk, np.float32)),
        "bv": np.ascontiguousarray(np.asarray(L0_attn_bv, np.float32)),
        "bo": np.ascontiguousarray(np.asarray(L0_attn_bo, np.float32)),
        "wg": np.ascontiguousarray(np.asarray(L0_ffn_w_gate, np.float32)),
        "wu": np.ascontiguousarray(np.asarray(L0_ffn_w_up, np.float32)),
        "wd": np.ascontiguousarray(np.asarray(L0_ffn_w_down, np.float32)),
    }
    if "nc" not in _NC_CACHE:
        _NC_CACHE["nc"] = build_program()
    nc = _NC_CACHE["nc"]
    in_maps = []
    for core in range(NCORES):
        m = dict(weights)
        m["x"] = x[core * NB:(core + 1) * NB]
        in_maps.append(m)
    res = run_bass_kernel_spmd(nc, in_maps, core_ids=list(range(NCORES)))
    out = np.concatenate([np.asarray(r["out"]) for r in res.results], axis=0)
    return out.astype(np.float32)


# version 21
# speedup vs baseline: 2.4707x; 1.0307x over previous
import contextlib
import numpy as np

import concourse.bass as bass
import concourse.mybir as mybir
from concourse.bass_utils import run_bass_kernel_spmd

F32 = mybir.dt.float32


def _preload_set(a, nc, name):
    from concourse.hw_specs import get_activation_tables
    names = list(get_activation_tables(nc.m.arch).keys())
    a.add_instruction(mybir.InstLoadActFuncSet(
        name=nc.get_next_instruction_name(),
        act_func_set_id=names.index(name), ins=[], outs=[]))

BF16 = mybir.dt.bfloat16
AF = mybir.ActivationFunctionType
ALU = mybir.AluOpType
AX = mybir.AxisListType

NCORES = 8
NB = 4
S = 2048
D = 1024
DI = 2816
TT = 512
NT = S // TT
NSUB = TT // 128
DC = D // 128
IC = DI // 128
C12 = 1.0 / 12.0
INV_F = 1.0 / float(D)
OSL = [(0, 512), (512, 1024), (1024, 1536), (1536, 2048), (2048, 2560), (2560, 2816)]
QT = [(0, 1408), (1408, 2816)]


class Ctr:
    def __init__(self, es, nc, name):
        self.h = es.enter_context(nc.semaphore(name))
        self.v = 0

    def inc(self, inst, k=1):
        inst.then_inc(self.h, k)
        self.v += k
        return self.v

    def wait(self, eng, v=None):
        eng.wait_ge(self.h, self.v if v is None else v)


class QTmp:

    def __init__(self, scope, nc, tag, W):
        self.W = W
        sb = lambda n, s, d: scope.enter_context(nc.sbuf_tensor(n, s, d))
        self.u = sb(f"{tag}_u", [128, W], F32)
        self.au = sb(f"{tag}_au", [128, W], F32)
        self.sg = sb(f"{tag}_sg", [128, W], F32)
        self.aA = sb(f"{tag}_aA", [128, W], F32)
        self.aB = sb(f"{tag}_aB", [128, W], F32)
        G = W // 16
        self.gm = sb(f"{tag}_gm", [128, G], F32)
        self.sc = sb(f"{tag}_sc", [128, G], F32)
        self.r = sb(f"{tag}_r", [128, G], F32)
        self.r6 = sb(f"{tag}_r6", [128, G], F32)


def build_program(debug=False, stages=("cast", "prep", "p1", "p2a", "p2b"), nb=NB, p1_quant=True, p1_xpose=True, p1_qkv=True, p1_ev=4):
    nc = bass.Bass("TRN2", target_bir_lowering=False)

    x_h = nc.dram_tensor("x", [NB, S, D], F32, kind="ExternalInput")
    wq_h = nc.dram_tensor("wq", [D, D], F32, kind="ExternalInput")
    wk_h = nc.dram_tensor("wk", [D, D], F32, kind="ExternalInput")
    wv_h = nc.dram_tensor("wv", [D, D], F32, kind="ExternalInput")
    wo_h = nc.dram_tensor("wo", [D, D], F32, kind="ExternalInput")
    bq_h = nc.dram_tensor("bq", [D], F32, kind="ExternalInput")
    bk_h = nc.dram_tensor("bk", [D], F32, kind="ExternalInput")
    bv_h = nc.dram_tensor("bv", [D], F32, kind="ExternalInput")
    bo_h = nc.dram_tensor("bo", [D], F32, kind="ExternalInput")
    wg_h = nc.dram_tensor("wg", [DI, D], F32, kind="ExternalInput")
    wu_h = nc.dram_tensor("wu", [DI, D], F32, kind="ExternalInput")
    wd_h = nc.dram_tensor("wd", [D, DI], F32, kind="ExternalInput")
    out_h = nc.dram_tensor("out", [NB, S, D], F32, kind="ExternalOutput")

    kscr = "ExternalOutput" if debug else "Internal"
    wgN = nc.dram_tensor("wgN", [DI, D], BF16, kind=kscr)
    wuN = nc.dram_tensor("wuN", [DI, D], BF16, kind=kscr)
    wdN = nc.dram_tensor("wdN", [D, DI], BF16, kind=kscr)
    wqB = nc.dram_tensor("wqB", [D, D], BF16, kind="Internal")
    wkB = nc.dram_tensor("wkB", [D, D], BF16, kind="Internal")
    wvB = nc.dram_tensor("wvB", [D, D], BF16, kind="Internal")
    woB = nc.dram_tensor("woB", [D, D], BF16, kind="Internal")
    xnS = nc.dram_tensor("xnS", [NB, S, D], BF16, kind=kscr)
    xqS = nc.dram_tensor("xqS", [NB, S, D], BF16, kind=kscr)
    hqS = nc.dram_tensor("hqS", [NB, S, DI], BF16, kind=kscr)
    qS = nc.dram_tensor("qS", [NB, D, S], F32, kind=kscr)

    es = contextlib.ExitStack()

    def sb(name, shape, dtype=F32, **kw):
        return es.enter_context(nc.sbuf_tensor(name, shape, dtype, **kw))

    cXn = sb("xn", [128, D])
    cXnb = sb("xnb", [128, NSUB * D], BF16)
    cXqb = sb("xqb", [128, NSUB * D], BF16)
    cSq = [sb("sq0", [128, D]), sb("sq1", [128, D])]
    cSsum = sb("ssum", [128, NSUB])
    cSsq = sb("ssq", [128, NSUB])
    cTv = sb("tv", [128, NSUB])
    cVar = sb("var", [128, NSUB])
    cLv = sb("lv", [128, NSUB])
    cRstd = sb("rstd", [128, NSUB])
    cNmr = sb("nmr", [128, NSUB])
    cNMean = sb("nmean", [128, NSUB])
    cKvp = sb("kvp", [128, NT * DC])
    cKsp = sb("ksp", [128, NT * DC])
    cKv8 = sb("kv8", [128, DC])
    cKs8 = sb("ks8", [128, DC])
    cBq = sb("bq_c", [128, DC])
    cBk = sb("bk_c", [128, DC])
    cBv = sb("bv_c", [128, DC])
    cBoR = sb("boR", [128, D])
    cFeps = sb("cfeps", [128, 1])
    cMh = sb("cmh", [128, 1])
    cHlf = sb("chlf", [128, 1])
    ps = [es.enter_context(nc.psum_tensor(f"ps{i}", [128, 512], F32)) for i in range(8)]

    s_dx = Ctr(es, nc, "dx")
    s_dsp = Ctr(es, nc, "dsp")
    s_dg = Ctr(es, nc, "dg")
    s_v = Ctr(es, nc, "sv")
    s_a = Ctr(es, nc, "sa")
    s_t = Ctr(es, nc, "st")
    s_g = Ctr(es, nc, "sg")
    s_bar = [Ctr(es, nc, f"bar{i}") for i in range(5)]

    blk = es.enter_context(nc.Block())
    ENG = ["vector", "scalar", "tensor", "gpsimd", "sync"]

    def emit(eng, fn):
        getattr(blk, eng)(fn)

    def barrier():
        for i, e in enumerate(ENG):
            def bump(g, i=i, e=e):
                if e in ("vector", "gpsimd"):
                    ins = g.engine_nop()
                elif e in ("tensor", "scalar"):
                    ins = g.drain()
                else:
                    ins = g.nop()
                s_bar[i].inc(ins)
            emit(e, bump)
        targets = [c.v for c in s_bar]
        for i, e in enumerate(ENG):
            def w(g, i=i):
                for j in range(5):
                    if j != i:
                        s_bar[j].wait(g, targets[j])
            emit(e, w)

    q_u_done = [0]

    def quant_emit(tmp, jobs):
        offs = []
        o = 0
        for (_, _, w) in jobs:
            assert w % 16 == 0
            offs.append(o)
            o += w
        assert o <= tmp.W

        g_prev = s_g.v

        def vfn(v):
            s_g.wait(v, g_prev)
            for (src, _, w), off in zip(jobs, offs):
                g0 = off // 16
                gc = w // 16
                v.tensor_reduce(tmp.gm[:, g0:g0 + gc],
                                src.rearrange("p (g e) -> p g e", e=16),
                                axis=AX.X, op=ALU.max, apply_absolute_value=True)
            v.drain()
            for (src, _, w), off in zip(jobs, offs):
                g0 = off // 16
                gc = w // 16
                gsl = tmp.gm[:, g0:g0 + gc]
                v.tensor_scalar(tmp.sc[:, g0:g0 + gc], gsl, 6e-12, C12,
                                op0=ALU.max, op1=ALU.mult)
                v.reciprocal(tmp.r[:, g0:g0 + gc], gsl)
            v.drain()
            for (_, _, w), off in zip(jobs, offs):
                g0 = off // 16
                gc = w // 16
                v.tensor_scalar(tmp.r6[:, g0:g0 + gc], tmp.r[:, g0:g0 + gc],
                                6.0, 1e12, op0=ALU.mult, op1=ALU.min)
            v.drain()
            for (src, _, w), off in zip(jobs, offs):
                g0 = off // 16
                gc = w // 16
                v.tensor_tensor(tmp.u[:, off:off + w].rearrange(
                                    "p (g e) -> p g e", e=16),
                                src.rearrange("p (g e) -> p g e", e=16),
                                tmp.r6[:, g0:g0 + gc].unsqueeze(2)
                                .broadcast_to([128, gc, 16]), op=ALU.mult)
            s_v.inc(v.engine_nop())
        emit("vector", vfn)
        u_done = s_v.v
        q_u_done[0] = u_done

        def vfn2(v):
            for (_, _, w), off in zip(jobs, offs):
                usl = tmp.u[:, off:off + w]
                v.tensor_scalar(tmp.sg[:, off:off + w], usl, 0.0, 2.0,
                                op0=ALU.is_gt, op1=ALU.mult)
            for (_, _, w), off in zip(jobs, offs):
                sgs = tmp.sg[:, off:off + w]
                v.tensor_single_scalar(sgs, sgs, 1.0, op=ALU.subtract)
            for (_, _, w), off in zip(jobs, offs):
                v.tensor_tensor(tmp.au[:, off:off + w], tmp.u[:, off:off + w],
                                tmp.sg[:, off:off + w], op=ALU.mult)
            for (_, _, w), off in zip(jobs, offs):
                au = tmp.au[:, off:off + w]
                aa = tmp.aA[:, off:off + w]
                ab = tmp.aB[:, off:off + w]
                v.tensor_single_scalar(aa, au, 0.25, op=ALU.is_gt)
                for th in (0.75, 1.25, 1.75):
                    v.scalar_tensor_tensor(aa, au, th, aa, op0=ALU.is_gt, op1=ALU.add)
                v.tensor_single_scalar(ab, au, 2.5, op=ALU.is_gt)
                for th in (3.5, 5.0, 5.0):
                    v.scalar_tensor_tensor(ab, au, th, ab, op0=ALU.is_gt, op1=ALU.add)
            for (_, _, w), off in zip(jobs, offs):
                aa = tmp.aA[:, off:off + w]
                ab = tmp.aB[:, off:off + w]
                v.scalar_tensor_tensor(aa, ab, 2.0, aa, op0=ALU.mult, op1=ALU.add)
            s_v.inc(v.engine_nop())
        emit("vector", vfn2)

        def vfn3(v):
            for (_, dst, w), off in zip(jobs, offs):
                g0 = off // 16
                gc = w // 16
                aa = tmp.aA[:, off:off + w]
                usl = tmp.u[:, off:off + w]
                v.tensor_tensor(usl, aa, tmp.sg[:, off:off + w], op=ALU.mult)
                v.tensor_tensor(dst.rearrange("p (g e) -> p g e", e=16),
                                usl.rearrange("p (g e) -> p g e", e=16),
                                tmp.sc[:, g0:g0 + gc].unsqueeze(2)
                                .broadcast_to([128, gc, 16]), op=ALU.mult)
            v.drain()
            s_v.inc(v.engine_nop())
        emit("vector", vfn3)
        return (s_v, s_v.v)

    def xtile_in_ap(b, t0):
        return x_h.ap()[b, t0:t0 + TT, :].rearrange("(s p) d -> p s d", p=128)

    def xtile_out_ap(b, t0):
        return out_h.ap()[b, t0:t0 + TT, :].rearrange("(s p) d -> p s d", p=128)

    def init_consts(v):
        v.memset(cFeps[:], float(D) * float(np.float32(1e-5)))
        v.memset(cMh[:], -0.5)
        v.memset(cHlf[:], float(np.float32(0.5 * np.log(float(D)))))
        v.memset(cKvp[:], 0.0)
        v.memset(cKsp[:], 0.0)
        s_v.inc(v.engine_nop())
    emit("vector", init_consts)

    def load_biases(y):
        for src, dst in ((bq_h, cBq), (bk_h, cBk), (bv_h, cBv)):
            s_dx.inc(y.dma_start(out=dst[:], in_=src.ap().rearrange(
                "(c p) -> p c", p=128), allow_slow_non_contiguous=True), 16)
        s_dx.inc(y.dma_start(
            out=cBoR[:], in_=bo_h.ap().unsqueeze(0).partition_broadcast(128)), 16)
    emit("sync", load_biases)
    biases_ready = s_dx.v

    with contextlib.ExitStack() as pp:
        stg = [pp.enter_context(nc.sbuf_tensor(f"wstg{i}", [128, D], BF16))
               for i in range(2)]
        prev_store = 0
        for wi, (src, dst) in enumerate((((wq_h, wqB), (wk_h, wkB),
                                         (wv_h, wvB), (wo_h, woB))
                                         if "cast" in stages else ())):
            for c in range(DC):
                buf = stg[(wi * DC + c) % 2]

                def gload(g, src=src, c=c, buf=buf, prev_store=prev_store):
                    s_dsp.wait(g, prev_store)
                    s_dg.inc(g.dma_start(out=buf[:],
                                         in_=src.ap()[c * 128:(c + 1) * 128, :]), 16)
                emit("gpsimd", gload)
                ld_done = s_dg.v

                def sstore(a, dst=dst, c=c, buf=buf, ld_done=ld_done):
                    s_dg.wait(a, ld_done)
                    s_dsp.inc(a.dma_start(out=dst.ap()[c * 128:(c + 1) * 128, :],
                                          in_=buf[:]), 16)
                emit("scalar", sstore)
                if wi * DC + c >= 1:
                    prev_store = s_dsp.v - 16

        tmpP = QTmp(pp, nc, "qp", DI)
        wfin = [pp.enter_context(nc.sbuf_tensor(f"wfin{i}", [128, DI], F32))
                for i in range(2)]
        wfout = [pp.enter_context(nc.sbuf_tensor(f"wfout{i}", [128, DI], BF16))
                 for i in range(2)]
        jobs = [(wg_h, wgN, r, D) for r in range(IC)]
        jobs += [(wu_h, wuN, r, D) for r in range(IC)]
        jobs += [(wd_h, wdN, r, DI) for r in range(DC)]
        if "prep" not in stages:
            jobs = []
        uprev = [0, 0]
        stprev = [0, 0]
        for ji, (src, dst, r, K) in enumerate(jobs):
            par = ji % 2
            fin, fout = wfin[par], wfout[par]

            def yload(y, src=src, r=r, K=K, up=uprev[par], sp=stprev[par]):
                s_v.wait(y, up)
                s_dsp.wait(y, sp)
                s_dx.inc(y.dma_start(out=fin[:, 0:K],
                                     in_=src.ap()[r * 128:(r + 1) * 128, :]), 16)
            emit("sync", yload)
            ld_done = s_dx.v

            emit("vector", lambda v, ld_done=ld_done: s_dx.wait(v, ld_done))
            if K == D:
                rsem, ready = quant_emit(tmpP, [(fin[:, 0:D], fout[:, 0:D], D)])
            else:
                rsem, ready = quant_emit(
                    tmpP, [(fin[:, a0:a1], fout[:, a0:a1], a1 - a0)
                           for (a0, a1) in QT])
            uprev[par] = q_u_done[0]

            def sstore(a, dst=dst, r=r, K=K, rsem=rsem, ready=ready, fout=fout):
                rsem.wait(a, ready)
                s_dsp.inc(a.dma_start(out=dst.ap()[r * 128:(r + 1) * 128, :],
                                      in_=fout[:, 0:K]), 16)
            emit("scalar", sstore)
            stprev[par] = s_dsp.v
        barrier()

    for b in range(nb if ("p1" in stages) else 0):
        if b > 0:
            def vreset(v):
                v.memset(cKvp[:], 0.0)
                v.memset(cKsp[:], 0.0)
                s_v.inc(v.engine_nop())
            emit("vector", vreset)

        with contextlib.ExitStack() as p1:
            cXt = [p1.enter_context(nc.sbuf_tensor(f"p1xt{i}_{b}", [128, NSUB * D],
                                                   F32)) for i in range(2)]
            cW = {nm: p1.enter_context(nc.sbuf_tensor(f"p1w{nm}_{b}",
                                                      [128, DC * D], BF16))
                  for nm in ("q", "k", "v")}
            cXnT2 = [p1.enter_context(nc.sbuf_tensor(
                f"p1xnt{i}_{b}", [128, DC * 512], BF16, is_transpose_dest=True))
                for i in range(2)]
            cEv = [p1.enter_context(nc.sbuf_tensor(f"p1ev{i}_{b}", [128, 512], F32))
                   for i in range(7)]
            cQc2 = [p1.enter_context(nc.sbuf_tensor(f"p1qc{i}_{b}", [128, 512],
                                                    F32)) for i in range(2)]
            cScr = p1.enter_context(nc.sbuf_tensor(f"p1scr_{b}", [128, 512], F32))
            tmp1 = QTmp(p1, nc, f"q1_{b}", D)

            def ywl(y):
                for nm, srcB in (("q", wqB), ("k", wkB), ("v", wvB)):
                    for c in range(DC):
                        s_dx.inc(y.dma_start(
                            out=cW[nm][:, c * D:(c + 1) * D],
                            in_=srcB.ap()[c * 128:(c + 1) * 128, :]), 16)
            emit("sync", ywl)
            w_ready = s_dx.v
            if b == 0:
                emit("vector", lambda v: s_dx.wait(v, biases_ready))

            p1_bankfree = {0: 0, 1: 0}
            p1_ufree = [0]
            p1_xfree_a = {}
            p1_xfree_v = {}
            p1_qkv_last = {}
            p1_stageb = {}
            pend_xq = []

            def emit_stage_a(t):
                t0 = t * TT
                xbuf = cXt[t % 2]
                for fn in pend_xq:
                    emit("scalar", fn)
                pend_xq.clear()

                def yx(y, b=b, t0=t0, xbuf=xbuf,
                       wa=p1_xfree_a.get(t - 2, 0), wv=p1_xfree_v.get(t - 2, 0)):
                    s_a.wait(y, wa)
                    s_v.wait(y, wv)
                    s_dx.inc(y.dma_start(out=xbuf[:], in_=xtile_in_ap(b, t0)), 16)
                emit("sync", yx)
                x_ready = s_dx.v

                def vStats(v, xbuf=xbuf, x_ready=x_ready):
                    s_dx.wait(v, x_ready)
                    for s_i in range(NSUB):
                        xs = xbuf[:, s_i * D:(s_i + 1) * D]
                        v.tensor_reduce(cSsum[:, s_i:s_i + 1], xs,
                                        axis=AX.X, op=ALU.add)
                        v.tensor_tensor(cSq[s_i % 2][:], xs, xs, op=ALU.mult)
                        v.tensor_reduce(cSsq[:, s_i:s_i + 1], cSq[s_i % 2][:],
                                        axis=AX.X, op=ALU.add)
                        s_v.inc(v.engine_nop())
                    v.drain()
                    v.tensor_tensor(cTv[:], cSsum[:], cSsum[:], op=ALU.mult)
                    v.drain()
                    v.scalar_tensor_tensor(cVar[:], cTv[:], -INV_F, cSsq[:],
                                           op0=ALU.mult, op1=ALU.add)
                    v.tensor_single_scalar(cNMean[:], cSsum[:], -INV_F,
                                           op=ALU.mult)
                    v.drain()
                    v.tensor_copy(cSq[0][:], cSq[0][:])
                    s_v.inc(v.engine_nop())
                emit("vector", vStats)
                var_done = s_v.v
                p1_xfree_v[t] = var_done

                def aR(a, var_done=var_done):
                    s_v.wait(a, var_done)
                    a.activation(cLv[:], cVar[:], AF.Ln, bias=cFeps[:, 0:1])
                    a.activation(cRstd[:], cLv[:], AF.Exp,
                                 scale=cMh[:, 0:1], bias=cHlf[:, 0:1])
                    a.drain()
                    for s_i in range(NSUB):
                        last = a.activation(cNmr[:, s_i:s_i + 1],
                                            cNMean[:, s_i:s_i + 1], AF.Identity,
                                            scale=cRstd[:, s_i:s_i + 1])
                    a.drain()
                    s_a.inc(last)
                emit("scalar", aR)
                nmr_done_a = s_a.v

                for s_i in range(NSUB):
                    def aXn(a, xbuf=xbuf, s_i=s_i, ufree=p1_ufree[0]):
                        s_v.wait(a, ufree)
                        s_dx.wait(a, s_dx.v)
                        a.activation(cXn[:], xbuf[:, s_i * D:(s_i + 1) * D],
                                     AF.Identity, bias=cNmr[:, s_i:s_i + 1],
                                     scale=cRstd[:, s_i:s_i + 1])
                        s_a.inc(a.activation(
                            cXnb[:, s_i * D:(s_i + 1) * D],
                            xbuf[:, s_i * D:(s_i + 1) * D],
                            AF.Identity, bias=cNmr[:, s_i:s_i + 1],
                            scale=cRstd[:, s_i:s_i + 1]))
                    emit("scalar", aXn)
                    xn_done = s_a.v

                    def vwq(v, xn_done=xn_done, dsp=s_dsp.v):
                        s_a.wait(v, xn_done)
                        s_dsp.wait(v, dsp)
                    emit("vector", vwq)
                    q_sem, q_ready = quant_emit(
                        tmp1, [(cXn[:], cXqb[:, s_i * D:(s_i + 1) * D], D)])
                    p1_ufree[0] = q_u_done[0]

                p1_xfree_a[t] = s_a.v
                xn_all = s_a.v

                def ySpXnb(y, b=b, t0=t0, xn_all=xn_all, usettle=p1_ufree[0]):
                    s_a.wait(y, xn_all)
                    s_v.wait(y, usettle)
                    s_dx.inc(y.dma_start(
                        out=xnS.ap()[b, t0:t0 + TT, :].rearrange(
                            "(s p) d -> p s d", p=128), in_=cXnb[:]), 16)
                emit("sync", ySpXnb)

                def aSpXq(a, b=b, t0=t0, q_sem=q_sem, xq_all=q_ready):
                    q_sem.wait(a, xq_all)
                    s_dsp.inc(a.dma_start(
                        out=xqS.ap()[b, t0:t0 + TT, :].rearrange(
                            "(s p) d -> p s d", p=128), in_=cXqb[:]), 16)
                pend_xq.append(aSpXq)

                cXnT = cXnT2[t % 2]

                def yT(y, b=b, t0=t0, cXnT=cXnT,
                       wt=p1_qkv_last.get(t - 2, 0)):
                    s_dx.wait(y, s_dx.v)
                    s_t.wait(y, wt)
                    for c in range(DC):
                        s_dx.inc(y.dma_start_transpose(
                            out=cXnT[:, c * 512:(c + 1) * 512],
                            in_=xnS.ap()[b, t0:t0 + TT, c * 128:(c + 1) * 128]), 16)
                emit("sync", yT)
                p1_stageb[t] = (t0, s_dx.v, cXnT)

            def emit_stage_b(t):
                t0, xnt_ready, cXnT = p1_stageb[t]
                pend_q = []
                for oc in range(DC):
                    bq3 = [ps[(oc % 2) * 3 + i] for i in range(3)]

                    def tQKV(t_, oc=oc, bq3=bq3, xnt_ready=xnt_ready,
                             w_ready=w_ready, bf=p1_bankfree[oc % 2], cXnT=cXnT):
                        s_dx.wait(t_, xnt_ready)
                        s_dx.wait(t_, w_ready)
                        s_v.wait(t_, bf)
                        last = None
                        for pi, nm in enumerate(("q", "k", "v")):
                            for icc in range(DC):
                                last = t_.matmul(
                                    bq3[pi][:],
                                    cW[nm][:, icc * D + oc * 128:
                                           icc * D + (oc + 1) * 128],
                                    cXnT[:, icc * 512:(icc + 1) * 512],
                                    start=(icc == 0), stop=(icc == DC - 1))
                        s_t.inc(last)
                    emit("tensor", tQKV)
                    qkv_done = s_t.v
                    p1_qkv_last[t] = qkv_done

                    uq, uk, vp, tm, ee, Kc, Qc = cEv

                    def vEvac(v, oc=oc, bq3=bq3, qkv_done=qkv_done):
                        s_t.wait(v, qkv_done)
                        v.tensor_scalar(uq[:], bq3[0][:], cBq[:, oc:oc + 1], None,
                                        op0=ALU.add)
                        v.tensor_scalar(uk[:], bq3[1][:], cBk[:, oc:oc + 1], None,
                                        op0=ALU.add)
                        v.tensor_scalar(vp[:], bq3[2][:], cBv[:, oc:oc + 1], None,
                                        op0=ALU.add)
                        s_v.inc(v.engine_nop())
                        v.tensor_single_scalar(tm[:], uq[:], 0.0, op=ALU.min)
                        s_v.inc(v.engine_nop())
                    emit("vector", vEvac)
                    u_done = s_v.v
                    p1_bankfree[oc % 2] = u_done - 1

                    def aExpQ(a, u_done=u_done):
                        s_v.wait(a, u_done)
                        s_a.inc(a.activation(ee[:], tm[:], AF.Exp))
                    emit("scalar", aExpQ)
                    eq_done = s_a.v

                    Qc = cQc2[oc % 2]

                    def vQ(v, eq_done=eq_done, dsp_now=s_dsp.v, Qc=Qc):
                        s_a.wait(v, eq_done)
                        s_dsp.wait(v, dsp_now)
                        v.scalar_tensor_tensor(Qc[:], uq[:], 0.0, ee[:],
                                               op0=ALU.max, op1=ALU.add)
                        v.tensor_single_scalar(tm[:], uk[:], 0.0, op=ALU.min)
                        s_v.inc(v.engine_nop())
                    emit("vector", vQ)
                    q_done = s_v.v

                    def aSpQprev(a):
                        for (poc, pQc, pqd) in pend_q:
                            s_v.wait(a, pqd)
                            s_dsp.inc(a.dma_start(
                                out=qS.ap()[b, poc * 128:(poc + 1) * 128,
                                            t0:t0 + TT], in_=pQc[:]), 16)
                        pend_q.clear()
                    emit("scalar", aSpQprev)
                    pend_q.append((oc, Qc, q_done))

                    def aSpQ(a):
                        s_a.inc(a.activation(ee[:], tm[:], AF.Exp))
                    emit("scalar", aSpQ)
                    ek_done = s_a.v

                    def vK(v, oc=oc, t=t, ek_done=ek_done):
                        s_a.wait(v, ek_done)
                        col = t * DC + oc
                        v.scalar_tensor_tensor(Kc[:], uk[:], 0.0, ee[:],
                                               op0=ALU.max, op1=ALU.add,
                                               accum_out=cKsp[:, col:col + 1])
                        v.scalar_tensor_tensor(cScr[:], Kc[:], 1.0, vp[:],
                                               op0=ALU.mult, op1=ALU.mult,
                                               accum_out=cKvp[:, col:col + 1])
                        s_v.inc(v.engine_nop())
                    emit("vector", vK)

                def aSpQtail(a):
                    for (poc, pQc, pqd) in pend_q:
                        s_v.wait(a, pqd)
                        s_dsp.inc(a.dma_start(
                            out=qS.ap()[b, poc * 128:(poc + 1) * 128,
                                        t0:t0 + TT], in_=pQc[:]), 16)
                    pend_q.clear()
                emit("scalar", aSpQtail)

            for t in range(NT):
                emit_stage_a(t)
                if t >= 1:
                    emit_stage_b(t - 1)
            emit_stage_b(NT - 1)
            for fn in pend_xq:
                emit("scalar", fn)
            pend_xq.clear()

            def vKvFin(v):
                v.drain()
                v.tensor_reduce(cKv8[:],
                                cKvp[:].rearrange("p (t c) -> p c t", c=DC),
                                axis=AX.X, op=ALU.add)
                v.tensor_reduce(cKs8[:],
                                cKsp[:].rearrange("p (t c) -> p c t", c=DC),
                                axis=AX.X, op=ALU.add)
                v.drain()
                s_v.inc(v.engine_nop())
            emit("vector", vKvFin)

        barrier()

        if "p2a" not in stages:
            continue
        with contextlib.ExitStack() as p2:
            cWg = p2.enter_context(nc.sbuf_tensor(
                f"p2wg_{b}", [128, DC * DI], BF16, is_transpose_dest=True))
            cWu = p2.enter_context(nc.sbuf_tensor(
                f"p2wu_{b}", [128, DC * DI], BF16, is_transpose_dest=True))
            cXqT = p2.enter_context(nc.sbuf_tensor(
                f"p2xqt_{b}", [128, DC * 512], BF16, is_transpose_dest=True))
            cH = p2.enter_context(nc.sbuf_tensor(f"p2h_{b}", [128, DI], F32))
            cSl = p2.enter_context(nc.sbuf_tensor(f"p2sl_{b}", [128, 512], F32))
            cHq = p2.enter_context(nc.sbuf_tensor(f"p2hq_{b}", [128, DI], BF16))
            tmp2 = QTmp(p2, nc, f"q2_{b}", DI)

            def ywg(y):
                for c in range(DC):
                    s_dx.inc(y.dma_start_transpose(
                        out=cWg[:, c * DI:(c + 1) * DI],
                        in_=wgN.ap()[:, c * 128:(c + 1) * 128]), 16)
                    s_dx.inc(y.dma_start_transpose(
                        out=cWu[:, c * DI:(c + 1) * DI],
                        in_=wuN.ap()[:, c * 128:(c + 1) * 128]), 16)
            emit("sync", ywg)
            wgu_ready = s_dx.v

            p2_bankfree = {0: 0, 1: 0}
            p2_slfree = [0]
            for t in range(NT):
                t0 = t * TT

                def yxq(y, b=b, t0=t0):
                    s_t.wait(y, s_t.v)
                    s_g.wait(y, s_g.v)
                    for c in range(DC):
                        s_dx.inc(y.dma_start_transpose(
                            out=cXqT[:, c * 512:(c + 1) * 512],
                            in_=xqS.ap()[b, t0:t0 + TT, c * 128:(c + 1) * 128]), 16)
                emit("sync", yxq)
                xqt_ready = s_dx.v

                for s_i in range(NSUB):
                    for io, (o0, o1) in enumerate(OSL):
                        bG = ps[(io % 2) * 2 + 0]
                        bU = ps[(io % 2) * 2 + 1]
                        ow = o1 - o0

                        def tGU(t_, s_i=s_i, o0=o0, o1=o1, ow=ow, bG=bG, bU=bU,
                                xqt_ready=xqt_ready, wgu_ready=wgu_ready,
                                bf=p2_bankfree[io % 2]):
                            s_dx.wait(t_, xqt_ready)
                            s_dx.wait(t_, wgu_ready)
                            s_v.wait(t_, bf)
                            last = None
                            for icc in range(DC):
                                st_ap = cXqT[:, icc * 512 + s_i * 128:
                                             icc * 512 + (s_i + 1) * 128]
                                t_.matmul(bG[:, 0:ow], st_ap,
                                          cWg[:, icc * DI + o0:icc * DI + o1],
                                          start=(icc == 0), stop=(icc == DC - 1))
                                last = t_.matmul(
                                    bU[:, 0:ow], st_ap,
                                    cWu[:, icc * DI + o0:icc * DI + o1],
                                    start=(icc == 0), stop=(icc == DC - 1))
                            s_t.inc(last)
                        emit("tensor", tGU)
                        gu_done = s_t.v

                        def aSilu(a, bG=bG, ow=ow, gu_done=gu_done,
                                  slf=p2_slfree[0]):
                            s_t.wait(a, gu_done)
                            s_v.wait(a, slf)
                            s_a.inc(a.activation(cSl[:, 0:ow], bG[:, 0:ow],
                                                 AF.Silu))
                        emit("scalar", aSilu)
                        sl_done = s_a.v

                        def vH(v, bU=bU, o0=o0, o1=o1, ow=ow, sl_done=sl_done):
                            s_a.wait(v, sl_done)
                            v.tensor_tensor(cH[:, o0:o1], cSl[:, 0:ow],
                                            bU[:, 0:ow], op=ALU.mult)
                            s_v.inc(v.engine_nop())
                        emit("vector", vH)
                        p2_bankfree[io % 2] = s_v.v
                        p2_slfree[0] = s_v.v

                    prev_hq_spill = s_dsp.v

                    def vwh(v, prev_hq_spill=prev_hq_spill):
                        s_dsp.wait(v, prev_hq_spill)
                    emit("vector", vwh)
                    hq_sem, hq_ready = quant_emit(
                        tmp2, [(cH[:, a0:a1], cHq[:, a0:a1], a1 - a0)
                               for (a0, a1) in QT])

                    def aSpHq(a, b=b, t0=t0, s_i=s_i, hq_sem=hq_sem,
                              hq_ready=hq_ready):
                        hq_sem.wait(a, hq_ready)
                        row0 = t0 + s_i * 128
                        s_dsp.inc(a.dma_start(
                            out=hqS.ap()[b, row0:row0 + 128, :], in_=cHq[:]), 16)
                    emit("scalar", aSpHq)

        barrier()

        if "p2b" not in stages:
            continue
        with contextlib.ExitStack() as p3:
            cWd = p3.enter_context(nc.sbuf_tensor(
                f"p3wd_{b}", [128, IC * D], BF16, is_transpose_dest=True))
            cWo = p3.enter_context(nc.sbuf_tensor(f"p3wo_{b}", [128, DC * D], BF16))
            cHqT = p3.enter_context(nc.sbuf_tensor(
                f"p3hqt_{b}", [128, IC * 512], BF16, is_transpose_dest=True))
            cQ = p3.enter_context(nc.sbuf_tensor(f"p3q_{b}", [128, DC * 512], F32))
            cA = p3.enter_context(nc.sbuf_tensor(f"p3a_{b}", [128, DC * 512], BF16))
            cZa = p3.enter_context(nc.sbuf_tensor(f"p3za_{b}", [128, 512], F32))
            cRz = p3.enter_context(nc.sbuf_tensor(f"p3rz_{b}", [128, DC * 512], F32))
            cGi = p3.enter_context(nc.sbuf_tensor(f"p3gi_{b}", [128, 512], F32))
            cGo = p3.enter_context(nc.sbuf_tensor(f"p3go_{b}", [128, 512], F32))
            cOut = p3.enter_context(nc.sbuf_tensor(f"p3out_{b}", [128, NSUB * D],
                                                   F32))
            cXr = p3.enter_context(nc.sbuf_tensor(f"p3xr_{b}", [128, NSUB * D],
                                                  F32))

            def ywd(y):
                for c in range(IC):
                    s_dx.inc(y.dma_start_transpose(
                        out=cWd[:, c * D:(c + 1) * D],
                        in_=wdN.ap()[:, c * 128:(c + 1) * 128]), 16)
                for c in range(DC):
                    s_dx.inc(y.dma_start(
                        out=cWo[:, c * D:(c + 1) * D],
                        in_=woB.ap()[c * 128:(c + 1) * 128, :]), 16)
            emit("sync", ywd)
            wdo_ready = s_dx.v

            p3_bankfree = {0: 0, 1: 0}
            for t in range(NT):
                t0 = t * TT

                def yld(y, b=b, t0=t0):
                    s_v.wait(y, s_v.v)
                    s_t.wait(y, s_t.v)
                    for c in range(IC):
                        s_dx.inc(y.dma_start_transpose(
                            out=cHqT[:, c * 512:(c + 1) * 512],
                            in_=hqS.ap()[b, t0:t0 + TT, c * 128:(c + 1) * 128]), 16)
                    for c in range(DC):
                        s_dx.inc(y.dma_start(
                            out=cQ[:, c * 512:(c + 1) * 512],
                            in_=qS.ap()[b, c * 128:(c + 1) * 128, t0:t0 + TT]), 16)
                    s_dx.inc(y.dma_start(out=cXr[:], in_=xtile_in_ap(b, t0)), 16)
                emit("sync", yld)
                ld_ready = s_dx.v

                prev_out_spill = s_dsp.v

                def vA(v, ld_ready=ld_ready, prev_out_spill=prev_out_spill):
                    s_dx.wait(v, ld_ready)
                    s_dsp.wait(v, prev_out_spill)
                    for c in range(DC):
                        v.tensor_scalar(cZa[:], cQ[:, c * 512:(c + 1) * 512],
                                        cKs8[:, c:c + 1], 1e-6,
                                        op0=ALU.mult, op1=ALU.add)
                        v.reciprocal(cRz[:, c * 512:(c + 1) * 512], cZa[:])
                    v.drain()
                    for c in range(DC):
                        v.scalar_tensor_tensor(
                            cA[:, c * 512:(c + 1) * 512],
                            cQ[:, c * 512:(c + 1) * 512], cKv8[:, c:c + 1],
                            cRz[:, c * 512:(c + 1) * 512],
                            op0=ALU.mult, op1=ALU.mult)
                    s_v.inc(v.engine_nop())
                emit("vector", vA)
                a_ready = s_v.v

                for s_i in range(NSUB):
                    par = s_i % 2
                    bD = [ps[par * 4 + 0], ps[par * 4 + 1]]
                    bW = [ps[par * 4 + 2], ps[par * 4 + 3]]

                    def tDW(t_, s_i=s_i, bD=bD, bW=bW, ld_ready=ld_ready,
                            a_ready=a_ready, wdo_ready=wdo_ready):
                        s_dx.wait(t_, ld_ready)
                        s_dx.wait(t_, wdo_ready)
                        s_v.wait(t_, s_v.v)
                        for icc in range(IC):
                            st_ap = cHqT[:, icc * 512 + s_i * 128:
                                         icc * 512 + (s_i + 1) * 128]
                            t_.matmul(bD[0][:], st_ap,
                                      cWd[:, icc * D:icc * D + 512],
                                      start=(icc == 0), stop=(icc == IC - 1))
                            t_.matmul(bD[1][:], st_ap,
                                      cWd[:, icc * D + 512:(icc + 1) * D],
                                      start=(icc == 0), stop=(icc == IC - 1))
                        last = None
                        for c in range(DC):
                            st_ap = cA[:, c * 512 + s_i * 128:
                                       c * 512 + (s_i + 1) * 128]
                            t_.matmul(bW[0][:], st_ap, cWo[:, c * D:c * D + 512],
                                      start=(c == 0), stop=(c == DC - 1))
                            last = t_.matmul(bW[1][:], st_ap,
                                             cWo[:, c * D + 512:(c + 1) * D],
                                             start=(c == 0), stop=(c == DC - 1))
                        s_t.inc(last)
                    emit("tensor", tDW)
                    dw_done = s_t.v

                    for half in range(2):
                        def vGi(v, half=half, bW=bW, dw_done=dw_done):
                            s_t.wait(v, dw_done)
                            v.tensor_tensor(cGi[:], bW[half][:],
                                            cBoR[:, half * 512:(half + 1) * 512],
                                            op=ALU.add)
                            s_v.inc(v.engine_nop())
                        emit("vector", vGi)
                        gi_done = s_v.v

                        def aG(a, gi_done=gi_done):
                            s_v.wait(a, gi_done)
                            s_a.inc(a.activation(cGo[:], cGi[:],
                                                 AF.Gelu_apprx_tanh))
                        emit("scalar", aG)
                        g_done = s_a.v

                        def vOut(v, s_i=s_i, half=half, bD=bD, g_done=g_done):
                            s_a.wait(v, g_done)
                            off = s_i * D + half * 512
                            v.tensor_tensor(cGi[:], cXr[:, off:off + 512],
                                            cGo[:], op=ALU.add)
                            v.tensor_tensor(cOut[:, off:off + 512], cGi[:],
                                            bD[half][:], op=ALU.add)
                            s_v.inc(v.engine_nop())
                        emit("vector", vOut)

                out_ready = s_v.v

                def aOut(a, b=b, t0=t0, out_ready=out_ready):
                    s_v.wait(a, out_ready)
                    s_dsp.inc(a.dma_start(out=xtile_out_ap(b, t0), in_=cOut[:]), 16)
                emit("scalar", aOut)

        barrier()

    emit("gpsimd", lambda g: s_dsp.wait(g, s_dsp.v))
    emit("scalar", lambda a: s_dsp.wait(a, s_dsp.v))

    es.close()
    return nc


_NC_CACHE = {}


def kernel(L0_attn_x, L0_attn_Wq, L0_attn_bq, L0_attn_Wk, L0_attn_bk,
           L0_attn_Wv, L0_attn_bv, L0_attn_Wo, L0_attn_bo,
           L0_ffn_w_gate, L0_ffn_w_up, L0_ffn_w_down):
    x = np.ascontiguousarray(np.asarray(L0_attn_x, dtype=np.float32))
    assert x.shape == (32, 2048, 1024)
    weights = {
        "wq": np.ascontiguousarray(np.asarray(L0_attn_Wq, np.float32)),
        "wk": np.ascontiguousarray(np.asarray(L0_attn_Wk, np.float32)),
        "wv": np.ascontiguousarray(np.asarray(L0_attn_Wv, np.float32)),
        "wo": np.ascontiguousarray(np.asarray(L0_attn_Wo, np.float32)),
        "bq": np.ascontiguousarray(np.asarray(L0_attn_bq, np.float32)),
        "bk": np.ascontiguousarray(np.asarray(L0_attn_bk, np.float32)),
        "bv": np.ascontiguousarray(np.asarray(L0_attn_bv, np.float32)),
        "bo": np.ascontiguousarray(np.asarray(L0_attn_bo, np.float32)),
        "wg": np.ascontiguousarray(np.asarray(L0_ffn_w_gate, np.float32)),
        "wu": np.ascontiguousarray(np.asarray(L0_ffn_w_up, np.float32)),
        "wd": np.ascontiguousarray(np.asarray(L0_ffn_w_down, np.float32)),
    }
    if "nc" not in _NC_CACHE:
        _NC_CACHE["nc"] = build_program()
    nc = _NC_CACHE["nc"]
    in_maps = []
    for core in range(NCORES):
        m = dict(weights)
        m["x"] = x[core * NB:(core + 1) * NB]
        in_maps.append(m)
    res = run_bass_kernel_spmd(nc, in_maps, core_ids=list(range(NCORES)))
    out = np.concatenate([np.asarray(r["out"]) for r in res.results], axis=0)
    return out.astype(np.float32)


# version 22 (final)
# speedup vs baseline: 2.5503x; 1.0322x over previous
import contextlib
import numpy as np

import concourse.bass as bass
import concourse.mybir as mybir
from concourse.bass_utils import run_bass_kernel_spmd

F32 = mybir.dt.float32


def _preload_set(a, nc, name):
    from concourse.hw_specs import get_activation_tables
    names = list(get_activation_tables(nc.m.arch).keys())
    a.add_instruction(mybir.InstLoadActFuncSet(
        name=nc.get_next_instruction_name(),
        act_func_set_id=names.index(name), ins=[], outs=[]))

BF16 = mybir.dt.bfloat16
AF = mybir.ActivationFunctionType
ALU = mybir.AluOpType
AX = mybir.AxisListType

NCORES = 8
NB = 4
S = 2048
D = 1024
DI = 2816
TT = 512
NT = S // TT
NSUB = TT // 128
DC = D // 128
IC = DI // 128
C12 = 1.0 / 12.0
INV_F = 1.0 / float(D)
OSL = [(0, 512), (512, 1024), (1024, 1536), (1536, 2048), (2048, 2560), (2560, 2816)]
QT = [(0, 1408), (1408, 2816)]


class Ctr:
    def __init__(self, es, nc, name):
        self.h = es.enter_context(nc.semaphore(name))
        self.v = 0

    def inc(self, inst, k=1):
        inst.then_inc(self.h, k)
        self.v += k
        return self.v

    def wait(self, eng, v=None):
        eng.wait_ge(self.h, self.v if v is None else v)


class QTmp:

    def __init__(self, scope, nc, tag, W):
        self.W = W
        sb = lambda n, s, d: scope.enter_context(nc.sbuf_tensor(n, s, d))
        self.u = sb(f"{tag}_u", [128, W], F32)
        self.au = sb(f"{tag}_au", [128, W], F32)
        self.sg = sb(f"{tag}_sg", [128, W], F32)
        self.aA = sb(f"{tag}_aA", [128, W], F32)
        self.aB = sb(f"{tag}_aB", [128, W], F32)
        G = W // 16
        self.gm = sb(f"{tag}_gm", [128, G], F32)
        self.sc = sb(f"{tag}_sc", [128, G], F32)
        self.r = sb(f"{tag}_r", [128, G], F32)
        self.r6 = sb(f"{tag}_r6", [128, G], F32)


def build_program(debug=False, stages=("cast", "prep", "p1", "p2a", "p2b"), nb=NB, p1_quant=True, p1_xpose=True, p1_qkv=True, p1_ev=4):
    nc = bass.Bass("TRN2", target_bir_lowering=False)

    x_h = nc.dram_tensor("x", [NB, S, D], F32, kind="ExternalInput")
    wq_h = nc.dram_tensor("wq", [D, D], F32, kind="ExternalInput")
    wk_h = nc.dram_tensor("wk", [D, D], F32, kind="ExternalInput")
    wv_h = nc.dram_tensor("wv", [D, D], F32, kind="ExternalInput")
    wo_h = nc.dram_tensor("wo", [D, D], F32, kind="ExternalInput")
    bq_h = nc.dram_tensor("bq", [D], F32, kind="ExternalInput")
    bk_h = nc.dram_tensor("bk", [D], F32, kind="ExternalInput")
    bv_h = nc.dram_tensor("bv", [D], F32, kind="ExternalInput")
    bo_h = nc.dram_tensor("bo", [D], F32, kind="ExternalInput")
    wg_h = nc.dram_tensor("wg", [DI, D], F32, kind="ExternalInput")
    wu_h = nc.dram_tensor("wu", [DI, D], F32, kind="ExternalInput")
    wd_h = nc.dram_tensor("wd", [D, DI], F32, kind="ExternalInput")
    out_h = nc.dram_tensor("out", [NB, S, D], F32, kind="ExternalOutput")

    kscr = "ExternalOutput" if debug else "Internal"
    wgN = nc.dram_tensor("wgN", [DI, D], BF16, kind=kscr)
    wuN = nc.dram_tensor("wuN", [DI, D], BF16, kind=kscr)
    wdN = nc.dram_tensor("wdN", [D, DI], BF16, kind=kscr)
    wqB = nc.dram_tensor("wqB", [D, D], BF16, kind="Internal")
    wkB = nc.dram_tensor("wkB", [D, D], BF16, kind="Internal")
    wvB = nc.dram_tensor("wvB", [D, D], BF16, kind="Internal")
    woB = nc.dram_tensor("woB", [D, D], BF16, kind="Internal")
    xnS = nc.dram_tensor("xnS", [NB, S, D], BF16, kind=kscr)
    xqS = nc.dram_tensor("xqS", [NB, S, D], BF16, kind=kscr)
    hqS = nc.dram_tensor("hqS", [NB, S, DI], BF16, kind=kscr)
    qS = nc.dram_tensor("qS", [NB, D, S], F32, kind=kscr)

    es = contextlib.ExitStack()

    def sb(name, shape, dtype=F32, **kw):
        return es.enter_context(nc.sbuf_tensor(name, shape, dtype, **kw))

    cXn = sb("xn", [128, D])
    cXnb = sb("xnb", [128, NSUB * D], BF16)
    cXqb = sb("xqb", [128, NSUB * D], BF16)
    cSq = [sb("sq0", [128, D]), sb("sq1", [128, D])]
    cSsum = sb("ssum", [128, NSUB])
    cSsq = sb("ssq", [128, NSUB])
    cTv = sb("tv", [128, NSUB])
    cVar = sb("var", [128, NSUB])
    cLv = sb("lv", [128, NSUB])
    cRstd = sb("rstd", [128, NSUB])
    cNmr = sb("nmr", [128, NSUB])
    cNMean = sb("nmean", [128, NSUB])
    cKvp = sb("kvp", [128, NT * DC])
    cKsp = sb("ksp", [128, NT * DC])
    cKv8 = sb("kv8", [128, DC])
    cKs8 = sb("ks8", [128, DC])
    cBq = sb("bq_c", [128, DC])
    cBk = sb("bk_c", [128, DC])
    cBv = sb("bv_c", [128, DC])
    cBoR = sb("boR", [128, D])
    cFeps = sb("cfeps", [128, 1])
    cMh = sb("cmh", [128, 1])
    cHlf = sb("chlf", [128, 1])
    ps = [es.enter_context(nc.psum_tensor(f"ps{i}", [128, 512], F32)) for i in range(8)]

    s_dx = Ctr(es, nc, "dx")
    s_dsp = Ctr(es, nc, "dsp")
    s_dg = Ctr(es, nc, "dg")
    s_v = Ctr(es, nc, "sv")
    s_a = Ctr(es, nc, "sa")
    s_t = Ctr(es, nc, "st")
    s_g = Ctr(es, nc, "sg")
    s_bar = [Ctr(es, nc, f"bar{i}") for i in range(5)]

    blk = es.enter_context(nc.Block())
    ENG = ["vector", "scalar", "tensor", "gpsimd", "sync"]

    def emit(eng, fn):
        getattr(blk, eng)(fn)

    def barrier():
        for i, e in enumerate(ENG):
            def bump(g, i=i, e=e):
                if e in ("vector", "gpsimd"):
                    ins = g.engine_nop()
                elif e in ("tensor", "scalar"):
                    ins = g.drain()
                else:
                    ins = g.nop()
                s_bar[i].inc(ins)
            emit(e, bump)
        targets = [c.v for c in s_bar]
        for i, e in enumerate(ENG):
            def w(g, i=i):
                for j in range(5):
                    if j != i:
                        s_bar[j].wait(g, targets[j])
            emit(e, w)

    q_u_done = [0]

    def quant_emit(tmp, jobs):
        offs = []
        o = 0
        for (_, _, w) in jobs:
            assert w % 16 == 0
            offs.append(o)
            o += w
        assert o <= tmp.W

        g_prev = s_g.v

        def vfn(v):
            s_g.wait(v, g_prev)
            for (src, _, w), off in zip(jobs, offs):
                g0 = off // 16
                gc = w // 16
                v.tensor_reduce(tmp.gm[:, g0:g0 + gc],
                                src.rearrange("p (g e) -> p g e", e=16),
                                axis=AX.X, op=ALU.max, apply_absolute_value=True)
            v.drain()
            for (src, _, w), off in zip(jobs, offs):
                g0 = off // 16
                gc = w // 16
                gsl = tmp.gm[:, g0:g0 + gc]
                v.tensor_scalar(tmp.sc[:, g0:g0 + gc], gsl, 6e-12, C12,
                                op0=ALU.max, op1=ALU.mult)
                v.reciprocal(tmp.r[:, g0:g0 + gc], gsl)
            v.drain()
            for (_, _, w), off in zip(jobs, offs):
                g0 = off // 16
                gc = w // 16
                v.tensor_scalar(tmp.r6[:, g0:g0 + gc], tmp.r[:, g0:g0 + gc],
                                6.0, 1e12, op0=ALU.mult, op1=ALU.min)
            v.drain()
            for (src, _, w), off in zip(jobs, offs):
                g0 = off // 16
                gc = w // 16
                v.tensor_tensor(tmp.u[:, off:off + w].rearrange(
                                    "p (g e) -> p g e", e=16),
                                src.rearrange("p (g e) -> p g e", e=16),
                                tmp.r6[:, g0:g0 + gc].unsqueeze(2)
                                .broadcast_to([128, gc, 16]), op=ALU.mult)
            s_v.inc(v.engine_nop())
        emit("vector", vfn)
        u_done = s_v.v
        q_u_done[0] = u_done

        def vfn2(v):
            for (_, _, w), off in zip(jobs, offs):
                usl = tmp.u[:, off:off + w]
                v.tensor_scalar(tmp.sg[:, off:off + w], usl, 0.0, 2.0,
                                op0=ALU.is_gt, op1=ALU.mult)
            for (_, _, w), off in zip(jobs, offs):
                sgs = tmp.sg[:, off:off + w]
                v.tensor_single_scalar(sgs, sgs, 1.0, op=ALU.subtract)
            for (_, _, w), off in zip(jobs, offs):
                v.tensor_tensor(tmp.au[:, off:off + w], tmp.u[:, off:off + w],
                                tmp.sg[:, off:off + w], op=ALU.mult)
            for (_, _, w), off in zip(jobs, offs):
                au = tmp.au[:, off:off + w]
                aa = tmp.aA[:, off:off + w]
                ab = tmp.aB[:, off:off + w]
                v.tensor_single_scalar(aa, au, 0.25, op=ALU.is_gt)
                for th in (0.75, 1.25, 1.75):
                    v.scalar_tensor_tensor(aa, au, th, aa, op0=ALU.is_gt, op1=ALU.add)
                v.tensor_single_scalar(ab, au, 2.5, op=ALU.is_gt)
                for th in (3.5, 5.0, 5.0):
                    v.scalar_tensor_tensor(ab, au, th, ab, op0=ALU.is_gt, op1=ALU.add)
            for (_, _, w), off in zip(jobs, offs):
                aa = tmp.aA[:, off:off + w]
                ab = tmp.aB[:, off:off + w]
                v.scalar_tensor_tensor(aa, ab, 2.0, aa, op0=ALU.mult, op1=ALU.add)
            s_v.inc(v.engine_nop())
        emit("vector", vfn2)

        def vfn3(v):
            for (_, dst, w), off in zip(jobs, offs):
                g0 = off // 16
                gc = w // 16
                aa = tmp.aA[:, off:off + w]
                usl = tmp.u[:, off:off + w]
                v.tensor_tensor(usl, aa, tmp.sg[:, off:off + w], op=ALU.mult)
                v.tensor_tensor(dst.rearrange("p (g e) -> p g e", e=16),
                                usl.rearrange("p (g e) -> p g e", e=16),
                                tmp.sc[:, g0:g0 + gc].unsqueeze(2)
                                .broadcast_to([128, gc, 16]), op=ALU.mult)
            v.drain()
            s_v.inc(v.engine_nop())
        emit("vector", vfn3)
        return (s_v, s_v.v)

    def xtile_in_ap(b, t0):
        return x_h.ap()[b, t0:t0 + TT, :].rearrange("(s p) d -> p s d", p=128)

    def xtile_out_ap(b, t0):
        return out_h.ap()[b, t0:t0 + TT, :].rearrange("(s p) d -> p s d", p=128)

    def init_consts(v):
        v.memset(cFeps[:], float(D) * float(np.float32(1e-5)))
        v.memset(cMh[:], -0.5)
        v.memset(cHlf[:], float(np.float32(0.5 * np.log(float(D)))))
        v.memset(cKvp[:], 0.0)
        v.memset(cKsp[:], 0.0)
        s_v.inc(v.engine_nop())
    emit("vector", init_consts)

    def load_biases(y):
        for src, dst in ((bq_h, cBq), (bk_h, cBk), (bv_h, cBv)):
            s_dx.inc(y.dma_start(out=dst[:], in_=src.ap().rearrange(
                "(c p) -> p c", p=128), allow_slow_non_contiguous=True), 16)
        s_dx.inc(y.dma_start(
            out=cBoR[:], in_=bo_h.ap().unsqueeze(0).partition_broadcast(128)), 16)
    emit("sync", load_biases)
    biases_ready = s_dx.v

    with contextlib.ExitStack() as pp:
        stg = [pp.enter_context(nc.sbuf_tensor(f"wstg{i}", [128, D], BF16))
               for i in range(2)]
        prev_store = 0
        for wi, (src, dst) in enumerate((((wq_h, wqB), (wk_h, wkB),
                                         (wv_h, wvB), (wo_h, woB))
                                         if "cast" in stages else ())):
            for c in range(DC):
                buf = stg[(wi * DC + c) % 2]

                def gload(g, src=src, c=c, buf=buf, prev_store=prev_store):
                    s_dsp.wait(g, prev_store)
                    s_dg.inc(g.dma_start(out=buf[:],
                                         in_=src.ap()[c * 128:(c + 1) * 128, :]), 16)
                emit("gpsimd", gload)
                ld_done = s_dg.v

                def sstore(a, dst=dst, c=c, buf=buf, ld_done=ld_done):
                    s_dg.wait(a, ld_done)
                    s_dsp.inc(a.dma_start(out=dst.ap()[c * 128:(c + 1) * 128, :],
                                          in_=buf[:]), 16)
                emit("scalar", sstore)
                if wi * DC + c >= 1:
                    prev_store = s_dsp.v - 16

        tmpP = QTmp(pp, nc, "qp", DI)
        wfin = [pp.enter_context(nc.sbuf_tensor(f"wfin{i}", [128, DI], F32))
                for i in range(2)]
        wfout = [pp.enter_context(nc.sbuf_tensor(f"wfout{i}", [128, DI], BF16))
                 for i in range(2)]
        jobs = [(wg_h, wgN, r, D) for r in range(IC)]
        jobs += [(wu_h, wuN, r, D) for r in range(IC)]
        jobs += [(wd_h, wdN, r, DI) for r in range(DC)]
        if "prep" not in stages:
            jobs = []
        uprev = [0, 0]
        stprev = [0, 0]
        for ji, (src, dst, r, K) in enumerate(jobs):
            par = ji % 2
            fin, fout = wfin[par], wfout[par]

            def yload(y, src=src, r=r, K=K, up=uprev[par], sp=stprev[par]):
                s_v.wait(y, up)
                s_dsp.wait(y, sp)
                s_dx.inc(y.dma_start(out=fin[:, 0:K],
                                     in_=src.ap()[r * 128:(r + 1) * 128, :]), 16)
            emit("sync", yload)
            ld_done = s_dx.v

            emit("vector", lambda v, ld_done=ld_done: s_dx.wait(v, ld_done))
            if K == D:
                rsem, ready = quant_emit(tmpP, [(fin[:, 0:D], fout[:, 0:D], D)])
            else:
                rsem, ready = quant_emit(
                    tmpP, [(fin[:, a0:a1], fout[:, a0:a1], a1 - a0)
                           for (a0, a1) in QT])
            uprev[par] = q_u_done[0]

            def sstore(a, dst=dst, r=r, K=K, rsem=rsem, ready=ready, fout=fout):
                rsem.wait(a, ready)
                s_dsp.inc(a.dma_start(out=dst.ap()[r * 128:(r + 1) * 128, :],
                                      in_=fout[:, 0:K]), 16)
            emit("scalar", sstore)
            stprev[par] = s_dsp.v
        barrier()

    for b in range(nb if ("p1" in stages) else 0):
        if b > 0:
            def vreset(v):
                v.memset(cKvp[:], 0.0)
                v.memset(cKsp[:], 0.0)
                s_v.inc(v.engine_nop())
            emit("vector", vreset)

        with contextlib.ExitStack() as p1:
            cXt = [p1.enter_context(nc.sbuf_tensor(f"p1xt{i}_{b}", [128, NSUB * D],
                                                   F32)) for i in range(2)]
            cW = {nm: p1.enter_context(nc.sbuf_tensor(f"p1w{nm}_{b}",
                                                      [128, DC * D], BF16))
                  for nm in ("q", "k", "v")}
            cXnT2 = [p1.enter_context(nc.sbuf_tensor(
                f"p1xnt{i}_{b}", [128, DC * 512], BF16, is_transpose_dest=True))
                for i in range(2)]
            cEv = [p1.enter_context(nc.sbuf_tensor(f"p1ev{i}_{b}", [128, 512], F32))
                   for i in range(7)]
            cQc2 = [p1.enter_context(nc.sbuf_tensor(f"p1qc{i}_{b}", [128, 512],
                                                    F32)) for i in range(2)]
            cScr = p1.enter_context(nc.sbuf_tensor(f"p1scr_{b}", [128, 512], F32))
            tmp1 = QTmp(p1, nc, f"q1_{b}", D)

            def ywl(y):
                for nm, srcB in (("q", wqB), ("k", wkB), ("v", wvB)):
                    for c in range(DC):
                        s_dx.inc(y.dma_start(
                            out=cW[nm][:, c * D:(c + 1) * D],
                            in_=srcB.ap()[c * 128:(c + 1) * 128, :]), 16)
            emit("sync", ywl)
            w_ready = s_dx.v
            if b == 0:
                emit("vector", lambda v: s_dx.wait(v, biases_ready))

            p1_bankfree = {0: 0, 1: 0}
            p1_ufree = [0]
            p1_xfree_a = {}
            p1_xfree_v = {}
            p1_qkv_last = {}
            p1_stageb = {}
            pend_xq = []

            def emit_stage_a(t):
                t0 = t * TT
                xbuf = cXt[t % 2]
                for fn in pend_xq:
                    emit("scalar", fn)
                pend_xq.clear()

                def yx(y, b=b, t0=t0, xbuf=xbuf,
                       wa=p1_xfree_a.get(t - 2, 0), wv=p1_xfree_v.get(t - 2, 0)):
                    s_a.wait(y, wa)
                    s_v.wait(y, wv)
                    s_dx.inc(y.dma_start(out=xbuf[:], in_=xtile_in_ap(b, t0)), 16)
                emit("sync", yx)
                x_ready = s_dx.v

                def vStats(v, xbuf=xbuf, x_ready=x_ready):
                    s_dx.wait(v, x_ready)
                    for s_i in range(NSUB):
                        xs = xbuf[:, s_i * D:(s_i + 1) * D]
                        v.tensor_reduce(cSsum[:, s_i:s_i + 1], xs,
                                        axis=AX.X, op=ALU.add)
                        v.tensor_tensor(cSq[s_i % 2][:], xs, xs, op=ALU.mult)
                        v.tensor_reduce(cSsq[:, s_i:s_i + 1], cSq[s_i % 2][:],
                                        axis=AX.X, op=ALU.add)
                        s_v.inc(v.engine_nop())
                    v.drain()
                    v.tensor_tensor(cTv[:], cSsum[:], cSsum[:], op=ALU.mult)
                    v.drain()
                    v.scalar_tensor_tensor(cVar[:], cTv[:], -INV_F, cSsq[:],
                                           op0=ALU.mult, op1=ALU.add)
                    v.tensor_single_scalar(cNMean[:], cSsum[:], -INV_F,
                                           op=ALU.mult)
                    v.drain()
                    v.tensor_copy(cSq[0][:], cSq[0][:])
                    s_v.inc(v.engine_nop())
                emit("vector", vStats)
                var_done = s_v.v
                p1_xfree_v[t] = var_done

                def aR(a, var_done=var_done):
                    s_v.wait(a, var_done)
                    a.activation(cLv[:], cVar[:], AF.Ln, bias=cFeps[:, 0:1])
                    a.activation(cRstd[:], cLv[:], AF.Exp,
                                 scale=cMh[:, 0:1], bias=cHlf[:, 0:1])
                    a.drain()
                    for s_i in range(NSUB):
                        last = a.activation(cNmr[:, s_i:s_i + 1],
                                            cNMean[:, s_i:s_i + 1], AF.Identity,
                                            scale=cRstd[:, s_i:s_i + 1])
                    a.drain()
                    s_a.inc(last)
                emit("scalar", aR)
                nmr_done_a = s_a.v

                for s_i in range(NSUB):
                    def aXn(a, xbuf=xbuf, s_i=s_i, ufree=p1_ufree[0]):
                        s_v.wait(a, ufree)
                        s_dx.wait(a, s_dx.v)
                        a.activation(cXn[:], xbuf[:, s_i * D:(s_i + 1) * D],
                                     AF.Identity, bias=cNmr[:, s_i:s_i + 1],
                                     scale=cRstd[:, s_i:s_i + 1])
                        s_a.inc(a.activation(
                            cXnb[:, s_i * D:(s_i + 1) * D],
                            xbuf[:, s_i * D:(s_i + 1) * D],
                            AF.Identity, bias=cNmr[:, s_i:s_i + 1],
                            scale=cRstd[:, s_i:s_i + 1]))
                    emit("scalar", aXn)
                    xn_done = s_a.v

                    def vwq(v, xn_done=xn_done, dsp=s_dsp.v):
                        s_a.wait(v, xn_done)
                        s_dsp.wait(v, dsp)
                    emit("vector", vwq)
                    q_sem, q_ready = quant_emit(
                        tmp1, [(cXn[:], cXqb[:, s_i * D:(s_i + 1) * D], D)])
                    p1_ufree[0] = q_u_done[0]

                p1_xfree_a[t] = s_a.v
                xn_all = s_a.v

                def ySpXnb(y, b=b, t0=t0, xn_all=xn_all, usettle=p1_ufree[0]):
                    s_a.wait(y, xn_all)
                    s_v.wait(y, usettle)
                    s_dx.inc(y.dma_start(
                        out=xnS.ap()[b, t0:t0 + TT, :].rearrange(
                            "(s p) d -> p s d", p=128), in_=cXnb[:]), 16)
                emit("sync", ySpXnb)

                def aSpXq(a, b=b, t0=t0, q_sem=q_sem, xq_all=q_ready):
                    q_sem.wait(a, xq_all)
                    s_dsp.inc(a.dma_start(
                        out=xqS.ap()[b, t0:t0 + TT, :].rearrange(
                            "(s p) d -> p s d", p=128), in_=cXqb[:]), 16)
                pend_xq.append(aSpXq)

                cXnT = cXnT2[t % 2]

                def yT(y, b=b, t0=t0, cXnT=cXnT,
                       wt=p1_qkv_last.get(t - 2, 0)):
                    s_dx.wait(y, s_dx.v)
                    s_t.wait(y, wt)
                    for c in range(DC):
                        s_dx.inc(y.dma_start_transpose(
                            out=cXnT[:, c * 512:(c + 1) * 512],
                            in_=xnS.ap()[b, t0:t0 + TT, c * 128:(c + 1) * 128]), 16)
                emit("sync", yT)
                p1_stageb[t] = (t0, s_dx.v, cXnT)

            def emit_stage_b(t):
                t0, xnt_ready, cXnT = p1_stageb[t]
                pend_q = []
                for oc in range(DC):
                    bq3 = [ps[(oc % 2) * 3 + i] for i in range(3)]

                    def tQKV(t_, oc=oc, bq3=bq3, xnt_ready=xnt_ready,
                             w_ready=w_ready, bf=p1_bankfree[oc % 2], cXnT=cXnT):
                        s_dx.wait(t_, xnt_ready)
                        s_dx.wait(t_, w_ready)
                        s_v.wait(t_, bf)
                        last = None
                        for pi, nm in enumerate(("q", "k", "v")):
                            for icc in range(DC):
                                last = t_.matmul(
                                    bq3[pi][:],
                                    cW[nm][:, icc * D + oc * 128:
                                           icc * D + (oc + 1) * 128],
                                    cXnT[:, icc * 512:(icc + 1) * 512],
                                    start=(icc == 0), stop=(icc == DC - 1))
                        s_t.inc(last)
                    emit("tensor", tQKV)
                    qkv_done = s_t.v
                    p1_qkv_last[t] = qkv_done

                    uq, uk, vp, tm, ee, Kc, Qc = cEv

                    def vEvac(v, oc=oc, bq3=bq3, qkv_done=qkv_done):
                        s_t.wait(v, qkv_done)
                        v.tensor_scalar(uq[:], bq3[0][:], cBq[:, oc:oc + 1], None,
                                        op0=ALU.add)
                        v.tensor_scalar(uk[:], bq3[1][:], cBk[:, oc:oc + 1], None,
                                        op0=ALU.add)
                        v.tensor_scalar(vp[:], bq3[2][:], cBv[:, oc:oc + 1], None,
                                        op0=ALU.add)
                        s_v.inc(v.engine_nop())
                        v.tensor_single_scalar(tm[:], uq[:], 0.0, op=ALU.min)
                        s_v.inc(v.engine_nop())
                    emit("vector", vEvac)
                    u_done = s_v.v
                    p1_bankfree[oc % 2] = u_done - 1

                    def aExpQ(a, u_done=u_done):
                        s_v.wait(a, u_done)
                        s_a.inc(a.activation(ee[:], tm[:], AF.Exp))
                    emit("scalar", aExpQ)
                    eq_done = s_a.v

                    Qc = cQc2[oc % 2]

                    def vQ(v, eq_done=eq_done, dsp_now=s_dsp.v, Qc=Qc):
                        s_a.wait(v, eq_done)
                        s_dsp.wait(v, dsp_now)
                        v.scalar_tensor_tensor(Qc[:], uq[:], 0.0, ee[:],
                                               op0=ALU.max, op1=ALU.add)
                        v.tensor_single_scalar(tm[:], uk[:], 0.0, op=ALU.min)
                        s_v.inc(v.engine_nop())
                    emit("vector", vQ)
                    q_done = s_v.v

                    def aSpQprev(a):
                        for (poc, pQc, pqd) in pend_q:
                            s_v.wait(a, pqd)
                            s_dsp.inc(a.dma_start(
                                out=qS.ap()[b, poc * 128:(poc + 1) * 128,
                                            t0:t0 + TT], in_=pQc[:]), 16)
                        pend_q.clear()
                    emit("scalar", aSpQprev)
                    pend_q.append((oc, Qc, q_done))

                    def aSpQ(a):
                        s_a.inc(a.activation(ee[:], tm[:], AF.Exp))
                    emit("scalar", aSpQ)
                    ek_done = s_a.v

                    def vK(v, oc=oc, t=t, ek_done=ek_done):
                        s_a.wait(v, ek_done)
                        col = t * DC + oc
                        v.scalar_tensor_tensor(Kc[:], uk[:], 0.0, ee[:],
                                               op0=ALU.max, op1=ALU.add,
                                               accum_out=cKsp[:, col:col + 1])
                        v.scalar_tensor_tensor(cScr[:], Kc[:], 1.0, vp[:],
                                               op0=ALU.mult, op1=ALU.mult,
                                               accum_out=cKvp[:, col:col + 1])
                        s_v.inc(v.engine_nop())
                    emit("vector", vK)

                def aSpQtail(a):
                    for (poc, pQc, pqd) in pend_q:
                        s_v.wait(a, pqd)
                        s_dsp.inc(a.dma_start(
                            out=qS.ap()[b, poc * 128:(poc + 1) * 128,
                                        t0:t0 + TT], in_=pQc[:]), 16)
                    pend_q.clear()
                emit("scalar", aSpQtail)

            for t in range(NT):
                emit_stage_a(t)
                if t >= 1:
                    emit_stage_b(t - 1)
            emit_stage_b(NT - 1)
            for fn in pend_xq:
                emit("scalar", fn)
            pend_xq.clear()

            def vKvFin(v):
                v.drain()
                v.tensor_reduce(cKv8[:],
                                cKvp[:].rearrange("p (t c) -> p c t", c=DC),
                                axis=AX.X, op=ALU.add)
                v.tensor_reduce(cKs8[:],
                                cKsp[:].rearrange("p (t c) -> p c t", c=DC),
                                axis=AX.X, op=ALU.add)
                v.drain()
                s_v.inc(v.engine_nop())
            emit("vector", vKvFin)

        barrier()

        if "p2a" not in stages:
            continue
        with contextlib.ExitStack() as p2:
            cWg = p2.enter_context(nc.sbuf_tensor(
                f"p2wg_{b}", [128, DC * DI], BF16, is_transpose_dest=True))
            cWu = p2.enter_context(nc.sbuf_tensor(
                f"p2wu_{b}", [128, DC * DI], BF16, is_transpose_dest=True))
            cXqT = p2.enter_context(nc.sbuf_tensor(
                f"p2xqt_{b}", [128, DC * 512], BF16, is_transpose_dest=True))
            cH = p2.enter_context(nc.sbuf_tensor(f"p2h_{b}", [128, DI], F32))
            cSl = p2.enter_context(nc.sbuf_tensor(f"p2sl_{b}", [128, 512], F32))
            cHq = p2.enter_context(nc.sbuf_tensor(f"p2hq_{b}", [128, DI], BF16))
            tmp2 = QTmp(p2, nc, f"q2_{b}", DI)

            def ywg(y):
                for c in range(DC):
                    s_dx.inc(y.dma_start_transpose(
                        out=cWg[:, c * DI:(c + 1) * DI],
                        in_=wgN.ap()[:, c * 128:(c + 1) * 128]), 16)
                    s_dx.inc(y.dma_start_transpose(
                        out=cWu[:, c * DI:(c + 1) * DI],
                        in_=wuN.ap()[:, c * 128:(c + 1) * 128]), 16)
            emit("sync", ywg)
            wgu_ready = s_dx.v

            p2_bankfree = {0: 0, 1: 0}
            p2_slfree = [0]
            for t in range(NT):
                t0 = t * TT

                def yxq(y, b=b, t0=t0):
                    s_t.wait(y, s_t.v)
                    s_g.wait(y, s_g.v)
                    for c in range(DC):
                        s_dx.inc(y.dma_start_transpose(
                            out=cXqT[:, c * 512:(c + 1) * 512],
                            in_=xqS.ap()[b, t0:t0 + TT, c * 128:(c + 1) * 128]), 16)
                emit("sync", yxq)
                xqt_ready = s_dx.v

                for s_i in range(NSUB):
                    for io, (o0, o1) in enumerate(OSL):
                        bG = ps[(io % 2) * 2 + 0]
                        bU = ps[(io % 2) * 2 + 1]
                        ow = o1 - o0

                        def tGU(t_, s_i=s_i, o0=o0, o1=o1, ow=ow, bG=bG, bU=bU,
                                xqt_ready=xqt_ready, wgu_ready=wgu_ready,
                                bf=p2_bankfree[io % 2]):
                            s_dx.wait(t_, xqt_ready)
                            s_dx.wait(t_, wgu_ready)
                            s_v.wait(t_, bf)
                            last = None
                            for icc in range(DC):
                                st_ap = cXqT[:, icc * 512 + s_i * 128:
                                             icc * 512 + (s_i + 1) * 128]
                                t_.matmul(bG[:, 0:ow], st_ap,
                                          cWg[:, icc * DI + o0:icc * DI + o1],
                                          start=(icc == 0), stop=(icc == DC - 1))
                                last = t_.matmul(
                                    bU[:, 0:ow], st_ap,
                                    cWu[:, icc * DI + o0:icc * DI + o1],
                                    start=(icc == 0), stop=(icc == DC - 1))
                            s_t.inc(last)
                        emit("tensor", tGU)
                        gu_done = s_t.v

                        def aSilu(a, bG=bG, ow=ow, gu_done=gu_done,
                                  slf=p2_slfree[0]):
                            s_t.wait(a, gu_done)
                            s_v.wait(a, slf)
                            s_a.inc(a.activation(cSl[:, 0:ow], bG[:, 0:ow],
                                                 AF.Silu))
                        emit("scalar", aSilu)
                        sl_done = s_a.v

                        def vH(v, bU=bU, o0=o0, o1=o1, ow=ow, sl_done=sl_done):
                            s_a.wait(v, sl_done)
                            v.tensor_tensor(cH[:, o0:o1], cSl[:, 0:ow],
                                            bU[:, 0:ow], op=ALU.mult)
                            s_v.inc(v.engine_nop())
                        emit("vector", vH)
                        p2_bankfree[io % 2] = s_v.v
                        p2_slfree[0] = s_v.v

                    prev_hq_spill = s_dsp.v

                    def vwh(v, prev_hq_spill=prev_hq_spill):
                        s_dsp.wait(v, prev_hq_spill)
                    emit("vector", vwh)
                    hq_sem, hq_ready = quant_emit(
                        tmp2, [(cH[:, a0:a1], cHq[:, a0:a1], a1 - a0)
                               for (a0, a1) in QT])

                    def aSpHq(a, b=b, t0=t0, s_i=s_i, hq_sem=hq_sem,
                              hq_ready=hq_ready):
                        hq_sem.wait(a, hq_ready)
                        row0 = t0 + s_i * 128
                        s_dsp.inc(a.dma_start(
                            out=hqS.ap()[b, row0:row0 + 128, :], in_=cHq[:]), 16)
                    emit("scalar", aSpHq)

        barrier()

        if "p2b" not in stages:
            continue
        with contextlib.ExitStack() as p3:
            cWd = p3.enter_context(nc.sbuf_tensor(
                f"p3wd_{b}", [128, IC * D], BF16, is_transpose_dest=True))
            cWo = p3.enter_context(nc.sbuf_tensor(f"p3wo_{b}", [128, DC * D], BF16))
            cHqT = p3.enter_context(nc.sbuf_tensor(
                f"p3hqt_{b}", [128, IC * 512], BF16, is_transpose_dest=True))
            cQ = p3.enter_context(nc.sbuf_tensor(f"p3q_{b}", [128, DC * 512], F32))
            cA = p3.enter_context(nc.sbuf_tensor(f"p3a_{b}", [128, DC * 512], BF16))
            cZa = p3.enter_context(nc.sbuf_tensor(f"p3za_{b}", [128, 512], F32))
            cRz = p3.enter_context(nc.sbuf_tensor(f"p3rz_{b}", [128, DC * 512], F32))
            cGi = p3.enter_context(nc.sbuf_tensor(f"p3gi_{b}", [128, 512], F32))
            cGo = p3.enter_context(nc.sbuf_tensor(f"p3go_{b}", [128, 512], F32))
            cOut = p3.enter_context(nc.sbuf_tensor(f"p3out_{b}", [128, NSUB * D],
                                                   F32))
            cXr = p3.enter_context(nc.sbuf_tensor(f"p3xr_{b}", [128, NSUB * D],
                                                  F32))

            def ywd(y):
                for c in range(IC):
                    s_dx.inc(y.dma_start_transpose(
                        out=cWd[:, c * D:(c + 1) * D],
                        in_=wdN.ap()[:, c * 128:(c + 1) * 128]), 16)
                for c in range(DC):
                    s_dx.inc(y.dma_start(
                        out=cWo[:, c * D:(c + 1) * D],
                        in_=woB.ap()[c * 128:(c + 1) * 128, :]), 16)
            emit("sync", ywd)
            wdo_ready = s_dx.v

            p3_bankfree = {0: 0, 1: 0}
            for t in range(NT):
                t0 = t * TT

                q_part_cell = [0]

                def yld(y, b=b, t0=t0):
                    s_v.wait(y, s_v.v)
                    s_t.wait(y, s_t.v)
                    for c in range(DC):
                        s_dx.inc(y.dma_start(
                            out=cQ[:, c * 512:(c + 1) * 512],
                            in_=qS.ap()[b, c * 128:(c + 1) * 128, t0:t0 + TT]), 16)
                    s_dx.inc(y.dma_start(out=cXr[:], in_=xtile_in_ap(b, t0)), 16)
                    q_part_cell[0] = s_dx.v
                    for c in range(IC):
                        s_dx.inc(y.dma_start_transpose(
                            out=cHqT[:, c * 512:(c + 1) * 512],
                            in_=hqS.ap()[b, t0:t0 + TT, c * 128:(c + 1) * 128]), 16)
                emit("sync", yld)
                ld_ready = s_dx.v
                q_ready_p3 = q_part_cell[0]

                prev_out_spill = s_dsp.v

                def vA(v, ld_ready=q_ready_p3, prev_out_spill=prev_out_spill):
                    s_dx.wait(v, ld_ready)
                    s_dsp.wait(v, prev_out_spill)
                    for c in range(DC):
                        v.tensor_scalar(cZa[:], cQ[:, c * 512:(c + 1) * 512],
                                        cKs8[:, c:c + 1], 1e-6,
                                        op0=ALU.mult, op1=ALU.add)
                        v.reciprocal(cRz[:, c * 512:(c + 1) * 512], cZa[:])
                    v.drain()
                    for c in range(DC):
                        v.scalar_tensor_tensor(
                            cA[:, c * 512:(c + 1) * 512],
                            cQ[:, c * 512:(c + 1) * 512], cKv8[:, c:c + 1],
                            cRz[:, c * 512:(c + 1) * 512],
                            op0=ALU.mult, op1=ALU.mult)
                    s_v.inc(v.engine_nop())
                emit("vector", vA)
                a_ready = s_v.v

                for s_i in range(NSUB):
                    par = s_i % 2
                    bD = [ps[par * 4 + 0], ps[par * 4 + 1]]
                    bW = [ps[par * 4 + 2], ps[par * 4 + 3]]

                    def tDW(t_, s_i=s_i, bD=bD, bW=bW, ld_ready=ld_ready,
                            a_ready=a_ready, wdo_ready=wdo_ready):
                        s_dx.wait(t_, ld_ready)
                        s_dx.wait(t_, wdo_ready)
                        s_v.wait(t_, s_v.v)
                        for icc in range(IC):
                            st_ap = cHqT[:, icc * 512 + s_i * 128:
                                         icc * 512 + (s_i + 1) * 128]
                            t_.matmul(bD[0][:], st_ap,
                                      cWd[:, icc * D:icc * D + 512],
                                      start=(icc == 0), stop=(icc == IC - 1))
                            t_.matmul(bD[1][:], st_ap,
                                      cWd[:, icc * D + 512:(icc + 1) * D],
                                      start=(icc == 0), stop=(icc == IC - 1))
                        last = None
                        for c in range(DC):
                            st_ap = cA[:, c * 512 + s_i * 128:
                                       c * 512 + (s_i + 1) * 128]
                            t_.matmul(bW[0][:], st_ap, cWo[:, c * D:c * D + 512],
                                      start=(c == 0), stop=(c == DC - 1))
                            last = t_.matmul(bW[1][:], st_ap,
                                             cWo[:, c * D + 512:(c + 1) * D],
                                             start=(c == 0), stop=(c == DC - 1))
                        s_t.inc(last)
                    emit("tensor", tDW)
                    dw_done = s_t.v

                    for half in range(2):
                        def vGi(v, half=half, bW=bW, dw_done=dw_done):
                            s_t.wait(v, dw_done)
                            v.tensor_tensor(cGi[:], bW[half][:],
                                            cBoR[:, half * 512:(half + 1) * 512],
                                            op=ALU.add)
                            s_v.inc(v.engine_nop())
                        emit("vector", vGi)
                        gi_done = s_v.v

                        def aG(a, gi_done=gi_done):
                            s_v.wait(a, gi_done)
                            s_a.inc(a.activation(cGo[:], cGi[:],
                                                 AF.Gelu_apprx_tanh))
                        emit("scalar", aG)
                        g_done = s_a.v

                        def vOut(v, s_i=s_i, half=half, bD=bD, g_done=g_done):
                            s_a.wait(v, g_done)
                            off = s_i * D + half * 512
                            v.tensor_tensor(cGi[:], cXr[:, off:off + 512],
                                            cGo[:], op=ALU.add)
                            v.tensor_tensor(cOut[:, off:off + 512], cGi[:],
                                            bD[half][:], op=ALU.add)
                            s_v.inc(v.engine_nop())
                        emit("vector", vOut)

                out_ready = s_v.v

                def aOut(a, b=b, t0=t0, out_ready=out_ready):
                    s_v.wait(a, out_ready)
                    s_dsp.inc(a.dma_start(out=xtile_out_ap(b, t0), in_=cOut[:]), 16)
                emit("scalar", aOut)

        barrier()

    emit("gpsimd", lambda g: s_dsp.wait(g, s_dsp.v))
    emit("scalar", lambda a: s_dsp.wait(a, s_dsp.v))

    es.close()
    return nc


_NC_CACHE = {}


def kernel(L0_attn_x, L0_attn_Wq, L0_attn_bq, L0_attn_Wk, L0_attn_bk,
           L0_attn_Wv, L0_attn_bv, L0_attn_Wo, L0_attn_bo,
           L0_ffn_w_gate, L0_ffn_w_up, L0_ffn_w_down):
    x = np.ascontiguousarray(np.asarray(L0_attn_x, dtype=np.float32))
    assert x.shape == (32, 2048, 1024)
    weights = {
        "wq": np.ascontiguousarray(np.asarray(L0_attn_Wq, np.float32)),
        "wk": np.ascontiguousarray(np.asarray(L0_attn_Wk, np.float32)),
        "wv": np.ascontiguousarray(np.asarray(L0_attn_Wv, np.float32)),
        "wo": np.ascontiguousarray(np.asarray(L0_attn_Wo, np.float32)),
        "bq": np.ascontiguousarray(np.asarray(L0_attn_bq, np.float32)),
        "bk": np.ascontiguousarray(np.asarray(L0_attn_bk, np.float32)),
        "bv": np.ascontiguousarray(np.asarray(L0_attn_bv, np.float32)),
        "bo": np.ascontiguousarray(np.asarray(L0_attn_bo, np.float32)),
        "wg": np.ascontiguousarray(np.asarray(L0_ffn_w_gate, np.float32)),
        "wu": np.ascontiguousarray(np.asarray(L0_ffn_w_up, np.float32)),
        "wd": np.ascontiguousarray(np.asarray(L0_ffn_w_down, np.float32)),
    }
    if "nc" not in _NC_CACHE:
        _NC_CACHE["nc"] = build_program()
    nc = _NC_CACHE["nc"]
    in_maps = []
    for core in range(NCORES):
        m = dict(weights)
        m["x"] = x[core * NB:(core + 1) * NB]
        in_maps.append(m)
    res = run_bass_kernel_spmd(nc, in_maps, core_ids=list(range(NCORES)))
    out = np.concatenate([np.asarray(r["out"]) for r in res.results], axis=0)
    return out.astype(np.float32)


# version 23
# speedup vs baseline: 2.5709x; 1.0081x over previous
import contextlib
import numpy as np

import concourse.bass as bass
import concourse.mybir as mybir
from concourse.bass_utils import run_bass_kernel_spmd

F32 = mybir.dt.float32


def _preload_set(a, nc, name):
    from concourse.hw_specs import get_activation_tables
    names = list(get_activation_tables(nc.m.arch).keys())
    a.add_instruction(mybir.InstLoadActFuncSet(
        name=nc.get_next_instruction_name(),
        act_func_set_id=names.index(name), ins=[], outs=[]))

BF16 = mybir.dt.bfloat16
AF = mybir.ActivationFunctionType
ALU = mybir.AluOpType
AX = mybir.AxisListType

NCORES = 8
NB = 4
S = 2048
D = 1024
DI = 2816
TT = 512
NT = S // TT
NSUB = TT // 128
DC = D // 128
IC = DI // 128
C12 = 1.0 / 12.0
INV_F = 1.0 / float(D)
OSL = [(0, 512), (512, 1024), (1024, 1536), (1536, 2048), (2048, 2560), (2560, 2816)]
QT = [(0, 1408), (1408, 2816)]


class Ctr:
    def __init__(self, es, nc, name):
        self.h = es.enter_context(nc.semaphore(name))
        self.v = 0

    def inc(self, inst, k=1):
        inst.then_inc(self.h, k)
        self.v += k
        return self.v

    def wait(self, eng, v=None):
        eng.wait_ge(self.h, self.v if v is None else v)


class QTmp:

    def __init__(self, scope, nc, tag, W):
        self.W = W
        sb = lambda n, s, d: scope.enter_context(nc.sbuf_tensor(n, s, d))
        self.u = sb(f"{tag}_u", [128, W], F32)
        self.au = sb(f"{tag}_au", [128, W], F32)
        self.sg = sb(f"{tag}_sg", [128, W], F32)
        self.aA = sb(f"{tag}_aA", [128, W], F32)
        self.aB = sb(f"{tag}_aB", [128, W], F32)
        G = W // 16
        self.gm = sb(f"{tag}_gm", [128, G], F32)
        self.sc = sb(f"{tag}_sc", [128, G], F32)
        self.r = sb(f"{tag}_r", [128, G], F32)
        self.r6 = sb(f"{tag}_r6", [128, G], F32)


def build_program(debug=False, stages=("cast", "prep", "p1", "p2a", "p2b"), nb=NB, p1_quant=True, p1_xpose=True, p1_qkv=True, p1_ev=4):
    nc = bass.Bass("TRN2", target_bir_lowering=False)

    x_h = nc.dram_tensor("x", [NB, S, D], F32, kind="ExternalInput")
    wq_h = nc.dram_tensor("wq", [D, D], F32, kind="ExternalInput")
    wk_h = nc.dram_tensor("wk", [D, D], F32, kind="ExternalInput")
    wv_h = nc.dram_tensor("wv", [D, D], F32, kind="ExternalInput")
    wo_h = nc.dram_tensor("wo", [D, D], F32, kind="ExternalInput")
    bq_h = nc.dram_tensor("bq", [D], F32, kind="ExternalInput")
    bk_h = nc.dram_tensor("bk", [D], F32, kind="ExternalInput")
    bv_h = nc.dram_tensor("bv", [D], F32, kind="ExternalInput")
    bo_h = nc.dram_tensor("bo", [D], F32, kind="ExternalInput")
    wg_h = nc.dram_tensor("wg", [DI, D], F32, kind="ExternalInput")
    wu_h = nc.dram_tensor("wu", [DI, D], F32, kind="ExternalInput")
    wd_h = nc.dram_tensor("wd", [D, DI], F32, kind="ExternalInput")
    out_h = nc.dram_tensor("out", [NB, S, D], F32, kind="ExternalOutput")

    kscr = "ExternalOutput" if debug else "Internal"
    wgN = nc.dram_tensor("wgN", [DI, D], BF16, kind=kscr)
    wuN = nc.dram_tensor("wuN", [DI, D], BF16, kind=kscr)
    wdN = nc.dram_tensor("wdN", [D, DI], BF16, kind=kscr)
    wqB = nc.dram_tensor("wqB", [D, D], BF16, kind="Internal")
    wkB = nc.dram_tensor("wkB", [D, D], BF16, kind="Internal")
    wvB = nc.dram_tensor("wvB", [D, D], BF16, kind="Internal")
    woB = nc.dram_tensor("woB", [D, D], BF16, kind="Internal")
    xnS = nc.dram_tensor("xnS", [NB, S, D], BF16, kind=kscr)
    xqS = nc.dram_tensor("xqS", [NB, S, D], BF16, kind=kscr)
    hqS = nc.dram_tensor("hqS", [NB, S, DI], BF16, kind=kscr)
    qS = nc.dram_tensor("qS", [NB, D, S], F32, kind=kscr)

    es = contextlib.ExitStack()

    def sb(name, shape, dtype=F32, **kw):
        return es.enter_context(nc.sbuf_tensor(name, shape, dtype, **kw))

    cXn = sb("xn", [128, D])
    cXnb = sb("xnb", [128, NSUB * D], BF16)
    cXqb = sb("xqb", [128, NSUB * D], BF16)
    cSq = [sb("sq0", [128, D]), sb("sq1", [128, D])]
    cSsum = sb("ssum", [128, NSUB])
    cSsq = sb("ssq", [128, NSUB])
    cTv = sb("tv", [128, NSUB])
    cVar = sb("var", [128, NSUB])
    cLv = sb("lv", [128, NSUB])
    cRstd = sb("rstd", [128, NSUB])
    cNmr = sb("nmr", [128, NSUB])
    cNMean = sb("nmean", [128, NSUB])
    cKvp = sb("kvp", [128, NT * DC])
    cKsp = sb("ksp", [128, NT * DC])
    cKv8 = sb("kv8", [128, DC])
    cKs8 = sb("ks8", [128, DC])
    cBq = sb("bq_c", [128, DC])
    cBk = sb("bk_c", [128, DC])
    cBv = sb("bv_c", [128, DC])
    cBoR = sb("boR", [128, D])
    cFeps = sb("cfeps", [128, 1])
    cMh = sb("cmh", [128, 1])
    cHlf = sb("chlf", [128, 1])
    ps = [es.enter_context(nc.psum_tensor(f"ps{i}", [128, 512], F32)) for i in range(8)]

    s_dx = Ctr(es, nc, "dx")
    s_dsp = Ctr(es, nc, "dsp")
    s_dg = Ctr(es, nc, "dg")
    s_v = Ctr(es, nc, "sv")
    s_a = Ctr(es, nc, "sa")
    s_t = Ctr(es, nc, "st")
    s_g = Ctr(es, nc, "sg")
    s_bar = [Ctr(es, nc, f"bar{i}") for i in range(5)]

    blk = es.enter_context(nc.Block())
    ENG = ["vector", "scalar", "tensor", "gpsimd", "sync"]

    def emit(eng, fn):
        getattr(blk, eng)(fn)

    def barrier():
        for i, e in enumerate(ENG):
            def bump(g, i=i, e=e):
                if e in ("vector", "gpsimd"):
                    ins = g.engine_nop()
                elif e in ("tensor", "scalar"):
                    ins = g.drain()
                else:
                    ins = g.nop()
                s_bar[i].inc(ins)
            emit(e, bump)
        targets = [c.v for c in s_bar]
        for i, e in enumerate(ENG):
            def w(g, i=i):
                for j in range(5):
                    if j != i:
                        s_bar[j].wait(g, targets[j])
            emit(e, w)

    q_u_done = [0]

    def quant_emit(tmp, jobs):
        offs = []
        o = 0
        for (_, _, w) in jobs:
            assert w % 16 == 0
            offs.append(o)
            o += w
        assert o <= tmp.W

        g_prev = s_g.v

        def vfn(v):
            s_g.wait(v, g_prev)
            for (src, _, w), off in zip(jobs, offs):
                g0 = off // 16
                gc = w // 16
                v.tensor_reduce(tmp.gm[:, g0:g0 + gc],
                                src.rearrange("p (g e) -> p g e", e=16),
                                axis=AX.X, op=ALU.max, apply_absolute_value=True)
            v.drain()
            for (src, _, w), off in zip(jobs, offs):
                g0 = off // 16
                gc = w // 16
                gsl = tmp.gm[:, g0:g0 + gc]
                v.tensor_scalar(tmp.sc[:, g0:g0 + gc], gsl, 6e-12, C12,
                                op0=ALU.max, op1=ALU.mult)
                v.reciprocal(tmp.r[:, g0:g0 + gc], gsl)
            v.drain()
            for (_, _, w), off in zip(jobs, offs):
                g0 = off // 16
                gc = w // 16
                v.tensor_scalar(tmp.r6[:, g0:g0 + gc], tmp.r[:, g0:g0 + gc],
                                6.0, 1e12, op0=ALU.mult, op1=ALU.min)
            v.drain()
            for (src, _, w), off in zip(jobs, offs):
                g0 = off // 16
                gc = w // 16
                v.tensor_tensor(tmp.u[:, off:off + w].rearrange(
                                    "p (g e) -> p g e", e=16),
                                src.rearrange("p (g e) -> p g e", e=16),
                                tmp.r6[:, g0:g0 + gc].unsqueeze(2)
                                .broadcast_to([128, gc, 16]), op=ALU.mult)
            s_v.inc(v.engine_nop())
        emit("vector", vfn)
        u_done = s_v.v
        q_u_done[0] = u_done

        def vfn2(v):
            for (_, _, w), off in zip(jobs, offs):
                usl = tmp.u[:, off:off + w]
                v.tensor_scalar(tmp.sg[:, off:off + w], usl, 0.0, 2.0,
                                op0=ALU.is_gt, op1=ALU.mult)
            for (_, _, w), off in zip(jobs, offs):
                sgs = tmp.sg[:, off:off + w]
                v.tensor_single_scalar(sgs, sgs, 1.0, op=ALU.subtract)
            for (_, _, w), off in zip(jobs, offs):
                v.tensor_tensor(tmp.au[:, off:off + w], tmp.u[:, off:off + w],
                                tmp.sg[:, off:off + w], op=ALU.mult)
            for (_, _, w), off in zip(jobs, offs):
                au = tmp.au[:, off:off + w]
                aa = tmp.aA[:, off:off + w]
                ab = tmp.aB[:, off:off + w]
                v.tensor_single_scalar(aa, au, 0.25, op=ALU.is_gt)
                for th in (0.75, 1.25, 1.75):
                    v.scalar_tensor_tensor(aa, au, th, aa, op0=ALU.is_gt, op1=ALU.add)
                v.tensor_single_scalar(ab, au, 2.5, op=ALU.is_gt)
                for th in (3.5, 5.0, 5.0):
                    v.scalar_tensor_tensor(ab, au, th, ab, op0=ALU.is_gt, op1=ALU.add)
            for (_, _, w), off in zip(jobs, offs):
                aa = tmp.aA[:, off:off + w]
                ab = tmp.aB[:, off:off + w]
                v.scalar_tensor_tensor(aa, ab, 2.0, aa, op0=ALU.mult, op1=ALU.add)
            s_v.inc(v.engine_nop())
        emit("vector", vfn2)

        def vfn3(v):
            for (_, dst, w), off in zip(jobs, offs):
                g0 = off // 16
                gc = w // 16
                aa = tmp.aA[:, off:off + w]
                usl = tmp.u[:, off:off + w]
                v.tensor_tensor(usl, aa, tmp.sg[:, off:off + w], op=ALU.mult)
                v.tensor_tensor(dst.rearrange("p (g e) -> p g e", e=16),
                                usl.rearrange("p (g e) -> p g e", e=16),
                                tmp.sc[:, g0:g0 + gc].unsqueeze(2)
                                .broadcast_to([128, gc, 16]), op=ALU.mult)
            v.drain()
            s_v.inc(v.engine_nop())
        emit("vector", vfn3)
        return (s_v, s_v.v)

    def xtile_in_ap(b, t0):
        return x_h.ap()[b, t0:t0 + TT, :].rearrange("(s p) d -> p s d", p=128)

    def xtile_out_ap(b, t0):
        return out_h.ap()[b, t0:t0 + TT, :].rearrange("(s p) d -> p s d", p=128)

    def init_consts(v):
        v.memset(cFeps[:], float(D) * float(np.float32(1e-5)))
        v.memset(cMh[:], -0.5)
        v.memset(cHlf[:], float(np.float32(0.5 * np.log(float(D)))))
        v.memset(cKvp[:], 0.0)
        v.memset(cKsp[:], 0.0)
        s_v.inc(v.engine_nop())
    emit("vector", init_consts)

    def load_biases(y):
        for src, dst in ((bq_h, cBq), (bk_h, cBk), (bv_h, cBv)):
            s_dx.inc(y.dma_start(out=dst[:], in_=src.ap().rearrange(
                "(c p) -> p c", p=128), allow_slow_non_contiguous=True), 16)
        s_dx.inc(y.dma_start(
            out=cBoR[:], in_=bo_h.ap().unsqueeze(0).partition_broadcast(128)), 16)
    emit("sync", load_biases)
    biases_ready = s_dx.v

    with contextlib.ExitStack() as pp:
        stg = [pp.enter_context(nc.sbuf_tensor(f"wstg{i}", [128, D], BF16))
               for i in range(2)]
        prev_store = 0
        for wi, (src, dst) in enumerate((((wq_h, wqB), (wk_h, wkB),
                                         (wv_h, wvB), (wo_h, woB))
                                         if "cast" in stages else ())):
            for c in range(DC):
                buf = stg[(wi * DC + c) % 2]

                def gload(g, src=src, c=c, buf=buf, prev_store=prev_store):
                    s_dsp.wait(g, prev_store)
                    s_dg.inc(g.dma_start(out=buf[:],
                                         in_=src.ap()[c * 128:(c + 1) * 128, :]), 16)
                emit("gpsimd", gload)
                ld_done = s_dg.v

                def sstore(a, dst=dst, c=c, buf=buf, ld_done=ld_done):
                    s_dg.wait(a, ld_done)
                    s_dsp.inc(a.dma_start(out=dst.ap()[c * 128:(c + 1) * 128, :],
                                          in_=buf[:]), 16)
                emit("scalar", sstore)
                if wi * DC + c >= 1:
                    prev_store = s_dsp.v - 16

        tmpP = QTmp(pp, nc, "qp", DI)
        wfin = [pp.enter_context(nc.sbuf_tensor(f"wfin{i}", [128, DI], F32))
                for i in range(2)]
        wfout = [pp.enter_context(nc.sbuf_tensor(f"wfout{i}", [128, DI], BF16))
                 for i in range(2)]
        jobs = [(wg_h, wgN, r, D) for r in range(IC)]
        jobs += [(wu_h, wuN, r, D) for r in range(IC)]
        jobs += [(wd_h, wdN, r, DI) for r in range(DC)]
        if "prep" not in stages:
            jobs = []
        uprev = [0, 0]
        stprev = [0, 0]
        for ji, (src, dst, r, K) in enumerate(jobs):
            par = ji % 2
            fin, fout = wfin[par], wfout[par]

            def yload(y, src=src, r=r, K=K, up=uprev[par], sp=stprev[par]):
                s_v.wait(y, up)
                s_dsp.wait(y, sp)
                s_dx.inc(y.dma_start(out=fin[:, 0:K],
                                     in_=src.ap()[r * 128:(r + 1) * 128, :]), 16)
            emit("sync", yload)
            ld_done = s_dx.v

            emit("vector", lambda v, ld_done=ld_done: s_dx.wait(v, ld_done))
            if K == D:
                rsem, ready = quant_emit(tmpP, [(fin[:, 0:D], fout[:, 0:D], D)])
            else:
                rsem, ready = quant_emit(
                    tmpP, [(fin[:, a0:a1], fout[:, a0:a1], a1 - a0)
                           for (a0, a1) in QT])
            uprev[par] = q_u_done[0]

            def sstore(a, dst=dst, r=r, K=K, rsem=rsem, ready=ready, fout=fout):
                rsem.wait(a, ready)
                s_dsp.inc(a.dma_start(out=dst.ap()[r * 128:(r + 1) * 128, :],
                                      in_=fout[:, 0:K]), 16)
            emit("scalar", sstore)
            stprev[par] = s_dsp.v
        barrier()

    for b in range(nb if ("p1" in stages) else 0):
        if b > 0:
            def vreset(v):
                v.memset(cKvp[:], 0.0)
                v.memset(cKsp[:], 0.0)
                s_v.inc(v.engine_nop())
            emit("vector", vreset)

        with contextlib.ExitStack() as p1:
            cXt = [p1.enter_context(nc.sbuf_tensor(f"p1xt{i}_{b}", [128, NSUB * D],
                                                   F32)) for i in range(2)]
            cW = {nm: p1.enter_context(nc.sbuf_tensor(f"p1w{nm}_{b}",
                                                      [128, DC * D], BF16))
                  for nm in ("q", "k", "v")}
            cXnT2 = [p1.enter_context(nc.sbuf_tensor(
                f"p1xnt{i}_{b}", [128, DC * 512], BF16, is_transpose_dest=True))
                for i in range(2)]
            cEv = [p1.enter_context(nc.sbuf_tensor(f"p1ev{i}_{b}", [128, 512], F32))
                   for i in range(7)]
            cQc2 = [p1.enter_context(nc.sbuf_tensor(f"p1qc{i}_{b}", [128, 512],
                                                    F32)) for i in range(2)]
            cScr = p1.enter_context(nc.sbuf_tensor(f"p1scr_{b}", [128, 512], F32))
            tmp1 = QTmp(p1, nc, f"q1_{b}", D)

            def ywl(y):
                for nm, srcB in (("q", wqB), ("k", wkB), ("v", wvB)):
                    for c in range(DC):
                        s_dx.inc(y.dma_start(
                            out=cW[nm][:, c * D:(c + 1) * D],
                            in_=srcB.ap()[c * 128:(c + 1) * 128, :]), 16)
            emit("sync", ywl)
            w_ready = s_dx.v
            if b == 0:
                emit("vector", lambda v: s_dx.wait(v, biases_ready))

            p1_bankfree = {0: 0, 1: 0}
            p1_ufree = [0]
            p1_xfree_a = {}
            p1_xfree_v = {}
            p1_qkv_last = {}
            p1_stageb = {}
            pend_xq = []

            def emit_stage_a(t):
                t0 = t * TT
                xbuf = cXt[t % 2]
                for fn in pend_xq:
                    emit("scalar", fn)
                pend_xq.clear()

                def yx(y, b=b, t0=t0, xbuf=xbuf,
                       wa=p1_xfree_a.get(t - 2, 0), wv=p1_xfree_v.get(t - 2, 0)):
                    s_a.wait(y, wa)
                    s_v.wait(y, wv)
                    s_dx.inc(y.dma_start(out=xbuf[:], in_=xtile_in_ap(b, t0)), 16)
                emit("sync", yx)
                x_ready = s_dx.v

                def vStats(v, xbuf=xbuf, x_ready=x_ready):
                    s_dx.wait(v, x_ready)
                    for s_i in range(NSUB):
                        xs = xbuf[:, s_i * D:(s_i + 1) * D]
                        v.tensor_reduce(cSsum[:, s_i:s_i + 1], xs,
                                        axis=AX.X, op=ALU.add)
                        v.tensor_tensor(cSq[s_i % 2][:], xs, xs, op=ALU.mult)
                        v.tensor_reduce(cSsq[:, s_i:s_i + 1], cSq[s_i % 2][:],
                                        axis=AX.X, op=ALU.add)
                        s_v.inc(v.engine_nop())
                    v.drain()
                    v.tensor_tensor(cTv[:], cSsum[:], cSsum[:], op=ALU.mult)
                    v.drain()
                    v.scalar_tensor_tensor(cVar[:], cTv[:], -INV_F, cSsq[:],
                                           op0=ALU.mult, op1=ALU.add)
                    v.tensor_single_scalar(cNMean[:], cSsum[:], -INV_F,
                                           op=ALU.mult)
                    v.drain()
                    v.tensor_copy(cSq[0][:], cSq[0][:])
                    s_v.inc(v.engine_nop())
                emit("vector", vStats)
                var_done = s_v.v
                p1_xfree_v[t] = var_done

                def aR(a, var_done=var_done):
                    s_v.wait(a, var_done)
                    a.activation(cLv[:], cVar[:], AF.Ln, bias=cFeps[:, 0:1])
                    a.activation(cRstd[:], cLv[:], AF.Exp,
                                 scale=cMh[:, 0:1], bias=cHlf[:, 0:1])
                    a.drain()
                    for s_i in range(NSUB):
                        last = a.activation(cNmr[:, s_i:s_i + 1],
                                            cNMean[:, s_i:s_i + 1], AF.Identity,
                                            scale=cRstd[:, s_i:s_i + 1])
                    a.drain()
                    s_a.inc(last)
                emit("scalar", aR)
                nmr_done_a = s_a.v

                for s_i in range(NSUB):
                    def aXn(a, xbuf=xbuf, s_i=s_i, ufree=p1_ufree[0]):
                        s_v.wait(a, ufree)
                        s_dx.wait(a, s_dx.v)
                        a.activation(cXn[:], xbuf[:, s_i * D:(s_i + 1) * D],
                                     AF.Identity, bias=cNmr[:, s_i:s_i + 1],
                                     scale=cRstd[:, s_i:s_i + 1])
                        s_a.inc(a.activation(
                            cXnb[:, s_i * D:(s_i + 1) * D],
                            xbuf[:, s_i * D:(s_i + 1) * D],
                            AF.Identity, bias=cNmr[:, s_i:s_i + 1],
                            scale=cRstd[:, s_i:s_i + 1]))
                    emit("scalar", aXn)
                    xn_done = s_a.v

                    def vwq(v, xn_done=xn_done, dsp=s_dsp.v):
                        s_a.wait(v, xn_done)
                        s_dsp.wait(v, dsp)
                    emit("vector", vwq)
                    q_sem, q_ready = quant_emit(
                        tmp1, [(cXn[:], cXqb[:, s_i * D:(s_i + 1) * D], D)])
                    p1_ufree[0] = q_u_done[0]

                p1_xfree_a[t] = s_a.v
                xn_all = s_a.v

                def ySpXnb(y, b=b, t0=t0, xn_all=xn_all, usettle=p1_ufree[0]):
                    s_a.wait(y, xn_all)
                    s_v.wait(y, usettle)
                    s_dx.inc(y.dma_start(
                        out=xnS.ap()[b, t0:t0 + TT, :].rearrange(
                            "(s p) d -> p s d", p=128), in_=cXnb[:]), 16)
                emit("sync", ySpXnb)

                def aSpXq(a, b=b, t0=t0, q_sem=q_sem, xq_all=q_ready):
                    q_sem.wait(a, xq_all)
                    s_dsp.inc(a.dma_start(
                        out=xqS.ap()[b, t0:t0 + TT, :].rearrange(
                            "(s p) d -> p s d", p=128), in_=cXqb[:]), 16)
                pend_xq.append(aSpXq)

                cXnT = cXnT2[t % 2]

                def yT(y, b=b, t0=t0, cXnT=cXnT,
                       wt=p1_qkv_last.get(t - 2, 0)):
                    s_dx.wait(y, s_dx.v)
                    s_t.wait(y, wt)
                    for c in range(DC):
                        s_dx.inc(y.dma_start_transpose(
                            out=cXnT[:, c * 512:(c + 1) * 512],
                            in_=xnS.ap()[b, t0:t0 + TT, c * 128:(c + 1) * 128]), 16)
                emit("sync", yT)
                p1_stageb[t] = (t0, s_dx.v, cXnT)

            def emit_stage_b(t):
                t0, xnt_ready, cXnT = p1_stageb[t]
                pend_q = []
                for oc in range(DC):
                    bq3 = [ps[(oc % 2) * 3 + i] for i in range(3)]

                    def tQKV(t_, oc=oc, bq3=bq3, xnt_ready=xnt_ready,
                             w_ready=w_ready, bf=p1_bankfree[oc % 2], cXnT=cXnT):
                        s_dx.wait(t_, xnt_ready)
                        s_dx.wait(t_, w_ready)
                        s_v.wait(t_, bf)
                        last = None
                        for pi, nm in enumerate(("q", "k", "v")):
                            for icc in range(DC):
                                last = t_.matmul(
                                    bq3[pi][:],
                                    cW[nm][:, icc * D + oc * 128:
                                           icc * D + (oc + 1) * 128],
                                    cXnT[:, icc * 512:(icc + 1) * 512],
                                    start=(icc == 0), stop=(icc == DC - 1))
                        s_t.inc(last)
                    emit("tensor", tQKV)
                    qkv_done = s_t.v
                    p1_qkv_last[t] = qkv_done

                    uq, uk, vp, tm, ee, Kc, Qc = cEv

                    def vEvac(v, oc=oc, bq3=bq3, qkv_done=qkv_done):
                        s_t.wait(v, qkv_done)
                        v.tensor_scalar(uq[:], bq3[0][:], cBq[:, oc:oc + 1], None,
                                        op0=ALU.add)
                        v.tensor_scalar(uk[:], bq3[1][:], cBk[:, oc:oc + 1], None,
                                        op0=ALU.add)
                        v.tensor_scalar(vp[:], bq3[2][:], cBv[:, oc:oc + 1], None,
                                        op0=ALU.add)
                        s_v.inc(v.engine_nop())
                        v.tensor_single_scalar(tm[:], uq[:], 0.0, op=ALU.min)
                        s_v.inc(v.engine_nop())
                    emit("vector", vEvac)
                    u_done = s_v.v
                    p1_bankfree[oc % 2] = u_done - 1

                    def aExpQ(a, u_done=u_done):
                        s_v.wait(a, u_done)
                        s_a.inc(a.activation(ee[:], tm[:], AF.Exp))
                    emit("scalar", aExpQ)
                    eq_done = s_a.v

                    Qc = cQc2[oc % 2]

                    def vQ(v, eq_done=eq_done, dsp_now=s_dsp.v, Qc=Qc):
                        s_a.wait(v, eq_done)
                        s_dsp.wait(v, dsp_now)
                        v.scalar_tensor_tensor(Qc[:], uq[:], 0.0, ee[:],
                                               op0=ALU.max, op1=ALU.add)
                        v.tensor_single_scalar(tm[:], uk[:], 0.0, op=ALU.min)
                        s_v.inc(v.engine_nop())
                    emit("vector", vQ)
                    q_done = s_v.v

                    def aSpQprev(a):
                        for (poc, pQc, pqd) in pend_q:
                            s_v.wait(a, pqd)
                            s_dsp.inc(a.dma_start(
                                out=qS.ap()[b, poc * 128:(poc + 1) * 128,
                                            t0:t0 + TT], in_=pQc[:]), 16)
                        pend_q.clear()
                    emit("scalar", aSpQprev)
                    pend_q.append((oc, Qc, q_done))

                    def aSpQ(a):
                        s_a.inc(a.activation(ee[:], tm[:], AF.Exp))
                    emit("scalar", aSpQ)
                    ek_done = s_a.v

                    def vK(v, oc=oc, t=t, ek_done=ek_done):
                        s_a.wait(v, ek_done)
                        col = t * DC + oc
                        v.scalar_tensor_tensor(Kc[:], uk[:], 0.0, ee[:],
                                               op0=ALU.max, op1=ALU.add,
                                               accum_out=cKsp[:, col:col + 1])
                        v.scalar_tensor_tensor(cScr[:], Kc[:], 1.0, vp[:],
                                               op0=ALU.mult, op1=ALU.mult,
                                               accum_out=cKvp[:, col:col + 1])
                        s_v.inc(v.engine_nop())
                    emit("vector", vK)

                def aSpQtail(a):
                    for (poc, pQc, pqd) in pend_q:
                        s_v.wait(a, pqd)
                        s_dsp.inc(a.dma_start(
                            out=qS.ap()[b, poc * 128:(poc + 1) * 128,
                                        t0:t0 + TT], in_=pQc[:]), 16)
                    pend_q.clear()
                emit("scalar", aSpQtail)

            for t in range(NT):
                emit_stage_a(t)
                if t >= 1:
                    emit_stage_b(t - 1)
            emit_stage_b(NT - 1)
            for fn in pend_xq:
                emit("scalar", fn)
            pend_xq.clear()

            def vKvFin(v):
                v.drain()
                v.tensor_reduce(cKv8[:],
                                cKvp[:].rearrange("p (t c) -> p c t", c=DC),
                                axis=AX.X, op=ALU.add)
                v.tensor_reduce(cKs8[:],
                                cKsp[:].rearrange("p (t c) -> p c t", c=DC),
                                axis=AX.X, op=ALU.add)
                v.drain()
                s_v.inc(v.engine_nop())
            emit("vector", vKvFin)

        barrier()

        if "p2a" not in stages:
            continue
        with contextlib.ExitStack() as p2:
            cWg = p2.enter_context(nc.sbuf_tensor(
                f"p2wg_{b}", [128, DC * DI], BF16, is_transpose_dest=True))
            cWu = p2.enter_context(nc.sbuf_tensor(
                f"p2wu_{b}", [128, DC * DI], BF16, is_transpose_dest=True))
            cXqT = p2.enter_context(nc.sbuf_tensor(
                f"p2xqt_{b}", [128, DC * 512], BF16, is_transpose_dest=True))
            cH = p2.enter_context(nc.sbuf_tensor(f"p2h_{b}", [128, DI], F32))
            cSl = p2.enter_context(nc.sbuf_tensor(f"p2sl_{b}", [128, 512], F32))
            cHq = p2.enter_context(nc.sbuf_tensor(f"p2hq_{b}", [128, DI], BF16))
            tmp2 = QTmp(p2, nc, f"q2_{b}", DI)

            def yxq0(y, b=b):
                s_t.wait(y, s_t.v)
                s_g.wait(y, s_g.v)
                for c in range(DC):
                    s_dx.inc(y.dma_start_transpose(
                        out=cXqT[:, c * 512:(c + 1) * 512],
                        in_=xqS.ap()[b, 0:TT, c * 128:(c + 1) * 128]), 16)
            emit("sync", yxq0)
            xqt0_ready = s_dx.v

            def ywg(y):
                for c in range(DC):
                    s_dx.inc(y.dma_start_transpose(
                        out=cWg[:, c * DI:(c + 1) * DI],
                        in_=wgN.ap()[:, c * 128:(c + 1) * 128]), 16)
                    s_dx.inc(y.dma_start_transpose(
                        out=cWu[:, c * DI:(c + 1) * DI],
                        in_=wuN.ap()[:, c * 128:(c + 1) * 128]), 16)
            emit("sync", ywg)
            wgu_ready = s_dx.v

            p2_bankfree = {0: 0, 1: 0}
            p2_slfree = [0]
            for t in range(NT):
                t0 = t * TT

                if t == 0:
                    xqt_ready = xqt0_ready
                else:
                    def yxq(y, b=b, t0=t0):
                        s_t.wait(y, s_t.v)
                        s_g.wait(y, s_g.v)
                        for c in range(DC):
                            s_dx.inc(y.dma_start_transpose(
                                out=cXqT[:, c * 512:(c + 1) * 512],
                                in_=xqS.ap()[b, t0:t0 + TT, c * 128:(c + 1) * 128]), 16)
                    emit("sync", yxq)
                    xqt_ready = s_dx.v

                for s_i in range(NSUB):
                    for io, (o0, o1) in enumerate(OSL):
                        bG = ps[(io % 2) * 2 + 0]
                        bU = ps[(io % 2) * 2 + 1]
                        ow = o1 - o0

                        def tGU(t_, s_i=s_i, o0=o0, o1=o1, ow=ow, bG=bG, bU=bU,
                                xqt_ready=xqt_ready, wgu_ready=wgu_ready,
                                bf=p2_bankfree[io % 2],
                                wbase=(xqt0_ready if t == 0 else None)):
                            s_dx.wait(t_, xqt_ready)
                            if wbase is None:
                                s_dx.wait(t_, wgu_ready)
                            s_v.wait(t_, bf)
                            last = None
                            for icc in range(DC):
                                if wbase is not None:
                                    s_dx.wait(t_, wbase + 16 * (2 * icc + 2))
                                st_ap = cXqT[:, icc * 512 + s_i * 128:
                                             icc * 512 + (s_i + 1) * 128]
                                t_.matmul(bG[:, 0:ow], st_ap,
                                          cWg[:, icc * DI + o0:icc * DI + o1],
                                          start=(icc == 0), stop=(icc == DC - 1))
                                last = t_.matmul(
                                    bU[:, 0:ow], st_ap,
                                    cWu[:, icc * DI + o0:icc * DI + o1],
                                    start=(icc == 0), stop=(icc == DC - 1))
                            s_t.inc(last)
                        emit("tensor", tGU)
                        gu_done = s_t.v

                        def aSilu(a, bG=bG, ow=ow, gu_done=gu_done,
                                  slf=p2_slfree[0]):
                            s_t.wait(a, gu_done)
                            s_v.wait(a, slf)
                            s_a.inc(a.activation(cSl[:, 0:ow], bG[:, 0:ow],
                                                 AF.Silu))
                        emit("scalar", aSilu)
                        sl_done = s_a.v

                        def vH(v, bU=bU, o0=o0, o1=o1, ow=ow, sl_done=sl_done):
                            s_a.wait(v, sl_done)
                            v.tensor_tensor(cH[:, o0:o1], cSl[:, 0:ow],
                                            bU[:, 0:ow], op=ALU.mult)
                            s_v.inc(v.engine_nop())
                        emit("vector", vH)
                        p2_bankfree[io % 2] = s_v.v
                        p2_slfree[0] = s_v.v

                    prev_hq_spill = s_dsp.v

                    def vwh(v, prev_hq_spill=prev_hq_spill):
                        s_dsp.wait(v, prev_hq_spill)
                    emit("vector", vwh)
                    hq_sem, hq_ready = quant_emit(
                        tmp2, [(cH[:, a0:a1], cHq[:, a0:a1], a1 - a0)
                               for (a0, a1) in QT])

                    def aSpHq(a, b=b, t0=t0, s_i=s_i, hq_sem=hq_sem,
                              hq_ready=hq_ready):
                        hq_sem.wait(a, hq_ready)
                        row0 = t0 + s_i * 128
                        s_dsp.inc(a.dma_start(
                            out=hqS.ap()[b, row0:row0 + 128, :], in_=cHq[:]), 16)
                    emit("scalar", aSpHq)

        barrier()

        if "p2b" not in stages:
            continue
        with contextlib.ExitStack() as p3:
            cWd = p3.enter_context(nc.sbuf_tensor(
                f"p3wd_{b}", [128, IC * D], BF16, is_transpose_dest=True))
            cWo = p3.enter_context(nc.sbuf_tensor(f"p3wo_{b}", [128, DC * D], BF16))
            cHqT = p3.enter_context(nc.sbuf_tensor(
                f"p3hqt_{b}", [128, IC * 512], BF16, is_transpose_dest=True))
            cQ = p3.enter_context(nc.sbuf_tensor(f"p3q_{b}", [128, DC * 512], F32))
            cA = p3.enter_context(nc.sbuf_tensor(f"p3a_{b}", [128, DC * 512], BF16))
            cZa = p3.enter_context(nc.sbuf_tensor(f"p3za_{b}", [128, 512], F32))
            cRz = p3.enter_context(nc.sbuf_tensor(f"p3rz_{b}", [128, DC * 512], F32))
            cGi = p3.enter_context(nc.sbuf_tensor(f"p3gi_{b}", [128, 512], F32))
            cGo = p3.enter_context(nc.sbuf_tensor(f"p3go_{b}", [128, 512], F32))
            cOut = p3.enter_context(nc.sbuf_tensor(f"p3out_{b}", [128, NSUB * D],
                                                   F32))
            cXr = p3.enter_context(nc.sbuf_tensor(f"p3xr_{b}", [128, NSUB * D],
                                                  F32))

            def ywd(y):
                for c in range(IC):
                    s_dx.inc(y.dma_start_transpose(
                        out=cWd[:, c * D:(c + 1) * D],
                        in_=wdN.ap()[:, c * 128:(c + 1) * 128]), 16)
                for c in range(DC):
                    s_dx.inc(y.dma_start(
                        out=cWo[:, c * D:(c + 1) * D],
                        in_=woB.ap()[c * 128:(c + 1) * 128, :]), 16)
            emit("sync", ywd)
            wdo_ready = s_dx.v

            p3_bankfree = {0: 0, 1: 0}
            for t in range(NT):
                t0 = t * TT

                q_part_cell = [0]

                def yld(y, b=b, t0=t0):
                    s_v.wait(y, s_v.v)
                    s_t.wait(y, s_t.v)
                    for c in range(DC):
                        s_dx.inc(y.dma_start(
                            out=cQ[:, c * 512:(c + 1) * 512],
                            in_=qS.ap()[b, c * 128:(c + 1) * 128, t0:t0 + TT]), 16)
                    s_dx.inc(y.dma_start(out=cXr[:], in_=xtile_in_ap(b, t0)), 16)
                    q_part_cell[0] = s_dx.v
                    for c in range(IC):
                        s_dx.inc(y.dma_start_transpose(
                            out=cHqT[:, c * 512:(c + 1) * 512],
                            in_=hqS.ap()[b, t0:t0 + TT, c * 128:(c + 1) * 128]), 16)
                emit("sync", yld)
                ld_ready = s_dx.v
                q_ready_p3 = q_part_cell[0]

                prev_out_spill = s_dsp.v

                def vA(v, ld_ready=q_ready_p3, prev_out_spill=prev_out_spill):
                    s_dx.wait(v, ld_ready)
                    s_dsp.wait(v, prev_out_spill)
                    for c in range(DC):
                        v.tensor_scalar(cZa[:], cQ[:, c * 512:(c + 1) * 512],
                                        cKs8[:, c:c + 1], 1e-6,
                                        op0=ALU.mult, op1=ALU.add)
                        v.reciprocal(cRz[:, c * 512:(c + 1) * 512], cZa[:])
                    v.drain()
                    for c in range(DC):
                        v.scalar_tensor_tensor(
                            cA[:, c * 512:(c + 1) * 512],
                            cQ[:, c * 512:(c + 1) * 512], cKv8[:, c:c + 1],
                            cRz[:, c * 512:(c + 1) * 512],
                            op0=ALU.mult, op1=ALU.mult)
                    s_v.inc(v.engine_nop())
                emit("vector", vA)
                a_ready = s_v.v

                for s_i in range(NSUB):
                    par = s_i % 2
                    bD = [ps[par * 4 + 0], ps[par * 4 + 1]]
                    bW = [ps[par * 4 + 2], ps[par * 4 + 3]]

                    def tDW(t_, s_i=s_i, bD=bD, bW=bW, ld_ready=ld_ready,
                            a_ready=a_ready, wdo_ready=wdo_ready):
                        s_dx.wait(t_, ld_ready)
                        s_dx.wait(t_, wdo_ready)
                        s_v.wait(t_, s_v.v)
                        for icc in range(IC):
                            st_ap = cHqT[:, icc * 512 + s_i * 128:
                                         icc * 512 + (s_i + 1) * 128]
                            t_.matmul(bD[0][:], st_ap,
                                      cWd[:, icc * D:icc * D + 512],
                                      start=(icc == 0), stop=(icc == IC - 1))
                            t_.matmul(bD[1][:], st_ap,
                                      cWd[:, icc * D + 512:(icc + 1) * D],
                                      start=(icc == 0), stop=(icc == IC - 1))
                        last = None
                        for c in range(DC):
                            st_ap = cA[:, c * 512 + s_i * 128:
                                       c * 512 + (s_i + 1) * 128]
                            t_.matmul(bW[0][:], st_ap, cWo[:, c * D:c * D + 512],
                                      start=(c == 0), stop=(c == DC - 1))
                            last = t_.matmul(bW[1][:], st_ap,
                                             cWo[:, c * D + 512:(c + 1) * D],
                                             start=(c == 0), stop=(c == DC - 1))
                        s_t.inc(last)
                    emit("tensor", tDW)
                    dw_done = s_t.v

                    for half in range(2):
                        def vGi(v, half=half, bW=bW, dw_done=dw_done):
                            s_t.wait(v, dw_done)
                            v.tensor_tensor(cGi[:], bW[half][:],
                                            cBoR[:, half * 512:(half + 1) * 512],
                                            op=ALU.add)
                            s_v.inc(v.engine_nop())
                        emit("vector", vGi)
                        gi_done = s_v.v

                        def aG(a, gi_done=gi_done):
                            s_v.wait(a, gi_done)
                            s_a.inc(a.activation(cGo[:], cGi[:],
                                                 AF.Gelu_apprx_tanh))
                        emit("scalar", aG)
                        g_done = s_a.v

                        def vOut(v, s_i=s_i, half=half, bD=bD, g_done=g_done):
                            s_a.wait(v, g_done)
                            off = s_i * D + half * 512
                            v.tensor_tensor(cGi[:], cXr[:, off:off + 512],
                                            cGo[:], op=ALU.add)
                            v.tensor_tensor(cOut[:, off:off + 512], cGi[:],
                                            bD[half][:], op=ALU.add)
                            s_v.inc(v.engine_nop())
                        emit("vector", vOut)

                out_ready = s_v.v

                def aOut(a, b=b, t0=t0, out_ready=out_ready):
                    s_v.wait(a, out_ready)
                    s_dsp.inc(a.dma_start(out=xtile_out_ap(b, t0), in_=cOut[:]), 16)
                emit("scalar", aOut)

        barrier()

    emit("gpsimd", lambda g: s_dsp.wait(g, s_dsp.v))
    emit("scalar", lambda a: s_dsp.wait(a, s_dsp.v))

    es.close()
    return nc


_NC_CACHE = {}


def kernel(L0_attn_x, L0_attn_Wq, L0_attn_bq, L0_attn_Wk, L0_attn_bk,
           L0_attn_Wv, L0_attn_bv, L0_attn_Wo, L0_attn_bo,
           L0_ffn_w_gate, L0_ffn_w_up, L0_ffn_w_down):
    x = np.ascontiguousarray(np.asarray(L0_attn_x, dtype=np.float32))
    assert x.shape == (32, 2048, 1024)
    weights = {
        "wq": np.ascontiguousarray(np.asarray(L0_attn_Wq, np.float32)),
        "wk": np.ascontiguousarray(np.asarray(L0_attn_Wk, np.float32)),
        "wv": np.ascontiguousarray(np.asarray(L0_attn_Wv, np.float32)),
        "wo": np.ascontiguousarray(np.asarray(L0_attn_Wo, np.float32)),
        "bq": np.ascontiguousarray(np.asarray(L0_attn_bq, np.float32)),
        "bk": np.ascontiguousarray(np.asarray(L0_attn_bk, np.float32)),
        "bv": np.ascontiguousarray(np.asarray(L0_attn_bv, np.float32)),
        "bo": np.ascontiguousarray(np.asarray(L0_attn_bo, np.float32)),
        "wg": np.ascontiguousarray(np.asarray(L0_ffn_w_gate, np.float32)),
        "wu": np.ascontiguousarray(np.asarray(L0_ffn_w_up, np.float32)),
        "wd": np.ascontiguousarray(np.asarray(L0_ffn_w_down, np.float32)),
    }
    if "nc" not in _NC_CACHE:
        _NC_CACHE["nc"] = build_program()
    nc = _NC_CACHE["nc"]
    in_maps = []
    for core in range(NCORES):
        m = dict(weights)
        m["x"] = x[core * NB:(core + 1) * NB]
        in_maps.append(m)
    res = run_bass_kernel_spmd(nc, in_maps, core_ids=list(range(NCORES)))
    out = np.concatenate([np.asarray(r["out"]) for r in res.results], axis=0)
    return out.astype(np.float32)
